# version 01
# baseline (speedup 1.0000x reference)
import os
os.environ.setdefault("JAX_COMPILATION_CACHE_DIR", "/root/.jax_cache")
import contextlib
import numpy as np
import concourse.bass as bass
import concourse.mybir as mybir
from concourse.bass_utils import run_bass_kernel_spmd

F32 = mybir.dt.float32
F32R = mybir.dt.float32r
BF16 = mybir.dt.bfloat16
I32 = mybir.dt.int32
AF = mybir.ActivationFunctionType
ALU = mybir.AluOpType
AX = mybir.AxisListType

NCORES = 8
T = 1024
D = 2048
DC = D // 128
RG = [[0, 1, 2, 3, 4, 5, 6, 7]]
EPS = 1e-5
NONE = []


class Sem:
    def __init__(self, ctx, nc, name):
        self.h = ctx.enter_context(nc.semaphore(name))
        self.n = 0

    def inc(self, instr, k=1):
        instr.then_inc(self.h, k)
        self.n += k
        return self.n


class B:
    def __init__(self, name):
        self.nc = bass.Bass("TRN2", target_bir_lowering=False)
        self.ctx = contextlib.ExitStack()
        self.name = name
        self.block = self.ctx.enter_context(self.nc.Block())
        self.sems = {s: Sem(self.ctx, self.nc, "sem_" + s)
                     for s in ["ld", "ld2", "st", "t", "v", "s", "p", "cc"]}
        self.pool = [self.ctx.enter_context(
            self.nc.psum_tensor(f"bank{i}", [128, 512], F32)) for i in range(8)]
        self.bank_free = [NONE] * 8

    def sbuf(self, name, shape, dtype):
        return self.ctx.enter_context(self.nc.sbuf_tensor(name, shape, dtype))

    def on(self, eng):
        return getattr(self.block, eng)

    def waits(self, eng, *marks):
        flat = []
        for m in marks:
            if isinstance(m, list):
                flat.extend(m)
            elif m is not None:
                flat.append(m)
        best = {}
        for sem_name, cnt in flat:
            best[sem_name] = max(best.get(sem_name, 0), cnt)
        for sem_name, cnt in best.items():
            if cnt > 0:
                eng.wait_ge(self.sems[sem_name].h, cnt)

    def mark(self, sem_name):
        return (sem_name, self.sems[sem_name].n)

    def chain(self, eng, sem_name, ins):
        n = self.sems[sem_name].inc(ins)
        eng.wait_ge(self.sems[sem_name].h, n)
        return ins

    def close(self):
        self.ctx.close()


def fr(ap):
    return ap.bitcast(F32R)


class Arena:
    def __init__(self, b, name, cols):
        self.b = b
        self.name = name
        self.cols = cols
        self.n = 0
        used = b.nc.SBUF_PARTITION_SIZE_BYTES - b.nc.sbuf_bytes_remaining
        self.base = ((used + 31) // 32) * 32
        assert self.base + cols * 4 <= b.nc.SBUF_PARTITION_SIZE_BYTES, (
            self.base, cols)

    def t(self, col, shape, dtype):
        assert col % 8 == 0, col
        esz = 4 if dtype in (F32, F32R, I32) else 2
        ncols = (shape[-1] * esz + 3) // 4
        assert col + ncols <= self.cols, (col, shape, self.cols)
        self.n += 1
        h = self.b.nc.alloc_sbuf_tensor_at(
            f"{self.name}_{self.n}", list(shape), dtype,
            offset=self.base + col * 4)
        return h[tuple(slice(0, d) for d in shape)]

    def f32(self, col, n, p=128):
        return self.t(col, [p, n], F32)

    def f32r(self, col, n, p=128):
        return self.t(col, [p, n], F32R)



def host_tile_w_ocmajor(w, mt=128):
    K, M = w.shape
    assert K % 128 == 0 and M % mt == 0
    a = w.reshape(K // 128, 128, M // mt, mt)
    a = a.transpose(2, 1, 0, 3)
    return np.ascontiguousarray(a.reshape(M // mt, 128, (K // 128) * mt))


def host_consts(core):
    ident = np.eye(128, dtype=np.float32)
    ones_col = np.ones((128, 1), np.float32)
    ones_row = np.ones((1, 128), np.float32)
    lt = np.zeros((128, 8), np.float32)
    lt[:, :core] = 1.0
    prev = np.zeros((128, 8), np.float32)
    if core > 0:
        prev[:, core - 1] = 1.0
    return dict(c_ident=ident, c_ones_col=ones_col, c_ones_row=ones_row,
                c_lt=lt, c_ltinv=np.ascontiguousarray(1.0 - lt), c_prev=prev)


def declare_consts(nc):
    return dict(
        c_ident=nc.dram_tensor("c_ident", [128, 128], F32, kind="ExternalInput").ap(),
        c_ones_col=nc.dram_tensor("c_ones_col", [128, 1], F32R, kind="ExternalInput").ap(),
        c_ones_row=nc.dram_tensor("c_ones_row", [1, 128], F32R, kind="ExternalInput").ap(),
        c_lt=nc.dram_tensor("c_lt", [128, 8], F32, kind="ExternalInput").ap(),
        c_ltinv=nc.dram_tensor("c_ltinv", [128, 8], F32, kind="ExternalInput").ap(),
        c_prev=nc.dram_tensor("c_prev", [128, 8], F32, kind="ExternalInput").ap(),
    )


def load_consts(b, aps):
    C = {}
    C["ident"] = b.sbuf("cs_ident", [128, 128], F32)
    C["ones_col"] = b.sbuf("cs_ones_col", [128, 1], F32R)
    C["ones_row"] = b.sbuf("cs_ones_row", [1, 128], F32R)
    C["lt"] = b.sbuf("cs_lt", [128, 8], F32)
    C["ltinv"] = b.sbuf("cs_ltinv", [128, 8], F32)
    C["prev"] = b.sbuf("cs_prev", [128, 8], F32)
    ccols = b.sbuf("cs_cols", [128, 4], F32)

    @b.on("sync")
    def _(q):
        s = b.sems["ld"]
        for key, dst in [("c_ident", C["ident"]), ("c_ones_col", C["ones_col"]),
                         ("c_ones_row", C["ones_row"]), ("c_lt", C["lt"]),
                         ("c_ltinv", C["ltinv"]), ("c_prev", C["prev"])]:
            s.inc(q.dma_start(out=dst[:], in_=aps[key][:, :]), 16)

    @b.on("vector")
    def _(v):
        v.memset(ccols[:, 0:1], 1.0)
        v.memset(ccols[:, 1:2], -1.0)
        v.memset(ccols[:, 2:3], EPS)
        b.sems["v"].inc(v.memset(ccols[:, 3:4], 0.0))

    C["one"] = ccols[:, 0:1]
    C["negone"] = ccols[:, 1:2]
    C["eps"] = ccols[:, 2:3]
    C["zero"] = ccols[:, 3:4]
    C["ready"] = [b.mark("ld"), b.mark("v")]
    return C



def emit_ln(b, C, resid_tiles, scr, dep_marks, banks, out_aps,
            pre_wait_fn=None, post_fn=None, width=T):
    npc = (width + 511) // 512
    pieces = [(j * 512, min(width, (j + 1) * 512)) for j in range(npc)]
    sx = [b.pool[banks[0]][0:1, 0:512], b.pool[banks[1]][0:1, 0:512]]
    sqp = [b.pool[banks[2]][0:1, 0:512], b.pool[banks[3]][0:1, 0:512]]
    bcm, bcr = b.pool[banks[4]], b.pool[banks[5]]
    sq = scr["sq"]
    mean = scr["mean"][:, 0:width]
    var = scr["var"][:, 0:width]
    rstd = scr["rstd"][:, 0:width]

    sq_used = [NONE, NONE]
    for c in range(DC):
        for j, (n0, n1) in enumerate(pieces):
            @b.on("scalar")
            def _(s, c=c, j=j, n0=n0, n1=n1):
                if c == 0 and j == 0:
                    b.waits(s, *dep_marks)
                b.waits(s, sq_used[j])
                b.sems["s"].inc(s.activation(out=sq[j][:, 0:n1 - n0],
                                             in_=resid_tiles[c][:, n0:n1],
                                             func=AF.Square))
            sq_ready = b.mark("s")

            @b.on("tensor")
            def _(t, c=c, j=j, n0=n0, n1=n1, sq_ready=sq_ready):
                if c == 0 and j == 0:
                    b.waits(t, *dep_marks)
                    for bk in banks:
                        b.waits(t, b.bank_free[bk])
                b.waits(t, sq_ready)
                t.matmul(out=sx[j][:, 0:n1 - n0], lhsT=C["ones_col"][:, :],
                         rhs=resid_tiles[c][:, n0:n1], start=(c == 0),
                         stop=(c == DC - 1))
                ins = t.matmul(out=sqp[j][:, 0:n1 - n0], lhsT=C["ones_col"][:, :],
                               rhs=sq[j][:, 0:n1 - n0], start=(c == 0),
                               stop=(c == DC - 1))
                b.sems["t"].inc(ins)
            sq_used[j] = b.mark("t")

    mm_done = b.mark("t")

    @b.on("vector")
    def _(v):
        b.waits(v, mm_done)
        for j, (n0, n1) in enumerate(pieces):
            v.tensor_scalar_mul(out=mean[:, n0:n1], in0=sx[j][:, 0:n1 - n0],
                                scalar1=1.0 / D)
            v.tensor_scalar_mul(out=var[:, n0:n1], in0=sqp[j][:, 0:n1 - n0],
                                scalar1=1.0 / D)
        b.chain(v, "v", v.tensor_tensor(out=rstd, in0=mean, in1=mean,
                                        op=ALU.mult))
        b.sems["v"].inc(v.tensor_tensor(out=var, in0=var, in1=rstd,
                                        op=ALU.subtract))
    stats1 = b.mark("v")

    @b.on("scalar")
    def _(s):
        b.waits(s, stats1)
        b.sems["s"].inc(s.activation(out=rstd, in_=var, func=AF.Sqrt,
                                     bias=C["eps"][0:1, :]))
    stats2 = b.mark("s")

    @b.on("vector")
    def _(v):
        b.waits(v, stats2)
        with b.nc.allow_low_precision(reason="f32r LN stats"):
            b.sems["v"].inc(v.reciprocal(out=rstd, in_=rstd))
    stats3 = b.mark("v")

    for bk in banks[0:4]:
        b.bank_free[bk] = [stats1]

    prev_norm = None
    for j, (n0, n1) in enumerate(pieces):
        @b.on("tensor")
        def _(t, j=j, n0=n0, n1=n1, prev_norm=prev_norm):
            b.waits(t, stats3 if j == 0 else prev_norm)
            t.matmul(out=bcm[:, 0:n1 - n0], lhsT=C["ones_row"][:, :],
                     rhs=mean[:, n0:n1], start=True, stop=True)
            ins = t.matmul(out=bcr[:, 0:n1 - n0], lhsT=C["ones_row"][:, :],
                           rhs=rstd[:, n0:n1], start=True, stop=True)
            b.sems["t"].inc(ins)
        bc_done = b.mark("t")

        for c in range(DC):
            pw = pre_wait_fn(c, j) if pre_wait_fn else NONE

            @b.on("vector")
            def _(v, c=c, n0=n0, n1=n1, bc_done=bc_done, pw=pw):
                if c == 0:
                    b.waits(v, bc_done)
                b.waits(v, pw)
                v.tensor_tensor(out=out_aps[c][:, n0:n1],
                                in0=resid_tiles[c][:, n0:n1],
                                in1=bcm[:, 0:n1 - n0], op=ALU.subtract)
                ins = v.tensor_tensor(out=out_aps[c][:, n0:n1],
                                      in0=out_aps[c][:, n0:n1],
                                      in1=bcr[:, 0:n1 - n0], op=ALU.mult)
                b.sems["v"].inc(ins)
            if post_fn is not None:
                post_fn(c, j, b.mark("v"))
        prev_norm = b.mark("v")

    done = prev_norm
    for bk in banks[4:6]:
        b.bank_free[bk] = [done]
    return done



def declare_l0attn(nc):
    io = {}
    io["x_loc"] = nc.dram_tensor("x_loc", [T + 2, D], F32, kind="ExternalInput").ap()
    io["pcc"] = nc.dram_tensor("pcc", [4096, 22], F32, kind="ExternalInput").ap()
    io["w_in_t"] = nc.dram_tensor("w_in_t", [64, 128, DC * 128], F32R,
                                  kind="ExternalInput").ap()
    io["w_x_t"] = nc.dram_tensor("w_x_t", [128, 32 * 160], F32R,
                                 kind="ExternalInput").ap()
    io["w_dt_t"] = nc.dram_tensor("w_dt_t", [128, 4096], F32R,
                                  kind="ExternalInput").ap()
    io["w_out_t"] = nc.dram_tensor("w_out_t", [16, 128, 32 * 128], F32R,
                                   kind="ExternalInput").ap()
    io["xfm1"] = nc.dram_tensor("xfm1", [D, T], F32, kind="ExternalOutput").ap()
    for nm, dt_ in [("xc_h", F32R), ("z_h", F32R), ("y_h", F32), ("cds_h", F32),
                    ("yfin_h", F32R)]:
        io[nm] = nc.dram_tensor(nm, [4096, T], dt_, kind="Internal").ap()
    io["ccFD_in"] = nc.dram_tensor("ccFD_in", [128, 1024], F32, kind="Internal").ap()
    io["ccFD_out"] = nc.dram_tensor("ccFD_out", [1024, 1024], F32, kind="Internal").ap()
    return io


def host_l0attn(inp, core):
    x = np.asarray(inp["L0_attn_x"], np.float32).reshape(8192, D)
    t0 = core * T
    lo, hi = t0 - 1, t0 + T + 1
    xs = x[max(lo, 0):min(hi, 8192)]
    x_loc = np.pad(xs, ((max(0, -lo), max(0, hi - 8192)), (0, 0)))
    assert x_loc.shape == (T + 2, D)
    pcc = np.concatenate([
        np.asarray(inp["L0_attn_conv_w"], np.float32).reshape(4096, 3),
        np.asarray(inp["L0_attn_conv_b"], np.float32).reshape(4096, 1),
        np.asarray(inp["L0_attn_b_dt"], np.float32).reshape(4096, 1),
        np.asarray(inp["L0_attn_D"], np.float32).reshape(4096, 1),
        np.asarray(inp["L0_attn_A_log"], np.float32).reshape(4096, 16),
    ], axis=1)
    return dict(x_loc=np.ascontiguousarray(x_loc), pcc=np.ascontiguousarray(pcc))


def host_l0attn_weights(inp):
    w_in = np.asarray(inp["L0_attn_W_in"], np.float32)
    w_x = np.asarray(inp["L0_attn_W_x"], np.float32)
    w_dt = np.asarray(inp["L0_attn_W_dt"], np.float32)
    w_out = np.asarray(inp["L0_attn_W_out"], np.float32)
    w_in_t = host_tile_w_ocmajor(w_in)
    w_x_t = np.ascontiguousarray(
        w_x.reshape(32, 128, 160).transpose(1, 0, 2).reshape(128, 32 * 160))
    w_out_t = host_tile_w_ocmajor(w_out)
    return dict(w_in_t=w_in_t, w_x_t=w_x_t, w_dt_t=np.ascontiguousarray(w_dt),
                w_out_t=w_out_t)


def build_l0attn(b, C, io):
    ar = Arena(b, "ar1", 47936)
    pcc_all = ar.f32(0, 704)
    A_all = ar.f32(704, 512)
    F_all = ar.f32(1216, 512)
    D_all = ar.f32(1728, 512)
    H_all = ar.f32(2240, 512)
    ones_t = ar.f32(2752, 1024)
    dtin_fr = ar.f32r(3776, 1024)
    bc_fr = ar.f32r(4800, 1024, p=32)
    LNS = 5824
    ln_scr = dict(sq=[ar.f32r(LNS, 512), ar.f32r(LNS + 512, 512)],
                  mean=ar.f32r(LNS + 1024, 1024, p=1),
                  var=ar.f32r(LNS + 2048, 1024, p=1),
                  rstd=ar.f32r(LNS + 3072, 1024, p=1))
    OA = LNS + 4096
    out_acc = [ar.f32r(OA + c * 1024, 1024) for c in range(DC)]
    RS = OA + 16384

    xfm = [ar.f32r(RS + c * 1032, 1026) for c in range(DC)]
    xrow = [ar.f32(OA + i * 2048, 2048) for i in range(2)]
    win_wb = [ar.f32r(OA + 4096 + i * 2048, 2048) for i in range(2)]
    xp_sb = ar.f32(OA + 8192, 1026)
    xc_st = ar.f32(OA + 9224, 1024)
    xc_frt = ar.f32r(OA + 10248, 1024)
    z_stg = ar.f32r(OA + 11272, 512)
    dt_j = ar.f32(RS + 0, 1024)
    spB = ar.f32(RS + 1024, 1024)
    cds_j = ar.f32(RS + 2048, 1024)
    v_j = ar.f32(RS + 3072, 1024)
    dA_t = ar.f32(RS + 4096, 1024)
    w_t = ar.f32(RS + 5120, 1024)
    h_t = ar.f32(RS + 6144, 1024)
    Bsb = ar.f32(RS + 7168, 1024)
    Csb = ar.f32(RS + 8192, 1024)
    y_j = ar.f32(RS + 9216, 1024)
    xcld = [ar.f32r(RS + 10240 + i * 1024, 1024) for i in range(2)]
    Brow = [ar.f32r(RS + 12288 + i * 1024, 1024, p=1) for i in range(2)]
    Crow = [ar.f32r(RS + 14336 + i * 1024, 1024, p=1) for i in range(2)]
    FD_r = ar.f32(RS + 0, 1024)
    Dm_s = ar.f32(RS + 1024, 512)
    yld = ar.f32(RS + 3072, 1024)
    yldr = ar.f32r(RS + 3072, 1024)
    xcld2 = ar.f32r(RS + 5120, 1024)
    t9 = ar.f32(RS + 1024, 1024)
    zld2 = ar.f32r(RS + 7168, 1024)
    ybufs = [ar.f32r(RS + i * 1024, 1024) for i in range(8)]
    wout_wb = [ar.f32r(RS + 8192 + i * 4096, 4096) for i in range(2)]
    xrow2 = [ar.f32(RS + 10240 + i * 2048, 2048) for i in range(2)]
    lnout = [ar.f32r(RS + 14336, 1024), ar.f32r(RS + 15360, 1024)]

    sld = b.sems["ld"]
    sld2 = b.sems["ld2"]
    sst = b.sems["st"]

    wx_all = ar.f32r(42816, 5120)
    wdt_all = ar.f32r(OA + 11784, 4096)

    @b.on("sync")
    def _(q):
        b.waits(q, *C["ready"])
        sld.inc(q.dma_start(out=wx_all[:], in_=io["w_x_t"][:, :]), 16)
        sld.inc(q.dma_start(out=wdt_all[:], in_=io["w_dt_t"][:, :]), 16)
        for j in range(32):
            sld.inc(q.dma_start(out=pcc_all[:, j * 22:(j + 1) * 22],
                                in_=io["pcc"][j * 128:(j + 1) * 128, :]), 16)
    pcc_ld = b.mark("ld")

    @b.on("scalar")
    def _(s):
        b.waits(s, pcc_ld)
        for j in range(32):
            ins = s.activation(out=A_all[:, j * 16:(j + 1) * 16],
                               in_=pcc_all[:, j * 22 + 6:j * 22 + 22], func=AF.Exp)
        b.sems["s"].inc(ins)
    aexp = b.mark("s")

    @b.on("vector")
    def _(v):
        b.waits(v, aexp)
        v.tensor_scalar_mul(out=A_all, in0=A_all, scalar1=-1.0)
        b.sems["v"].inc(v.memset(ones_t, 1.0))
    a_ready = b.mark("v")

    row_sizes = [(i * 128, min(T + 2, (i + 1) * 128)) for i in range(9)]
    xrow_free = [NONE, NONE]
    for i, (r0, r1) in enumerate(row_sizes):
        nr = r1 - r0

        @b.on("sync")
        def _(q, i=i, r0=r0, r1=r1):
            b.waits(q, xrow_free[i % 2])
            if i == 0:
                b.waits(q, *C["ready"])
            sld.inc(q.dma_start(out=xrow[i % 2][0:r1 - r0, 0:D],
                                in_=io["x_loc"][r0:r1, :]), 16)
        xld = b.mark("ld")

        for c in range(DC):
            bk = 6 + (i * DC + c) % 2

            @b.on("tensor")
            def _(t, i=i, c=c, nr=nr, bk=bk, xld=xld):
                if c == 0:
                    b.waits(t, xld)
                b.waits(t, b.bank_free[bk])
                ins = t.transpose(out=b.pool[bk][0:128, 0:nr],
                                  in_=xrow[i % 2][0:nr, c * 128:(c + 1) * 128],
                                  identity=C["ident"][0:nr, 0:nr])
                b.sems["t"].inc(ins)
            tr_mark = b.mark("t")

            @b.on("scalar")
            def _(s, c=c, r0=r0, nr=nr, bk=bk, tr_mark=tr_mark):
                b.waits(s, tr_mark)
                ins = s.activation(out=xfm[c][:, r0:r0 + nr],
                                   in_=b.pool[bk][0:128, 0:nr], func=AF.Copy)
                b.sems["s"].inc(ins)
            b.bank_free[bk] = [b.mark("s")]
        xrow_free[i % 2] = [b.mark("t")]
    xfm_ready = b.mark("s")

    win_free = [NONE, NONE]
    xp_free = NONE
    xcst_free = NONE
    xcfr_free = NONE
    zstg_free = NONE
    xp_tts = [(0, 512), (512, 1024), (1024, 1026)]
    z_tts = [(1, 513), (513, 1025)]
    wx_last = None

    for oc in range(64):
        is_xp = oc < 32
        j = oc if is_xp else oc - 32
        tts = xp_tts if is_xp else z_tts

        @b.on("sync")
        def _(q, oc=oc):
            b.waits(q, win_free[oc % 2])
            if oc == 0:
                b.waits(q, xfm_ready)
            sld.inc(q.dma_start(out=win_wb[oc % 2][:], in_=io["w_in_t"][oc, :, :]), 16)
        wld = b.mark("ld")

        evac_last = None
        for ti, (n0, n1) in enumerate(tts):
            bk = 4 + (oc * 3 + ti) % 2

            @b.on("tensor")
            def _(t, oc=oc, ti=ti, n0=n0, n1=n1, bk=bk, wld=wld):
                if ti == 0:
                    b.waits(t, wld)
                    if oc == 0:
                        b.waits(t, xfm_ready)
                b.waits(t, b.bank_free[bk])
                for k in range(DC):
                    ins = t.matmul(out=b.pool[bk][:, 0:n1 - n0],
                                   lhsT=win_wb[oc % 2][:, k * 128:(k + 1) * 128],
                                   rhs=xfm[k][:, n0:n1],
                                   start=(k == 0), stop=(k == DC - 1))
                b.sems["t"].inc(ins)
            mm_mark = b.mark("t")

            if is_xp:
                @b.on("scalar")
                def _(s, n0=n0, n1=n1, bk=bk, mm_mark=mm_mark):
                    b.waits(s, mm_mark, xp_free)
                    ins = s.copy(out=xp_sb[:, n0:n1], in_=b.pool[bk][:, 0:n1 - n0])
                    b.sems["s"].inc(ins)
                b.bank_free[bk] = [b.mark("s")]
                evac_last = b.mark("s")
            else:
                @b.on("scalar")
                def _(s, n0=n0, n1=n1, bk=bk, mm_mark=mm_mark):
                    b.waits(s, mm_mark, zstg_free)
                    ins = s.activation(out=z_stg[:, 0:n1 - n0],
                                       in_=b.pool[bk][:, 0:n1 - n0], func=AF.Silu)
                    b.sems["s"].inc(ins)
                b.bank_free[bk] = [b.mark("s")]
                zmk = b.mark("s")

                @b.on("gpsimd")
                def _(p, j=j, n0=n0, n1=n1, zmk=zmk):
                    b.waits(p, zmk)
                    sst.inc(p.dma_start(
                        out=io["z_h"][j * 128:(j + 1) * 128, n0 - 1:n1 - 1],
                        in_=z_stg[:, 0:n1 - n0]), 16)
                zstg_free = [("st", b.sems["st"].n)]

        if is_xp:
            @b.on("vector")
            def _(v, j=j, evac_last=evac_last):
                b.waits(v, evac_last, xcst_free)
                if j == 0:
                    b.waits(v, a_ready)
                cw = pcc_all[:, j * 22:j * 22 + 3]
                v.tensor_scalar_mul(out=xc_st, in0=xp_sb[:, 0:1024],
                                    scalar1=cw[:, 0:1])
                v.scalar_tensor_tensor(out=xc_st, in0=xp_sb[:, 1:1025],
                                       scalar=cw[:, 1:2], in1=xc_st,
                                       op0=ALU.mult, op1=ALU.add)
                ins = v.scalar_tensor_tensor(out=xc_st, in0=xp_sb[:, 2:1026],
                                             scalar=cw[:, 2:3], in1=xc_st,
                                             op0=ALU.mult, op1=ALU.add)
                b.sems["v"].inc(ins)
            xp_free = [b.mark("v")]
            conv_done = b.mark("v")

            @b.on("scalar")
            def _(s, j=j, conv_done=conv_done):
                b.waits(s, conv_done, xcfr_free)
                ins = s.activation(out=xc_frt, in_=xc_st, func=AF.Silu,
                                   bias=pcc_all[:, j * 22 + 3:j * 22 + 4])
                b.sems["s"].inc(ins)
            xcst_free = [b.mark("s")]
            silu_done = b.mark("s")

            @b.on("tensor")
            def _(t, j=j, silu_done=silu_done):
                b.waits(t, silu_done)
                for tt in range(2):
                    m0 = tt * 512
                    t.matmul(out=b.pool[tt][:, :],
                             lhsT=wx_all[:, j * 160:j * 160 + 128],
                             rhs=xc_frt[:, m0:m0 + 512],
                             start=(j == 0), stop=(j == 31))
                    ins = t.matmul(out=b.pool[2 + tt][0:32, :],
                                   lhsT=wx_all[:, j * 160 + 128:j * 160 + 160],
                                   rhs=xc_frt[:, m0:m0 + 512],
                                   start=(j == 0), stop=(j == 31))
                b.sems["t"].inc(ins)
            wx_mark = b.mark("t")
            wx_last = wx_mark

            @b.on("gpsimd")
            def _(p, j=j, silu_done=silu_done):
                b.waits(p, silu_done)
                sst.inc(p.dma_start(out=io["xc_h"][j * 128:(j + 1) * 128, :],
                                    in_=xc_frt[:, :]), 16)
            xcfr_free = [wx_mark, ("st", b.sems["st"].n)]

        win_free[oc % 2] = [b.mark("t")]

    xc_all_stored = b.mark("st")
    wx_all_done = wx_last

    @b.on("scalar")
    def _(s):
        b.waits(s, wx_all_done)
        for tt in range(2):
            s.activation(out=dtin_fr[:, tt * 512:(tt + 1) * 512],
                         in_=b.pool[tt][:, :], func=AF.Copy)
            ins = s.activation(out=bc_fr[:, tt * 512:(tt + 1) * 512],
                               in_=b.pool[2 + tt][0:32, :], func=AF.Copy)
        b.sems["s"].inc(ins)
    xdbl_done = b.mark("s")
    for bk in range(4):
        b.bank_free[bk] = [xdbl_done]

    brow_free = [NONE, NONE]
    crow_free = [NONE, NONE]
    xcld_free = [NONE, NONE]
    dtj_free = NONE
    spB_free = NONE
    cdsj_free = NONE
    yj_free = NONE
    dA_free = NONE
    bsb_free = NONE
    wt_free = NONE

    for j in range(32):
        @b.on("tensor")
        def _(t, j=j):
            if j == 0:
                b.waits(t, xdbl_done)
            for tt in range(2):
                b.waits(t, b.bank_free[tt])
                ins = t.matmul(out=b.pool[tt][:, :],
                               lhsT=wdt_all[:, j * 128:(j + 1) * 128],
                               rhs=dtin_fr[:, tt * 512:(tt + 1) * 512],
                               start=True, stop=True)
            b.sems["t"].inc(ins)
        dtp_mark = b.mark("t")

        bcol = pcc_all[:, j * 22 + 4:j * 22 + 5]

        @b.on("vector")
        def _(v, dtp_mark=dtp_mark, bcol=bcol):
            b.waits(v, dtp_mark, dtj_free)
            for tt in range(2):
                n0 = tt * 512
                ins = v.tensor_scalar(out=dt_j[:, n0:n0 + 512],
                                      in0=b.pool[tt][:, :], scalar1=bcol,
                                      scalar2=0.0, op0=ALU.add, op1=ALU.max)
            b.sems["v"].inc(ins)
        m_done = b.mark("v")

        @b.on("scalar")
        def _(s, dtp_mark=dtp_mark, bcol=bcol):
            b.waits(s, dtp_mark, spB_free)
            for tt in range(2):
                n0 = tt * 512
                s.activation(out=spB[:, n0:n0 + 512], in_=b.pool[tt][:, :],
                             func=AF.Abs, bias=bcol)
            s.activation(out=spB, in_=spB, func=AF.Exp, scale=C["negone"])
            ins = s.activation(out=spB, in_=spB, func=AF.Ln, bias=C["one"])
            b.sems["s"].inc(ins)
        sp_done = b.mark("s")
        b.bank_free[0] = [m_done, sp_done]
        b.bank_free[1] = [m_done, sp_done]

        @b.on("vector")
        def _(v, sp_done=sp_done):
            b.waits(v, sp_done)
            v.tensor_tensor(out=dt_j, in0=dt_j, in1=spB, op=ALU.add)
            b.waits(v, cdsj_free)
            ins = v.tensor_tensor_scan(out=cds_j, data0=ones_t, data1=dt_j,
                                       initial=0.0, op0=ALU.mult, op1=ALU.add)
            b.sems["v"].inc(ins)
        spB_free = [b.mark("v")]
        cds_done = b.mark("v")

        @b.on("sync")
        def _(q, j=j):
            if j == 0:
                b.waits(q, xc_all_stored)
            b.waits(q, xcld_free[j % 2])
            sld2.inc(q.dma_start(out=xcld[j % 2][:, :],
                                 in_=io["xc_h"][j * 128:(j + 1) * 128, :]), 16)
        xc_ld = b.mark("ld2")

        @b.on("vector")
        def _(v, j=j, xc_ld=xc_ld):
            b.waits(v, xc_ld, yj_free)
            v.tensor_tensor(out=v_j, in0=dt_j, in1=xcld[j % 2], op=ALU.mult)
            ins = v.memset(y_j, 0.0)
            b.sems["v"].inc(ins)
        vj_done = b.mark("v")
        xcld_free[j % 2] = [vj_done]

        for si in range(16):
            @b.on("scalar")
            def _(s, j=j, si=si):
                b.waits(s, dA_free)
                ins = s.activation(out=dA_t, in_=dt_j, func=AF.Exp,
                                   scale=A_all[:, j * 16 + si:j * 16 + si + 1])
                b.sems["s"].inc(ins)
            dA_done = b.mark("s")

            @b.on("gpsimd")
            def _(p, j=j, si=si):
                if j == 0 and si == 0:
                    b.waits(p, xdbl_done)
                b.waits(p, brow_free[si % 2])
                sst.inc(p.dma_start(out=Brow[si % 2][:, :],
                                    in_=bc_fr[si:si + 1, :]), 16)
                b.waits(p, crow_free[si % 2])
                sst.inc(p.dma_start(out=Crow[si % 2][:, :],
                                    in_=bc_fr[16 + si:17 + si, :]), 16)
            row_ld = b.mark("st")

            @b.on("tensor")
            def _(t, si=si, row_ld=row_ld):
                b.waits(t, row_ld)
                for tt in range(2):
                    n0 = tt * 512
                    b.waits(t, b.bank_free[2 + tt])
                    t.matmul(out=b.pool[2 + tt][:, :], lhsT=C["ones_row"][:, :],
                             rhs=Brow[si % 2][:, n0:n0 + 512], start=True,
                             stop=True)
                    b.waits(t, b.bank_free[4 + tt])
                    ins = t.matmul(out=b.pool[4 + tt][:, :],
                                   lhsT=C["ones_row"][:, :],
                                   rhs=Crow[si % 2][:, n0:n0 + 512], start=True,
                                   stop=True)
                b.sems["t"].inc(ins)
            bc_mm = b.mark("t")
            brow_free[si % 2] = [bc_mm]
            crow_free[si % 2] = [bc_mm]

            @b.on("scalar")
            def _(s, bc_mm=bc_mm):
                b.waits(s, bc_mm, bsb_free)
                for tt in range(2):
                    n0 = tt * 512
                    s.copy(out=Bsb[:, n0:n0 + 512], in_=b.pool[2 + tt][:, :])
                    ins = s.copy(out=Csb[:, n0:n0 + 512], in_=b.pool[4 + tt][:, :])
                b.sems["s"].inc(ins)
            bc_evac = b.mark("s")
            for bk in [2, 3, 4, 5]:
                b.bank_free[bk] = [bc_evac]

            @b.on("vector")
            def _(v, j=j, si=si, bc_evac=bc_evac, dA_done=dA_done,
                  vj_done=vj_done):
                b.waits(v, bc_evac, dA_done, wt_free)
                if si == 0:
                    b.waits(v, vj_done)
                v.tensor_tensor(out=w_t, in0=v_j, in1=Bsb, op=ALU.mult)
                v.tensor_tensor_scan(out=h_t, data0=dA_t, data1=w_t,
                                     initial=0.0, op0=ALU.mult, op1=ALU.add)
                v.tensor_copy(out=F_all[:, j * 16 + si:j * 16 + si + 1],
                              in_=h_t[:, 1023:1024])
                ins = v.tensor_tensor(out=w_t, in0=h_t, in1=Csb, op=ALU.mult)
                b.sems["v"].inc(ins)
            hC_done = b.mark("v")
            dA_free = [hC_done]
            bsb_free = [hC_done]

            @b.on("gpsimd")
            def _(p, hC_done=hC_done):
                b.waits(p, hC_done)
                ins = p.tensor_tensor(out=y_j, in0=y_j, in1=w_t, op=ALU.add)
                b.sems["p"].inc(ins)
            wt_free = [b.mark("p")]
        ydone = b.mark("p")

        @b.on("gpsimd")
        def _(p, j=j, cds_done=cds_done):
            sst.inc(p.dma_start(out=io["y_h"][j * 128:(j + 1) * 128, :],
                                in_=y_j[:, :]), 16)
            b.waits(p, cds_done)
            sst.inc(p.dma_start(out=io["cds_h"][j * 128:(j + 1) * 128, :],
                                in_=cds_j[:, :]), 16)
        ycds_st = ("st", b.sems["st"].n)
        yj_free = [ycds_st]

        @b.on("scalar")
        def _(s, j=j, cds_done=cds_done):
            b.waits(s, cds_done)
            ins = s.activation(out=D_all[:, j * 16:(j + 1) * 16],
                               in_=A_all[:, j * 16:(j + 1) * 16], func=AF.Exp,
                               scale=cds_j[:, 1023:1024])
            b.sems["s"].inc(ins)
        dall_done = b.mark("s")
        dtj_free = [dA_free[0], m_done]
        cdsj_free = [ycds_st, dall_done]

    scan1_done = [b.mark("p"), b.mark("s"), b.mark("st"), b.mark("v"),
                  b.mark("t")]

    @b.on("gpsimd")
    def _(p):
        b.waits(p, *scan1_done)
        sst.inc(p.dma_start(out=io["ccFD_in"][:, 0:512], in_=F_all[:, :]), 16)
        sst.inc(p.dma_start(out=io["ccFD_in"][:, 512:1024], in_=D_all[:, :]), 16)
        p.wait_ge(b.sems["st"].h, b.sems["st"].n)
        b.sems["cc"].inc(p.collective_compute(
            "AllGather", ALU.bypass, replica_groups=RG,
            ins=[io["ccFD_in"].opt()], outs=[io["ccFD_out"].opt()]))
    cc_done = b.mark("cc")

    @b.on("vector")
    def _(v):
        b.waits(v, cc_done)
        b.sems["v"].inc(v.memset(H_all, 0.0))
    fd_prev = b.mark("v")
    for r in range(8):
        @b.on("sync")
        def _(q, r=r, fd_prev=fd_prev):
            b.waits(q, fd_prev)
            sld2.inc(q.dma_start(out=FD_r[:, :],
                                 in_=io["ccFD_out"][r * 128:(r + 1) * 128, :]), 16)
        fd_ld = b.mark("ld2")

        @b.on("vector")
        def _(v, r=r, fd_ld=fd_ld):
            b.waits(v, fd_ld)
            v.tensor_scalar(out=Dm_s, in0=FD_r[:, 512:1024],
                            scalar1=C["lt"][:, r:r + 1],
                            scalar2=C["ltinv"][:, r:r + 1],
                            op0=ALU.mult, op1=ALU.add)
            v.tensor_tensor(out=H_all, in0=H_all, in1=Dm_s, op=ALU.mult)
            v.tensor_scalar_mul(out=Dm_s, in0=FD_r[:, 0:512],
                                scalar1=C["lt"][:, r:r + 1])
            ins = v.tensor_tensor(out=H_all, in0=H_all, in1=Dm_s, op=ALU.add)
            b.sems["v"].inc(ins)
        fd_prev = b.mark("v")
    H_done = fd_prev

    yld_free = NONE
    cds2_free = NONE
    xc2_free = NONE
    z2_free = NONE
    crow_free2 = crow_free
    csb2_free = bsb_free
    e_free = dA_free
    t9_free = NONE

    for j in range(32):
        @b.on("sync")
        def _(q, j=j):
            if j == 0:
                b.waits(q, H_done, *scan1_done)
            b.waits(q, yld_free, cds2_free, xc2_free, z2_free)
            sld2.inc(q.dma_start(out=yld[:, :],
                                 in_=io["y_h"][j * 128:(j + 1) * 128, :]), 16)
            sld2.inc(q.dma_start(out=cds_j[:, :],
                                 in_=io["cds_h"][j * 128:(j + 1) * 128, :]), 16)
            sld2.inc(q.dma_start(out=xcld2[:, :],
                                 in_=io["xc_h"][j * 128:(j + 1) * 128, :]), 16)
            sld2.inc(q.dma_start(out=zld2[:, :],
                                 in_=io["z_h"][j * 128:(j + 1) * 128, :]), 16)
        ph2_ld = b.mark("ld2")

        for si in range(16):
            @b.on("scalar")
            def _(s, j=j, si=si, ph2_ld=ph2_ld):
                if si == 0:
                    b.waits(s, ph2_ld)
                b.waits(s, e_free)
                ins = s.activation(out=dA_t, in_=cds_j, func=AF.Exp,
                                   scale=A_all[:, j * 16 + si:j * 16 + si + 1])
                b.sems["s"].inc(ins)
            e_done = b.mark("s")

            @b.on("gpsimd")
            def _(p, si=si):
                b.waits(p, crow_free2[si % 2])
                sst.inc(p.dma_start(out=Crow[si % 2][:, :],
                                    in_=bc_fr[16 + si:17 + si, :]), 16)
            crow_ld = b.mark("st")

            @b.on("tensor")
            def _(t, si=si, crow_ld=crow_ld):
                b.waits(t, crow_ld)
                for tt in range(2):
                    n0 = tt * 512
                    b.waits(t, b.bank_free[2 + tt])
                    ins = t.matmul(out=b.pool[2 + tt][:, :],
                                   lhsT=C["ones_row"][:, :],
                                   rhs=Crow[si % 2][:, n0:n0 + 512],
                                   start=True, stop=True)
                b.sems["t"].inc(ins)
            cmm = b.mark("t")
            crow_free2[si % 2] = [cmm]

            @b.on("scalar")
            def _(s, cmm=cmm):
                b.waits(s, cmm, csb2_free)
                for tt in range(2):
                    n0 = tt * 512
                    ins = s.copy(out=Csb[:, n0:n0 + 512], in_=b.pool[2 + tt][:, :])
                b.sems["s"].inc(ins)
            cevac = b.mark("s")
            b.bank_free[2] = [cevac]
            b.bank_free[3] = [cevac]

            @b.on("vector")
            def _(v, j=j, si=si, e_done=e_done, cevac=cevac, ph2_ld=ph2_ld):
                b.waits(v, e_done, cevac, t9_free)
                ins = v.scalar_tensor_tensor(
                    out=t9, in0=dA_t,
                    scalar=H_all[:, j * 16 + si:j * 16 + si + 1],
                    in1=Csb, op0=ALU.mult, op1=ALU.mult)
                b.sems["v"].inc(ins)
            t9_done = b.mark("v")
            e_free = [t9_done]
            csb2_free = [t9_done]

            @b.on("gpsimd")
            def _(p, j=j, si=si, t9_done=t9_done, ph2_ld=ph2_ld):
                b.waits(p, t9_done)
                if si == 0:
                    b.waits(p, ph2_ld)
                ins = p.tensor_tensor(out=yld, in0=yld, in1=t9, op=ALU.add)
                b.sems["p"].inc(ins)
            t9_free = [b.mark("p")]
        cds2_free = e_free
        corr_done = b.mark("p")

        @b.on("vector")
        def _(v, j=j, corr_done=corr_done):
            b.waits(v, corr_done)
            ins = v.scalar_tensor_tensor(out=yld, in0=xcld2,
                                         scalar=pcc_all[:, j * 22 + 5:j * 22 + 6],
                                         in1=yld, op0=ALU.mult, op1=ALU.add)
            b.sems["v"].inc(ins)
        d_done = b.mark("v")
        xc2_free = [d_done]

        @b.on("vector")
        def _(v, d_done=d_done, ph2_ld=ph2_ld):
            b.waits(v, d_done)
            ins = v.tensor_tensor(out=yldr, in0=yld, in1=zld2, op=ALU.mult)
            b.sems["v"].inc(ins)
        yfin_done = b.mark("v")
        z2_free = [yfin_done]

        @b.on("gpsimd")
        def _(p, j=j, yfin_done=yfin_done):
            b.waits(p, yfin_done)
            sst.inc(p.dma_start(out=io["yfin_h"][j * 128:(j + 1) * 128, :],
                                in_=yldr[:, :]), 16)
        yld_free = [("st", b.sems["st"].n)]
    yfin_all = b.mark("st")
    ph2_all = [b.mark("p"), b.mark("s"), b.mark("v"), b.mark("t"), yfin_all]

    ybuf_free = [ph2_all] * 8
    wout_free = [ph2_all, ph2_all]
    for oc in range(16):
        @b.on("sync")
        def _(q, oc=oc):
            if oc == 0:
                b.waits(q, yfin_all)
            b.waits(q, wout_free[oc % 2])
            sld.inc(q.dma_start(out=wout_wb[oc % 2][:],
                                in_=io["w_out_t"][oc, :, :]), 16)
        wo_ld = b.mark("ld")

        grp_marks = []
        for grp in range(4):
            @b.on("sync")
            def _(q, oc=oc, grp=grp):
                for k in range(8):
                    b.waits(q, ybuf_free[k])
                    kk = grp * 8 + k
                    sld2.inc(q.dma_start(
                        out=ybufs[k][:, :],
                        in_=io["yfin_h"][kk * 128:(kk + 1) * 128, :]), 16)
            y_ld = b.mark("ld2")

            for tt in range(2):
                bk = 6 + tt

                @b.on("tensor")
                def _(t, oc=oc, grp=grp, tt=tt, bk=bk, y_ld=y_ld, wo_ld=wo_ld):
                    b.waits(t, y_ld)
                    if grp == 0:
                        b.waits(t, wo_ld, b.bank_free[bk])
                    n0 = tt * 512
                    for k in range(8):
                        kk = grp * 8 + k
                        ins = t.matmul(out=b.pool[bk][:, :],
                                       lhsT=wout_wb[oc % 2][:, kk * 128:(kk + 1) * 128],
                                       rhs=ybufs[k][:, n0:n0 + 512],
                                       start=(grp == 0 and k == 0),
                                       stop=(grp == 3 and k == 7))
                    b.sems["t"].inc(ins)
            gm = b.mark("t")
            for k in range(8):
                ybuf_free[k] = [gm]
        wout_free[oc % 2] = [b.mark("t")]
        oc_mm = b.mark("t")

        @b.on("vector")
        def _(v, oc=oc, oc_mm=oc_mm):
            b.waits(v, oc_mm)
            for tt in range(2):
                n0 = tt * 512
                ins = v.tensor_copy(out=out_acc[oc][:, n0:n0 + 512],
                                    in_=b.pool[6 + tt][:, :])
            b.sems["v"].inc(ins)
        b.bank_free[6] = [b.mark("v")]
        b.bank_free[7] = [b.mark("v")]
    wout_done = b.mark("v")

    xrow2_free = [NONE, NONE]
    for i in range(8):
        @b.on("sync")
        def _(q, i=i):
            if i < 2:
                b.waits(q, wout_done)
            b.waits(q, xrow2_free[i % 2])
            sld.inc(q.dma_start(out=xrow2[i % 2][0:128, 0:D],
                                in_=io["x_loc"][1 + i * 128:1 + (i + 1) * 128, :]),
                    16)
        x2_ld = b.mark("ld")

        for c in range(DC):
            bk = (i * DC + c) % 2

            @b.on("tensor")
            def _(t, i=i, c=c, bk=bk, x2_ld=x2_ld):
                if c == 0:
                    b.waits(t, x2_ld)
                b.waits(t, b.bank_free[bk])
                if i == 0 and c == 0:
                    b.waits(t, wout_done)
                ins = t.transpose(out=b.pool[bk][0:128, 0:128],
                                  in_=xrow2[i % 2][0:128, c * 128:(c + 1) * 128],
                                  identity=C["ident"][:, :])
                b.sems["t"].inc(ins)
            tr_mark = b.mark("t")

            @b.on("vector")
            def _(v, i=i, c=c, bk=bk, tr_mark=tr_mark):
                b.waits(v, tr_mark)
                ins = v.tensor_tensor(out=out_acc[c][:, i * 128:(i + 1) * 128],
                                      in0=out_acc[c][:, i * 128:(i + 1) * 128],
                                      in1=b.pool[bk][0:128, 0:128], op=ALU.add)
                b.sems["v"].inc(ins)
            b.bank_free[bk] = [b.mark("v")]
        xrow2_free[i % 2] = [b.mark("t")]
    resid_done = b.mark("v")

    resid_tiles = out_acc
    lnout_free = [NONE, NONE]

    def pre_wait_fn(c, j):
        return lnout_free[c % 2]

    def post_fn(c, j, vmark):
        @b.on("gpsimd")
        def _(p, c=c, j=j, vmark=vmark):
            b.waits(p, vmark)
            n0, n1 = j * 512, (j + 1) * 512
            sst.inc(p.dma_start(out=io["xfm1"][c * 128:(c + 1) * 128, n0:n1],
                                in_=lnout[c % 2].bitcast(F32)[:, n0:n1]), 16)
        lnout_free[c % 2] = [("st", b.sems["st"].n)]

    out_aps = {c: lnout[c % 2] for c in range(DC)}
    emit_ln(b, C, resid_tiles, ln_scr, [resid_done], [0, 1, 2, 3, 4, 5],
            out_aps, pre_wait_fn, post_fn)

    @b.on("gpsimd")
    def _(p):
        p.wait_ge(b.sems["st"].h, b.sems["st"].n)
    return io



def declare_l0ffn(nc):
    io = {}
    io["xin"] = nc.dram_tensor("xin", [D, T], F32R, kind="ExternalInput").ap()
    io["gate_t"] = nc.dram_tensor("gate_t", [128, 256], F32R, kind="ExternalInput").ap()
    io["ebias"] = nc.dram_tensor("ebias", [16, 1], F32, kind="ExternalInput").ap()
    io["wg_t"] = nc.dram_tensor("wg_t", [136, 128, 2048], F32R, kind="ExternalInput").ap()
    io["wu_t"] = nc.dram_tensor("wu_t", [136, 128, 2048], F32R, kind="ExternalInput").ap()
    io["wd_t"] = nc.dram_tensor("wd_t", [272, 128, 1024], F32R, kind="ExternalInput").ap()
    io["xfm2"] = nc.dram_tensor("xfm2", [D, T], F32, kind="ExternalOutput").ap()
    io["dbg_sc"] = nc.dram_tensor("dbg_sc", [16, T], F32, kind="ExternalOutput").ap()
    io["dbg_cw"] = nc.dram_tensor("dbg_cw", [16, T], F32, kind="ExternalOutput").ap()
    io["dbg_dn"] = nc.dram_tensor("dbg_dn", [D, T], F32, kind="ExternalOutput").ap()
    return io


def host_l0ffn_weights(inp):
    wg = np.asarray(inp["L0_ffn_w_gate"], np.float32)
    wu = np.asarray(inp["L0_ffn_w_up"], np.float32)
    wd = np.asarray(inp["L0_ffn_w_down"], np.float32)
    swg = np.asarray(inp["L0_ffn_sw_gate"], np.float32)
    swu = np.asarray(inp["L0_ffn_sw_up"], np.float32)
    swd = np.asarray(inp["L0_ffn_sw_down"], np.float32)
    gate_w = np.asarray(inp["L0_ffn_gate_w"], np.float32)

    def tile_gu(w):
        return host_tile_w_ocmajor(np.ascontiguousarray(w.T))

    wg_t = np.concatenate([tile_gu(wg[e]) for e in range(16)] + [tile_gu(swg)])
    wu_t = np.concatenate([tile_gu(wu[e]) for e in range(16)] + [tile_gu(swu)])

    def tile_d(w):
        return host_tile_w_ocmajor(np.ascontiguousarray(w.T))

    wd_t = np.concatenate([tile_d(wd[e]) for e in range(16)] + [tile_d(swd)])
    gt = np.ascontiguousarray(
        gate_w.T.reshape(16, 128, 16).transpose(1, 0, 2).reshape(128, 256))
    eb = np.ascontiguousarray(np.asarray(inp["L0_ffn_expert_bias"],
                                         np.float32).reshape(16, 1))
    return dict(gate_t=gt, ebias=eb,
                wg_t=np.ascontiguousarray(wg_t),
                wu_t=np.ascontiguousarray(wu_t),
                wd_t=np.ascontiguousarray(wd_t))


def build_l0ffn(b, C, io):
    ar = Arena(b, "ar2", 49500)
    xn = [ar.f32r(c * 1024, 1024) for c in range(DC)]
    dn_acc = [ar.f32(16384 + c * 1024, 1024) for c in range(DC)]
    HG = 32768
    h_half = [ar.f32r(HG + i * 512, 512) for i in range(8)]
    lnout = [ar.f32r(HG, 1024), ar.f32r(HG + 1024, 1024)]
    WB = 36864
    wgb = [ar.f32r(WB, 2048)]
    wub = [ar.f32r(WB + 2048, 2048)]
    wdb = [ar.f32r(WB + 4096 + i * 1024, 1024) for i in range(2)]
    ln_scr = dict(sq=[ar.f32r(WB, 512), ar.f32r(WB + 512, 512)],
                  mean=ar.f32r(WB + 1024, 1024, p=1),
                  var=ar.f32r(WB + 2048, 1024, p=1),
                  rstd=ar.f32r(WB + 3072, 1024, p=1))
    MS = 43008
    gate_all = ar.f32r(MS, 256)
    sc_E = ar.f32(MS + 256, 1024, p=16)
    bb_E = ar.f32(MS + 1280, 1024, p=16)
    cwE = ar.f32r(MS + 2304, 1024, p=16)
    ebias_s = ar.f32(MS + 3328, 8, p=16)
    smallT = ar.f32(MS + 3336, 80)
    bb_T = smallT[:, 0:16]
    scT = smallT[:, 16:32]
    indacc = smallT[:, 32:48]
    tmp16 = smallT[:, 48:64]
    mxcol = smallT[:, 64:65]
    rscol = smallT[:, 65:66]
    cwrow = ar.f32r(MS + 3416, 1024, p=1)
    cwb = ar.f32(MS + 4440, 1024)
    gsb = ar.f32(MS + 5464, 512)
    hh = ar.f32(MS + 5976, 512)

    sld = b.sems["ld"]
    sld2 = b.sems["ld2"]
    sst = b.sems["st"]

    @b.on("sync")
    def _(q):
        b.waits(q, *C["ready"])
        for c in range(DC):
            sld.inc(q.dma_start(out=xn[c][:, :],
                                in_=io["xin"][c * 128:(c + 1) * 128, :]), 16)
        sld.inc(q.dma_start(out=gate_all[:, :], in_=io["gate_t"][:, :]), 16)
        sld.inc(q.dma_start(out=ebias_s[:, 0:1], in_=io["ebias"][:, :]), 16)
    xn_ld = b.mark("ld")

    @b.on("tensor")
    def _(t):
        b.waits(t, xn_ld)
        for tt in range(2):
            n0 = tt * 512
            for k in range(DC):
                ins = t.matmul(out=b.pool[tt][0:16, :],
                               lhsT=gate_all[:, k * 16:(k + 1) * 16],
                               rhs=xn[k][:, n0:n0 + 512],
                               start=(k == 0), stop=(k == DC - 1))
        b.sems["t"].inc(ins)
    lg_mm = b.mark("t")

    @b.on("scalar")
    def _(s):
        b.waits(s, lg_mm)
        for tt in range(2):
            n0 = tt * 512
            ins = s.activation(out=sc_E[:, n0:n0 + 512], in_=b.pool[tt][0:16, :],
                               func=AF.Sigmoid)
        b.sems["s"].inc(ins)
    sc_done = b.mark("s")
    b.bank_free[0] = [sc_done]
    b.bank_free[1] = [sc_done]

    @b.on("vector")
    def _(v):
        b.waits(v, sc_done)
        ins = v.tensor_scalar_add(out=bb_E, in0=sc_E, scalar1=ebias_s[:, 0:1])
        b.sems["v"].inc(ins)
    bb_done = b.mark("v")

    for tg in range(8):
        m0 = tg * 128

        @b.on("tensor")
        def _(t, tg=tg, m0=m0):
            if tg == 0:
                b.waits(t, bb_done)
            b.waits(t, b.bank_free[2], b.bank_free[3])
            t.transpose(out=b.pool[2][0:128, 0:16], in_=bb_E[0:16, m0:m0 + 128],
                        identity=C["ident"][0:16, 0:16])
            ins = t.transpose(out=b.pool[3][0:128, 0:16],
                              in_=sc_E[0:16, m0:m0 + 128],
                              identity=C["ident"][0:16, 0:16])
            b.sems["t"].inc(ins)
        tr_done = b.mark("t")

        @b.on("vector")
        def _(v, tg=tg, tr_done=tr_done):
            b.waits(v, tr_done)
            b.chain(v, "v", v.tensor_copy(out=bb_T, in_=b.pool[2][0:128, 0:16]))
            b.chain(v, "v", v.tensor_copy(out=scT, in_=b.pool[3][0:128, 0:16]))
            b.chain(v, "v", v.memset(indacc, 0.0))
            for it in range(4):
                b.chain(v, "v", v.reduce_max(out=mxcol, in_=bb_T, axis=AX.X))
                b.chain(v, "v", v.tensor_single_scalar(out=tmp16, in_=bb_T,
                                                       scalar=mxcol,
                                                       op=ALU.is_ge))
                b.chain(v, "v", v.tensor_tensor(out=indacc, in0=indacc,
                                                in1=tmp16, op=ALU.max))
                b.chain(v, "v", v.scalar_tensor_tensor(out=bb_T, in0=tmp16,
                                                       scalar=-1e30, in1=bb_T,
                                                       op0=ALU.mult, op1=ALU.add))
            b.chain(v, "v", v.tensor_tensor(out=tmp16, in0=indacc, in1=scT,
                                            op=ALU.mult))
            b.chain(v, "v", v.reduce_sum(out=rscol, in_=tmp16, axis=AX.X))
            b.chain(v, "v", v.reciprocal(out=rscol, in_=rscol))
            b.chain(v, "v", v.tensor_scalar_mul(out=tmp16, in0=tmp16,
                                                scalar1=rscol))
        cw_done = b.mark("v")
        b.bank_free[2] = [cw_done]
        b.bank_free[3] = [cw_done]

        @b.on("tensor")
        def _(t, tg=tg, cw_done=cw_done):
            b.waits(t, cw_done, b.bank_free[4])
            ins = t.transpose(out=b.pool[4][0:16, 0:128], in_=tmp16[:, :],
                              identity=C["ident"][:, :])
            b.sems["t"].inc(ins)
        cwT_done = b.mark("t")

        @b.on("scalar")
        def _(s, tg=tg, m0=m0, cwT_done=cwT_done):
            b.waits(s, cwT_done)
            ins = s.activation(out=cwE[:, m0:m0 + 128], in_=b.pool[4][0:16, 0:128],
                               func=AF.Copy)
            b.sems["s"].inc(ins)
        b.bank_free[4] = [b.mark("s")]
    cwE_done = b.mark("s")

    @b.on("gpsimd")
    def _(p):
        b.waits(p, cwE_done)
        sst.inc(p.dma_start(out=io["dbg_sc"][:, :], in_=sc_E[:, :]), 16)
        sst.inc(p.dma_start(out=io["dbg_cw"][:, :],
                            in_=cwE.bitcast(F32)[:, :]), 16)

    wg_free = [NONE]
    wu_free = [NONE]
    wd_free = [NONE, NONE]
    cwrow_free = NONE
    cwb_free = NONE
    gsb_free = NONE
    hh_free = NONE
    hhalf_free = [NONE] * 8
    dn_marks = [NONE] * DC

    for e in range(17):
        routed = e < 16
        if routed:
            @b.on("gpsimd")
            def _(p, e=e):
                if e == 0:
                    b.waits(p, cwE_done)
                b.waits(p, cwrow_free)
                sst.inc(p.dma_start(out=cwrow[:, :], in_=cwE[e:e + 1, :]), 16)
            cwrow_ld = b.mark("st")

            @b.on("tensor")
            def _(t, cwrow_ld=cwrow_ld):
                b.waits(t, cwrow_ld, b.bank_free[0], b.bank_free[1])
                for tt in range(2):
                    n0 = tt * 512
                    ins = t.matmul(out=b.pool[tt][:, :], lhsT=C["ones_row"][:, :],
                                   rhs=cwrow[:, n0:n0 + 512], start=True,
                                   stop=True)
                b.sems["t"].inc(ins)
            cwmm = b.mark("t")
            cwrow_free = [cwmm]

            @b.on("vector")
            def _(v, cwmm=cwmm):
                b.waits(v, cwmm, cwb_free)
                for tt in range(2):
                    n0 = tt * 512
                    ins = v.tensor_copy(out=cwb[:, n0:n0 + 512],
                                        in_=b.pool[tt][:, :])
                b.sems["v"].inc(ins)
            cwb_done = b.mark("v")
            b.bank_free[0] = [cwb_done]
            b.bank_free[1] = [cwb_done]
        else:
            cwb_done = NONE

        for tt in range(2):
            n0 = tt * 512
            for ic in range(8):
                wslot = e * 8 + ic

                @b.on("sync")
                def _(q, e=e, tt=tt, ic=ic, wslot=wslot):
                    b.waits(q, wg_free[0], wu_free[0])
                    sld.inc(q.dma_start(out=wgb[0][:, :],
                                        in_=io["wg_t"][wslot, :, :]), 16)
                    sld.inc(q.dma_start(out=wub[0][:, :],
                                        in_=io["wu_t"][wslot, :, :]), 16)
                w_ld = b.mark("ld")

                gbk = 2 + ic % 2
                ubk = 4 + ic % 2

                @b.on("tensor")
                def _(t, e=e, tt=tt, ic=ic, n0=n0, gbk=gbk, ubk=ubk, w_ld=w_ld):
                    b.waits(t, w_ld, b.bank_free[gbk], b.bank_free[ubk])
                    for k in range(DC):
                        t.matmul(out=b.pool[gbk][:, :],
                                 lhsT=wgb[0][:, k * 128:(k + 1) * 128],
                                 rhs=xn[k][:, n0:n0 + 512],
                                 start=(k == 0), stop=(k == DC - 1))
                    for k in range(DC):
                        ins = t.matmul(out=b.pool[ubk][:, :],
                                       lhsT=wub[0][:, k * 128:(k + 1) * 128],
                                       rhs=xn[k][:, n0:n0 + 512],
                                       start=(k == 0), stop=(k == DC - 1))
                    b.sems["t"].inc(ins)
                gu_mm = b.mark("t")
                wg_free[0] = [gu_mm]
                wu_free[0] = [gu_mm]

                @b.on("scalar")
                def _(s, gbk=gbk, gu_mm=gu_mm):
                    b.waits(s, gu_mm, gsb_free)
                    ins = s.activation(out=gsb[:, :], in_=b.pool[gbk][:, :],
                                       func=AF.Silu)
                    b.sems["s"].inc(ins)
                gs_done = b.mark("s")
                b.bank_free[gbk] = [gs_done]

                @b.on("vector")
                def _(v, ubk=ubk, gs_done=gs_done, routed=routed, ic=ic,
                      n0=n0, cwb_done=cwb_done):
                    b.waits(v, gs_done, hh_free, hhalf_free[ic])
                    if routed:
                        v.tensor_tensor(out=hh, in0=gsb, in1=b.pool[ubk][:, :],
                                        op=ALU.mult)
                        ins = v.tensor_tensor(out=h_half[ic], in0=hh,
                                              in1=cwb[:, n0:n0 + 512],
                                              op=ALU.mult)
                    else:
                        ins = v.tensor_tensor(out=h_half[ic], in0=gsb,
                                              in1=b.pool[ubk][:, :], op=ALU.mult)
                    b.sems["v"].inc(ins)
                hdone = b.mark("v")
                gsb_free = [hdone]
                hh_free = [hdone]
                b.bank_free[ubk] = [hdone]
                hhalf_free[ic] = NONE

            for hc in range(DC):
                wslot = e * 16 + hc

                @b.on("sync")
                def _(q, wslot=wslot, hc=hc):
                    b.waits(q, wd_free[hc % 2])
                    sld2.inc(q.dma_start(out=wdb[hc % 2][:, :],
                                         in_=io["wd_t"][wslot, :, :]), 16)
                wd_ld = b.mark("ld2")

                dbk = 6 + hc % 2
                hmark = b.mark("v")

                @b.on("tensor")
                def _(t, hc=hc, wd_ld=wd_ld, dbk=dbk, hmark=hmark):
                    b.waits(t, wd_ld, b.bank_free[dbk])
                    if hc == 0:
                        b.waits(t, hmark)
                    for k in range(8):
                        ins = t.matmul(out=b.pool[dbk][:, :],
                                       lhsT=wdb[hc % 2][:, k * 128:(k + 1) * 128],
                                       rhs=h_half[k][:, :],
                                       start=(k == 0), stop=(k == 7))
                    b.sems["t"].inc(ins)
                dn_mm = b.mark("t")
                wd_free[hc % 2] = [dn_mm]

                @b.on("vector")
                def _(v, e=e, tt=tt, hc=hc, n0=n0, dbk=dbk, dn_mm=dn_mm):
                    b.waits(v, dn_mm)
                    if e == 0:
                        ins = v.tensor_copy(out=dn_acc[hc][:, n0:n0 + 512],
                                            in_=b.pool[dbk][:, :])
                    else:
                        ins = v.tensor_tensor(out=dn_acc[hc][:, n0:n0 + 512],
                                              in0=dn_acc[hc][:, n0:n0 + 512],
                                              in1=b.pool[dbk][:, :], op=ALU.add)
                    b.sems["v"].inc(ins)
                b.bank_free[dbk] = [b.mark("v")]
            down_done = b.mark("t")
            for ic in range(8):
                hhalf_free[ic] = [down_done]
    dn_done = b.mark("v")

    @b.on("gpsimd")
    def _(p):
        b.waits(p, dn_done)
        for c in range(DC):
            sst.inc(p.dma_start(out=io["dbg_dn"][c * 128:(c + 1) * 128, :],
                                in_=dn_acc[c][:, :]), 16)

    @b.on("vector")
    def _(v):
        b.waits(v, dn_done)
        for c in range(DC):
            ins = v.tensor_tensor(out=xn[c][:, :], in0=xn[c][:, :],
                                  in1=dn_acc[c][:, :], op=ALU.add)
        b.sems["v"].inc(ins)
    resid_done = b.mark("v")

    lnout_free = [NONE, NONE]

    def pre_wait_fn(c, j):
        return lnout_free[c % 2]

    def post_fn(c, j, vmark):
        @b.on("gpsimd")
        def _(p, c=c, j=j, vmark=vmark):
            b.waits(p, vmark)
            n0, n1 = j * 512, (j + 1) * 512
            sst.inc(p.dma_start(out=io["xfm2"][c * 128:(c + 1) * 128, n0:n1],
                                in_=lnout[c % 2].bitcast(F32)[:, n0:n1]), 16)
        lnout_free[c % 2] = [("st", b.sems["st"].n)]

    out_aps = {c: lnout[c % 2] for c in range(DC)}
    emit_ln(b, C, xn, ln_scr, [resid_done], [0, 1, 2, 3, 4, 5],
            out_aps, pre_wait_fn, post_fn)

    @b.on("gpsimd")
    def _(p):
        p.wait_ge(b.sems["st"].h, b.sems["st"].n)
    return io



def declare_l1attn(nc):
    io = {}
    io["xin"] = nc.dram_tensor("xin", [D, T], F32R, kind="ExternalInput").ap()
    io["wact_t"] = nc.dram_tensor("wact_t", [16, 128, 4096], F32R,
                                  kind="ExternalInput").ap()
    io["wrec_t"] = nc.dram_tensor("wrec_t", [16, 128, 4096], F32R,
                                  kind="ExternalInput").ap()
    io["wval_t"] = nc.dram_tensor("wval_t", [16, 128, 2048], F32R,
                                  kind="ExternalInput").ap()
    io["xfm3"] = nc.dram_tensor("xfm3", [D, T], F32, kind="ExternalOutput").ap()
    io["cc_sh_in"] = nc.dram_tensor("cc_sh_in", [2048, 1], F32, kind="Internal").ap()
    io["cc_sh_out"] = nc.dram_tensor("cc_sh_out", [16384, 1], F32, kind="Internal").ap()
    io["cc_mx_in"] = nc.dram_tensor("cc_mx_in", [1024, 1], F32, kind="Internal").ap()
    io["cc_mx_out"] = nc.dram_tensor("cc_mx_out", [8192, 1], F32, kind="Internal").ap()
    io["cc_q_in"] = nc.dram_tensor("cc_q_in", [2048, 1], F32, kind="Internal").ap()
    io["cc_q_out"] = nc.dram_tensor("cc_q_out", [16384, 1], F32, kind="Internal").ap()
    return io


def host_l1attn_weights(inp):
    W_act = np.asarray(inp["L1_attn_W_act"], np.float32)
    W_rec = np.asarray(inp["L1_attn_W_rec"], np.float32)
    W_val = np.asarray(inp["L1_attn_W_val"], np.float32)
    perm = np.concatenate([np.arange(0, 2048, 2), np.arange(1, 2048, 2)])
    wact = np.ascontiguousarray(W_act.T[:, perm])
    wrec = np.ascontiguousarray(W_rec.T)
    wval = np.ascontiguousarray(W_val.T[perm, :])
    return dict(wact_t=host_tile_w_ocmajor(wact),
                wrec_t=host_tile_w_ocmajor(wrec),
                wval_t=host_tile_w_ocmajor(wval))


def build_l1attn(b, C, io):
    ar = Arena(b, "ar3", 50400)
    x1e = [ar.f32r(c * 1032, 1032) for c in range(DC)]
    XV = 16512
    xv = [ar.f32(XV + c * 1024, 1024) for c in range(DC)]
    kv = [ar.f32r(XV + c * 1024, 1024) for c in range(DC)]
    resid = kv
    M = 32896
    m_t = [ar.f32(M + i * 1024, 1024) for i in range(8)]
    W = 41088
    wactb = ar.f32r(W, 4096)
    xvtmp = ar.f32(W + 4096, 512)
    pong = [ar.f32(W + i * 1024, 1024) for i in range(2)]
    tsc = [ar.f32(W + 2048 + i * 1024, 1024) for i in range(4)]
    mscr = ar.f32(W + 6144, 1024)
    wvalb = ar.f32r(W, 2048)
    wrecb = ar.f32r(W + 2048, 4096)
    rr = ar.f32(W + 6144, 512)
    r2 = ar.f32(W + 6656, 512)
    r4 = ar.f32(W + 7168, 512)
    ln_scr = dict(sq=[ar.f32r(W, 512), ar.f32r(W + 512, 512)],
                  mean=ar.f32r(W + 1024, 1024, p=1),
                  var=ar.f32r(W + 2048, 1024, p=1),
                  rstd=ar.f32r(W + 3072, 1024, p=1))
    lnout = [ar.f32r(W + 4096, 1024), ar.f32r(W + 5120, 1024)]
    SM = 49280
    shtab = ar.f32(SM, 8)
    mtab = ar.f32(SM + 8, 8)
    rtab = ar.f32(SM + 16, 8)
    itab = ar.f32(SM + 24, 8)
    maxoff = ar.f32(SM + 32, 8)
    pcol = ar.f32(SM + 40, 8)
    shcol = ar.f32(SM + 48, 8)
    assert SM + 56 <= 50400

    sld = b.sems["ld"]
    sld2 = b.sems["ld2"]
    sst = b.sems["st"]

    @b.on("sync")
    def _(q):
        b.waits(q, *C["ready"])
        for c in range(DC):
            sld.inc(q.dma_start(out=x1e[c][:, 1:1025],
                                in_=io["xin"][c * 128:(c + 1) * 128, :]), 16)
    x_ld = b.mark("ld")

    @b.on("gpsimd")
    def _(p):
        b.waits(p, x_ld)
        for c in range(DC):
            sst.inc(p.dma_start(out=io["cc_sh_in"][c * 128:(c + 1) * 128, :],
                                in_=x1e[c][:, 1024:1025].bitcast(F32)), 16)
        p.wait_ge(b.sems["st"].h, b.sems["st"].n)
        b.sems["cc"].inc(p.collective_compute(
            "AllGather", ALU.bypass, replica_groups=RG,
            ins=[io["cc_sh_in"].opt()], outs=[io["cc_sh_out"].opt()]))
    sh_cc = b.mark("cc")

    prev_comb = NONE
    for c in range(DC):
        @b.on("sync")
        def _(q, c=c, prev_comb=prev_comb):
            if c == 0:
                b.waits(q, sh_cc)
            b.waits(q, prev_comb)
            for r in range(8):
                sld2.inc(q.dma_start(
                    out=shtab[:, r:r + 1],
                    in_=io["cc_sh_out"][r * 2048 + c * 128:r * 2048 + (c + 1) * 128, :]),
                    16)
        sh_ld = b.mark("ld2")

        @b.on("vector")
        def _(v, c=c, sh_ld=sh_ld):
            b.waits(v, sh_ld)
            b.chain(v, "v", v.tensor_tensor(out=shcol, in0=shtab,
                                            in1=C["prev"][:, :], op=ALU.mult))
            with b.nc.allow_low_precision(reason="f32r shift column"):
                b.chain(v, "v", v.reduce_sum(out=x1e[c][:, 0:1], in_=shcol,
                                             axis=AX.X))
        prev_comb = b.mark("v")
    shift_done = b.mark("v")

    wact_free = NONE
    xvtmp_free = NONE

    def ct_rhs(k, n0, n1):
        if k < DC:
            return x1e[k][:, n0:n1]
        return x1e[k - DC][:, n0 + 1:n1 + 1]

    for oc in range(DC):
        @b.on("sync")
        def _(q, oc=oc):
            if oc == 0:
                b.waits(q, shift_done)
            b.waits(q, wact_free)
            sld.inc(q.dma_start(out=wactb[:, :], in_=io["wact_t"][oc, :, :]), 16)
        wld = b.mark("ld")

        for tt in range(2):
            n0 = tt * 512
            bk = tt

            @b.on("tensor")
            def _(t, oc=oc, tt=tt, n0=n0, bk=bk, wld=wld):
                if tt == 0:
                    b.waits(t, wld)
                    if oc == 0:
                        b.waits(t, shift_done)
                b.waits(t, b.bank_free[bk])
                for k in range(32):
                    ins = t.matmul(out=b.pool[bk][:, :],
                                   lhsT=wactb[:, k * 128:(k + 1) * 128],
                                   rhs=ct_rhs(k, n0, n0 + 512),
                                   start=(k == 0), stop=(k == 31))
                b.sems["t"].inc(ins)
            mm = b.mark("t")

            @b.on("vector")
            def _(v, bk=bk, mm=mm):
                b.waits(v, mm, xvtmp_free)
                ins = v.tensor_scalar_max(out=xvtmp, in0=b.pool[bk][:, :],
                                          scalar1=0.0)
                b.sems["v"].inc(ins)
            relu_done = b.mark("v")
            b.bank_free[bk] = [relu_done]

            @b.on("scalar")
            def _(s, oc=oc, n0=n0, relu_done=relu_done):
                b.waits(s, relu_done)
                ins = s.activation(out=xv[oc][:, n0:n0 + 512], in_=xvtmp,
                                   func=AF.Sigmoid)
                b.sems["s"].inc(ins)
            xvtmp_free = [b.mark("s")]
        wact_free = [b.mark("t")]
    xv_done = b.mark("s")

    mag_marks = []
    for pc in range(8):
        @b.on("scalar")
        def _(s, pc=pc):
            if pc == 0:
                b.waits(s, xv_done)
            b.waits(s, b.mark("v"))
            s.activation(out=mscr, in_=xv[pc][:, :], func=AF.Square)
            ins = s.activation(out=m_t[pc][:, :], in_=xv[8 + pc][:, :],
                               func=AF.Square)
            b.sems["s"].inc(ins)
        sq_done = b.mark("s")

        @b.on("vector")
        def _(v, pc=pc, sq_done=sq_done):
            b.waits(v, sq_done)
            ins = v.tensor_tensor(out=m_t[pc][:, :], in0=m_t[pc][:, :],
                                  in1=mscr, op=ALU.add)
            b.sems["v"].inc(ins)
        add_done = b.mark("v")

        @b.on("scalar")
        def _(s, pc=pc, add_done=add_done):
            b.waits(s, add_done)
            ins = s.activation(out=m_t[pc][:, :], in_=m_t[pc][:, :], func=AF.Sqrt)
            b.sems["s"].inc(ins)
        mag_done = b.mark("s")

        @b.on("vector")
        def _(v, pc=pc, mag_done=mag_done):
            b.waits(v, mag_done)
            cur, oth = m_t[pc], mscr
            for k in [1, 2, 4, 8, 16, 32, 64, 128, 256, 512]:
                b.chain(v, "v", v.tensor_copy(out=oth[:, 0:k], in_=cur[:, 0:k]))
                b.chain(v, "v", v.tensor_max(out=oth[:, k:1024], in0=cur[:, k:1024],
                                             in1=cur[:, 0:1024 - k]))
                cur, oth = oth, cur
            assert cur is m_t[pc]
        cm_done = b.mark("v")

        @b.on("gpsimd")
        def _(p, pc=pc, cm_done=cm_done):
            b.waits(p, cm_done)
            sst.inc(p.dma_start(out=io["cc_mx_in"][pc * 128:(pc + 1) * 128, :],
                                in_=m_t[pc][:, 1023:1024]), 16)
    mx_st = b.mark("st")

    @b.on("gpsimd")
    def _(p):
        p.wait_ge(b.sems["st"].h, b.sems["st"].n)
        b.sems["cc"].inc(p.collective_compute(
            "AllGather", ALU.bypass, replica_groups=RG,
            ins=[io["cc_mx_in"].opt()], outs=[io["cc_mx_out"].opt()]))
    mx_cc = b.mark("cc")

    @b.on("vector")
    def _(v):
        b.waits(v, mx_cc)
        ins = v.tensor_scalar_mul(out=maxoff, in0=C["ltinv"][:, :], scalar1=-1e30)
        b.sems["v"].inc(ins)
    moff_done = b.mark("v")

    q_marks = []
    for pc in range(8):
        @b.on("sync")
        def _(q, pc=pc):
            b.waits(q, mx_cc)
            if pc > 0:
                b.waits(q, q_marks[-1])
            for r in range(8):
                sld2.inc(q.dma_start(
                    out=mtab[:, r:r + 1],
                    in_=io["cc_mx_out"][r * 1024 + pc * 128:r * 1024 + (pc + 1) * 128, :]),
                    16)
        mt_ld = b.mark("ld2")

        @b.on("vector")
        def _(v, pc=pc, mt_ld=mt_ld):
            b.waits(v, mt_ld, moff_done)
            b.chain(v, "v", v.tensor_tensor(out=shcol, in0=mtab, in1=maxoff,
                                            op=ALU.add))
            b.chain(v, "v", v.reduce_max(out=pcol[:, 0:1], in_=shcol, axis=AX.X))
            b.chain(v, "v", v.tensor_scalar_max(out=m_t[pc][:, :],
                                                in0=m_t[pc][:, :],
                                                scalar1=pcol[:, 0:1]))
            v.reciprocal(out=m_t[pc][:, :], in_=m_t[pc][:, :])
            v.tensor_tensor(out=xv[pc][:, :], in0=xv[pc][:, :], in1=m_t[pc][:, :],
                            op=ALU.mult)
            ins = v.tensor_tensor(out=xv[8 + pc][:, :], in0=xv[8 + pc][:, :],
                                  in1=m_t[pc][:, :], op=ALU.mult)
            b.sems["v"].inc(ins)
        norm_done = b.mark("v")

        @b.on("vector")
        def _(v, pc=pc):
            ra, ia = xv[pc], xv[8 + pc]
            rb, ib = pong[0], pong[1]
            for k in [1, 2, 4, 8, 16, 32, 64, 128, 256, 512]:
                b.chain(v, "v", v.tensor_copy(out=rb[:, 0:k], in_=ra[:, 0:k]))
                b.chain(v, "v", v.tensor_copy(out=ib[:, 0:k], in_=ia[:, 0:k]))
                v.tensor_tensor(out=tsc[0][:, 0:1024 - k], in0=ra[:, k:1024],
                                in1=ra[:, 0:1024 - k], op=ALU.mult)
                v.tensor_tensor(out=tsc[1][:, 0:1024 - k], in0=ia[:, k:1024],
                                in1=ia[:, 0:1024 - k], op=ALU.mult)
                v.tensor_tensor(out=tsc[2][:, 0:1024 - k], in0=ra[:, k:1024],
                                in1=ia[:, 0:1024 - k], op=ALU.mult)
                v.tensor_tensor(out=tsc[3][:, 0:1024 - k], in0=ia[:, k:1024],
                                in1=ra[:, 0:1024 - k], op=ALU.mult)
                v.tensor_tensor(out=rb[:, k:1024], in0=tsc[0][:, 0:1024 - k],
                                in1=tsc[1][:, 0:1024 - k], op=ALU.subtract)
                b.chain(v, "v", v.tensor_tensor(out=ib[:, k:1024],
                                                in0=tsc[2][:, 0:1024 - k],
                                                in1=tsc[3][:, 0:1024 - k],
                                                op=ALU.add))
                ra, ia = rb, ib
                rb, ib = (xv[pc], xv[8 + pc]) if ra is pong[0] else (pong[0], pong[1])
            assert ra is xv[pc]
        cp_done = b.mark("v")

        @b.on("gpsimd")
        def _(p, pc=pc, cp_done=cp_done):
            b.waits(p, cp_done)
            sst.inc(p.dma_start(out=io["cc_q_in"][pc * 128:(pc + 1) * 128, :],
                                in_=xv[pc][:, 1023:1024]), 16)
            sst.inc(p.dma_start(out=io["cc_q_in"][1024 + pc * 128:1024 + (pc + 1) * 128, :],
                                in_=xv[8 + pc][:, 1023:1024]), 16)
        q_marks.append(("st", b.sems["st"].n))
    q_st = b.mark("st")

    @b.on("gpsimd")
    def _(p):
        p.wait_ge(b.sems["st"].h, b.sems["st"].n)
        b.sems["cc"].inc(p.collective_compute(
            "AllGather", ALU.bypass, replica_groups=RG,
            ins=[io["cc_q_in"].opt()], outs=[io["cc_q_out"].opt()]))
    q_cc = b.mark("cc")

    prev_q = NONE
    kvdone = []
    for pc in range(8):
        @b.on("sync")
        def _(q, pc=pc, prev_q=prev_q):
            if pc == 0:
                b.waits(q, q_cc)
            b.waits(q, prev_q)
            for r in range(8):
                sld2.inc(q.dma_start(
                    out=rtab[:, r:r + 1],
                    in_=io["cc_q_out"][r * 2048 + pc * 128:r * 2048 + (pc + 1) * 128, :]),
                    16)
                sld2.inc(q.dma_start(
                    out=itab[:, r:r + 1],
                    in_=io["cc_q_out"][r * 2048 + 1024 + pc * 128:
                                       r * 2048 + 1024 + (pc + 1) * 128, :]), 16)
        qt_ld = b.mark("ld2")

        @b.on("vector")
        def _(v, pc=pc, qt_ld=qt_ld):
            b.waits(v, qt_ld)
            Pr, Pi = pcol[:, 1:2], pcol[:, 2:3]
            qr, qi = pcol[:, 3:4], pcol[:, 4:5]
            t0c, t1c = pcol[:, 5:6], pcol[:, 6:7]
            b.chain(v, "v", v.memset(Pr, 1.0))
            b.chain(v, "v", v.memset(Pi, 0.0))
            for r in range(8):
                b.chain(v, "v", v.tensor_scalar(out=qr, in0=rtab[:, r:r + 1],
                                                scalar1=C["lt"][:, r:r + 1],
                                                scalar2=C["ltinv"][:, r:r + 1],
                                                op0=ALU.mult, op1=ALU.add))
                b.chain(v, "v", v.tensor_tensor(out=qi, in0=itab[:, r:r + 1],
                                                in1=C["lt"][:, r:r + 1],
                                                op=ALU.mult))
                b.chain(v, "v", v.tensor_tensor(out=t0c, in0=Pr, in1=qr,
                                                op=ALU.mult))
                b.chain(v, "v", v.tensor_tensor(out=t1c, in0=Pi, in1=qi,
                                                op=ALU.mult))
                b.chain(v, "v", v.tensor_tensor(out=t1c, in0=t0c, in1=t1c,
                                                op=ALU.subtract))
                b.chain(v, "v", v.tensor_tensor(out=t0c, in0=Pr, in1=qi,
                                                op=ALU.mult))
                b.chain(v, "v", v.tensor_tensor(out=Pi, in0=Pi, in1=qr,
                                                op=ALU.mult))
                b.chain(v, "v", v.tensor_tensor(out=Pi, in0=Pi, in1=t0c,
                                                op=ALU.add))
                b.chain(v, "v", v.tensor_copy(out=Pr, in_=t1c))
            v.tensor_scalar_mul(out=pong[0], in0=xv[pc][:, :], scalar1=Pr)
            v.tensor_scalar_mul(out=pong[1], in0=xv[8 + pc][:, :], scalar1=Pi)
            v.tensor_scalar_mul(out=mscr, in0=xv[8 + pc][:, :], scalar1=Pr)
            b.chain(v, "v", v.tensor_scalar_mul(out=tsc[0], in0=xv[pc][:, :],
                                                scalar1=Pi))
            v.tensor_tensor(out=kv[pc][:, :], in0=pong[0], in1=pong[1],
                            op=ALU.subtract)
            ins = v.tensor_tensor(out=kv[8 + pc][:, :], in0=mscr, in1=tsc[0],
                                  op=ALU.add)
            b.sems["v"].inc(ins)
        prev_q = b.mark("v")
        kvdone.append(prev_q)
    kv_done = b.mark("v")

    resid_h = [ar.f32r(M + i * 512, 512) for i in range(DC)]
    wv_free = NONE
    rr_free = NONE
    ln_done_prev = NONE
    for tt in range(2):
        n0 = tt * 512
        oc_marks = []
        for oc in range(DC):
            @b.on("sync")
            def _(q, tt=tt, oc=oc, ln_done_prev=ln_done_prev):
                if oc == 0:
                    if tt == 0:
                        b.waits(q, kv_done)
                    else:
                        b.waits(q, ln_done_prev)
                b.waits(q, wv_free)
                sld.inc(q.dma_start(out=wvalb[:, :], in_=io["wval_t"][oc, :, :]), 16)
                sld.inc(q.dma_start(out=wrecb[:, :], in_=io["wrec_t"][oc, :, :]), 16)
            wld = b.mark("ld")

            vbk = 2 + oc % 2
            rbk = 4 + oc % 2

            @b.on("tensor")
            def _(t, tt=tt, oc=oc, n0=n0, vbk=vbk, rbk=rbk, wld=wld,
                  ln_done_prev=ln_done_prev):
                b.waits(t, wld)
                if oc == 0 and tt == 0:
                    b.waits(t, kv_done)
                b.waits(t, b.bank_free[vbk], b.bank_free[rbk])
                for k in range(DC):
                    t.matmul(out=b.pool[vbk][:, :],
                             lhsT=wvalb[:, k * 128:(k + 1) * 128],
                             rhs=kv[k][:, n0:n0 + 512],
                             start=(k == 0), stop=(k == DC - 1))
                for k in range(32):
                    ins = t.matmul(out=b.pool[rbk][:, :],
                                   lhsT=wrecb[:, k * 128:(k + 1) * 128],
                                   rhs=ct_rhs(k, n0, n0 + 512),
                                   start=(k == 0), stop=(k == 31))
                b.sems["t"].inc(ins)
            mm = b.mark("t")
            wv_free = [mm]

            @b.on("vector")
            def _(v, rbk=rbk, mm=mm):
                b.waits(v, mm, rr_free)
                ins = v.tensor_scalar_max(out=rr, in0=b.pool[rbk][:, :],
                                          scalar1=0.0)
                b.sems["v"].inc(ins)
            relu_done = b.mark("v")
            b.bank_free[rbk] = [relu_done]

            @b.on("scalar")
            def _(s, relu_done=relu_done):
                b.waits(s, relu_done)
                ins = s.activation(out=rr, in_=rr, func=AF.Sigmoid)
                b.sems["s"].inc(ins)
            sig_done = b.mark("s")

            @b.on("vector")
            def _(v, vbk=vbk, sig_done=sig_done):
                b.waits(v, sig_done)
                v.tensor_tensor(out=r2, in0=rr, in1=rr, op=ALU.mult)
                b.chain(v, "v", v.tensor_tensor(out=r4, in0=r2, in1=r2,
                                                op=ALU.mult))
                ins = v.tensor_tensor(out=r2, in0=b.pool[vbk][:, :], in1=r4,
                                      op=ALU.mult)
                b.sems["v"].inc(ins)
            o_done = b.mark("v")
            rr_free = [o_done]
            b.bank_free[vbk] = [o_done]

            @b.on("gpsimd")
            def _(p, oc=oc, n0=n0, o_done=o_done):
                b.waits(p, o_done)
                ins = p.tensor_tensor(out=resid_h[oc][:, :], in0=r2,
                                      in1=x1e[oc][:, 1 + n0:513 + n0],
                                      op=ALU.add)
                b.sems["p"].inc(ins)
            rr_free = [b.mark("p")]
        resid_done = b.mark("p")

        lnout_free = [NONE, NONE]

        def pre_wait_fn(c, j, lnout_free=lnout_free):
            return lnout_free[c % 2]

        def post_fn(c, j, vmark, tt=tt, lnout_free=lnout_free):
            @b.on("gpsimd")
            def _(p, c=c, vmark=vmark, tt=tt):
                b.waits(p, vmark)
                n0 = tt * 512
                sst.inc(p.dma_start(
                    out=io["xfm3"][c * 128:(c + 1) * 128, n0:n0 + 512],
                    in_=lnout[c % 2].bitcast(F32)[:, 0:512]), 16)
            lnout_free[c % 2] = [("st", b.sems["st"].n)]

        out_aps = {c: lnout[c % 2][:, 0:512] for c in range(DC)}
        ln_done_prev = emit_ln(b, C, resid_h, ln_scr, [resid_done],
                               [0, 1, 2, 3, 4, 5], out_aps, pre_wait_fn,
                               post_fn, width=512)

    @b.on("gpsimd")
    def _(p):
        p.wait_ge(b.sems["st"].h, b.sems["st"].n)
    return io



def declare_l1ffn(nc):
    io = {}
    io["xin"] = nc.dram_tensor("xin", [D, T], F32R, kind="ExternalInput").ap()
    io["gq_t"] = nc.dram_tensor("gq_t", [64, 128, 2048], I32, kind="ExternalInput").ap()
    io["uq_t"] = nc.dram_tensor("uq_t", [64, 128, 2048], I32, kind="ExternalInput").ap()
    io["dq_t"] = nc.dram_tensor("dq_t", [128, 128, 1024], I32, kind="ExternalInput").ap()
    io["gusc"] = nc.dram_tensor("gusc", [8192, 2], F32, kind="ExternalInput").ap()
    io["dsc"] = nc.dram_tensor("dsc", [2048, 1], F32, kind="ExternalInput").ap()
    io["xfm4"] = nc.dram_tensor("xfm4", [D, T], F32, kind="ExternalOutput").ap()
    return io


def host_l1ffn_weights(inp):
    gq = np.asarray(inp["L1_ffn_gate_w"], np.int32)
    uq = np.asarray(inp["L1_ffn_up_w"], np.int32)
    dq = np.asarray(inp["L1_ffn_down_w"], np.int32)

    def tile_i32(w):
        K, M = w.shape
        a = w.reshape(K // 128, 128, M // 128, 128).transpose(2, 1, 0, 3)
        return np.ascontiguousarray(a.reshape(M // 128, 128, K))

    gq_t = tile_i32(np.ascontiguousarray(gq.T))
    uq_t = tile_i32(np.ascontiguousarray(uq.T))
    dqT = np.ascontiguousarray(dq.T)
    a = dqT.reshape(8, 8, 128, 16, 128)
    a = a.transpose(0, 3, 2, 1, 4)
    dq_t = np.ascontiguousarray(a.reshape(128, 128, 1024))
    gusc = np.ascontiguousarray(np.stack(
        [np.asarray(inp["L1_ffn_gate_scale"], np.float32),
         np.asarray(inp["L1_ffn_up_scale"], np.float32)], axis=1))
    dsc = np.ascontiguousarray(
        np.asarray(inp["L1_ffn_down_scale"], np.float32).reshape(2048, 1))
    return dict(gq_t=gq_t, uq_t=uq_t, dq_t=dq_t, gusc=gusc, dsc=dsc)


def build_l1ffn(b, C, io):
    ar = Arena(b, "ar4", 50400)
    xn = [ar.f32r(c * 1024, 1024) for c in range(DC)]
    dn_acc = [ar.f32(16384 + c * 1024, 1024) for c in range(DC)]
    H8 = 32768
    h8 = [ar.f32r(H8 + i * 1024, 1024) for i in range(8)]
    ln_scr = dict(sq=[ar.f32r(H8, 512), ar.f32r(H8 + 512, 512)],
                  mean=ar.f32r(H8 + 1024, 1024, p=1),
                  var=ar.f32r(H8 + 2048, 1024, p=1),
                  rstd=ar.f32r(H8 + 3072, 1024, p=1))
    lnout = [ar.f32r(H8 + 4096, 1024), ar.f32r(H8 + 5120, 1024)]
    WQ = 40960
    gqs = ar.t(WQ, [128, 2048], I32)
    uqs = ar.t(WQ + 2048, [128, 2048], I32)
    wg_fr = ar.f32r(WQ + 4096, 2048)
    wu_fr = ar.f32r(WQ + 6144, 2048)
    dqs = ar.t(WQ + 4096, [128, 1024], I32)
    wd_fr = ar.f32r(WQ + 5120, 1024)
    scl = ar.f32(49152, 144)
    gsb = ar.f32(49296, 512)
    usb = ar.f32(49808, 512)

    sld = b.sems["ld"]
    sld2 = b.sems["ld2"]
    sst = b.sems["st"]

    @b.on("sync")
    def _(q):
        b.waits(q, *C["ready"])
        for c in range(DC):
            sld.inc(q.dma_start(out=xn[c][:, :],
                                in_=io["xin"][c * 128:(c + 1) * 128, :]), 16)
        for ic in range(64):
            sld.inc(q.dma_start(out=scl[:, ic * 2:(ic + 1) * 2],
                                in_=io["gusc"][ic * 128:(ic + 1) * 128, :]), 16)
        for hc in range(DC):
            sld.inc(q.dma_start(out=scl[:, 128 + hc:129 + hc],
                                in_=io["dsc"][hc * 128:(hc + 1) * 128, :]), 16)
    xn_ld = b.mark("ld")

    gq_free = NONE
    wgfr_free = NONE
    wufr_free = NONE
    gsb_free = NONE
    usb_free = NONE
    h8_free = [NONE] * 8
    dq_free = NONE
    wdfr_free = NONE

    for icg in range(8):
        for i8 in range(8):
            ic = icg * 8 + i8

            @b.on("sync")
            def _(q, ic=ic):
                b.waits(q, gq_free)
                sld.inc(q.dma_start(out=gqs[:, :], in_=io["gq_t"][ic, :, :]), 16)
                sld.inc(q.dma_start(out=uqs[:, :], in_=io["uq_t"][ic, :, :]), 16)
            q_ld = b.mark("ld")

            @b.on("vector")
            def _(v, q_ld=q_ld):
                b.waits(v, q_ld, wgfr_free)
                ins = v.tensor_scalar_add(out=wg_fr, in0=gqs[:, :],
                                          scalar1=-128.0)
                b.sems["v"].inc(ins)
            wg_done = b.mark("v")

            @b.on("vector")
            def _(v, q_ld=q_ld):
                b.waits(v, q_ld, wufr_free)
                ins = v.tensor_scalar_add(out=wu_fr, in0=uqs[:, :],
                                          scalar1=-128.0)
                b.sems["v"].inc(ins)
            wu_done = b.mark("v")
            gq_free = [wg_done, wu_done]

            for tt in range(2):
                n0 = tt * 512
                gbk, ubk = tt, 2 + tt

                @b.on("tensor")
                def _(t, ic=ic, tt=tt, n0=n0, gbk=gbk, ubk=ubk,
                      wg_done=wg_done, wu_done=wu_done):
                    if tt == 0:
                        b.waits(t, wg_done, wu_done)
                        if ic == 0:
                            b.waits(t, xn_ld)
                    b.waits(t, b.bank_free[gbk], b.bank_free[ubk])
                    for k in range(DC):
                        t.matmul(out=b.pool[gbk][:, :],
                                 lhsT=wg_fr[:, k * 128:(k + 1) * 128],
                                 rhs=xn[k][:, n0:n0 + 512],
                                 start=(k == 0), stop=(k == DC - 1))
                    for k in range(DC):
                        ins = t.matmul(out=b.pool[ubk][:, :],
                                       lhsT=wu_fr[:, k * 128:(k + 1) * 128],
                                       rhs=xn[k][:, n0:n0 + 512],
                                       start=(k == 0), stop=(k == DC - 1))
                    b.sems["t"].inc(ins)
                mm = b.mark("t")
                wgfr_free = [mm]
                wufr_free = [mm]

                @b.on("scalar")
                def _(s, ic=ic, gbk=gbk, mm=mm):
                    b.waits(s, mm, gsb_free)
                    ins = s.activation(out=gsb, in_=b.pool[gbk][:, :],
                                       func=AF.Silu,
                                       scale=scl[:, ic * 2:ic * 2 + 1])
                    b.sems["s"].inc(ins)
                gs_done = b.mark("s")
                b.bank_free[gbk] = [gs_done]

                @b.on("vector")
                def _(v, ic=ic, ubk=ubk, mm=mm):
                    b.waits(v, mm, usb_free)
                    ins = v.tensor_scalar_mul(out=usb, in0=b.pool[ubk][:, :],
                                              scalar1=scl[:, ic * 2 + 1:ic * 2 + 2])
                    b.sems["v"].inc(ins)
                us_done = b.mark("v")
                b.bank_free[ubk] = [us_done]

                @b.on("gpsimd")
                def _(p, i8=i8, n0=n0, gs_done=gs_done, us_done=us_done):
                    b.waits(p, gs_done, us_done, h8_free[i8])
                    ins = p.tensor_tensor(out=h8[i8][:, n0:n0 + 512], in0=gsb,
                                          in1=usb, op=ALU.mult)
                    b.sems["p"].inc(ins)
                hm = b.mark("p")
                gsb_free = [hm]
                usb_free = [hm]
        h_ready = b.mark("p")

        for hc in range(DC):
            @b.on("sync")
            def _(q, icg=icg, hc=hc):
                b.waits(q, dq_free)
                if hc == 0:
                    b.waits(q, wgfr_free)
                sld2.inc(q.dma_start(out=dqs[:, :],
                                     in_=io["dq_t"][icg * 16 + hc, :, :]), 16)
            dq_ld = b.mark("ld2")

            @b.on("vector")
            def _(v, dq_ld=dq_ld):
                b.waits(v, dq_ld, wdfr_free)
                ins = v.tensor_scalar_add(out=wd_fr, in0=dqs[:, :],
                                          scalar1=-128.0)
                b.sems["v"].inc(ins)
            wd_done = b.mark("v")
            dq_free = [wd_done]

            for tt in range(2):
                n0 = tt * 512
                dbk = 4 + tt

                @b.on("tensor")
                def _(t, tt=tt, dbk=dbk, wd_done=wd_done, h_ready=h_ready):
                    if tt == 0:
                        b.waits(t, wd_done, h_ready)
                    b.waits(t, b.bank_free[dbk])
                    for k in range(8):
                        ins = t.matmul(out=b.pool[dbk][:, :],
                                       lhsT=wd_fr[:, k * 128:(k + 1) * 128],
                                       rhs=h8[k][:, n0:n0 + 512],
                                       start=(k == 0), stop=(k == 7))
                    b.sems["t"].inc(ins)
                dmm = b.mark("t")

                @b.on("vector")
                def _(v, icg=icg, hc=hc, tt=tt, n0=n0, dbk=dbk, dmm=dmm):
                    b.waits(v, dmm)
                    if icg == 0:
                        ins = v.tensor_scalar_mul(
                            out=dn_acc[hc][:, n0:n0 + 512], in0=b.pool[dbk][:, :],
                            scalar1=scl[:, 128 + hc:129 + hc])
                    else:
                        ins = v.scalar_tensor_tensor(
                            out=dn_acc[hc][:, n0:n0 + 512], in0=b.pool[dbk][:, :],
                            scalar=scl[:, 128 + hc:129 + hc],
                            in1=dn_acc[hc][:, n0:n0 + 512],
                            op0=ALU.mult, op1=ALU.add)
                    b.sems["v"].inc(ins)
                b.bank_free[dbk] = [b.mark("v")]
            wdfr_free = [b.mark("t")]
        down_done = b.mark("t")
        for i8 in range(8):
            h8_free[i8] = [down_done]
    dn_done = b.mark("v")

    @b.on("vector")
    def _(v):
        b.waits(v, dn_done)
        for c in range(DC):
            ins = v.tensor_tensor(out=xn[c][:, :], in0=xn[c][:, :],
                                  in1=dn_acc[c][:, :], op=ALU.add)
        b.sems["v"].inc(ins)
    resid_done = b.mark("v")

    lnout_free = [NONE, NONE]

    def pre_wait_fn(c, j):
        return lnout_free[c % 2]

    def post_fn(c, j, vmark):
        @b.on("gpsimd")
        def _(p, c=c, j=j, vmark=vmark):
            b.waits(p, vmark)
            n0, n1 = j * 512, (j + 1) * 512
            sst.inc(p.dma_start(out=io["xfm4"][c * 128:(c + 1) * 128, n0:n1],
                                in_=lnout[c % 2].bitcast(F32)[:, n0:n1]), 16)
        lnout_free[c % 2] = [("st", b.sems["st"].n)]

    out_aps = {c: lnout[c % 2] for c in range(DC)}
    emit_ln(b, C, xn, ln_scr, [resid_done], [0, 1, 2, 3, 4, 5],
            out_aps, pre_wait_fn, post_fn)

    @b.on("gpsimd")
    def _(p):
        p.wait_ge(b.sems["st"].h, b.sems["st"].n)
    return io



_PROGS = {}


def _get_progs():
    if _PROGS:
        return _PROGS
    for name, declare, build in [
        ("l0attn", declare_l0attn, build_l0attn),
        ("l0ffn", declare_l0ffn, build_l0ffn),
        ("l1attn", declare_l1attn, build_l1attn),
        ("l1ffn", declare_l1ffn, build_l1ffn),
    ]:
        b = B(name)
        caps = declare_consts(b.nc)
        io = declare(b.nc)
        C = load_consts(b, caps)
        build(b, C, io)
        b.close()
        _PROGS[name] = b
    return _PROGS


def kernel(**inputs):
    progs = _get_progs()
    consts = [host_consts(c) for c in range(NCORES)]

    w1 = host_l0attn_weights(inputs)
    maps = []
    for c in range(NCORES):
        m = dict(consts[c])
        m.update(host_l0attn(inputs, c))
        m.update(w1)
        maps.append(m)
    r1 = run_bass_kernel_spmd(progs["l0attn"].nc, maps,
                              core_ids=list(range(NCORES)))

    w2 = host_l0ffn_weights(inputs)
    maps = []
    for c in range(NCORES):
        m = dict(consts[c])
        m.update(w2)
        m["xin"] = np.ascontiguousarray(r1.results[c]["xfm1"])
        maps.append(m)
    r2 = run_bass_kernel_spmd(progs["l0ffn"].nc, maps,
                              core_ids=list(range(NCORES)))

    w3 = host_l1attn_weights(inputs)
    maps = []
    for c in range(NCORES):
        m = dict(consts[c])
        m.update(w3)
        m["xin"] = np.ascontiguousarray(r2.results[c]["xfm2"])
        maps.append(m)
    r3 = run_bass_kernel_spmd(progs["l1attn"].nc, maps,
                              core_ids=list(range(NCORES)))

    w4 = host_l1ffn_weights(inputs)
    maps = []
    for c in range(NCORES):
        m = dict(consts[c])
        m.update(w4)
        m["xin"] = np.ascontiguousarray(r3.results[c]["xfm3"])
        maps.append(m)
    r4 = run_bass_kernel_spmd(progs["l1ffn"].nc, maps,
                              core_ids=list(range(NCORES)))

    out = np.concatenate([r4.results[c]["xfm4"].T for c in range(NCORES)],
                         axis=0)
    return np.ascontiguousarray(out.reshape(1, 8192, D).astype(np.float32))


# version 03
# speedup vs baseline: 1.4032x; 1.4032x over previous
import os
os.environ.setdefault("JAX_COMPILATION_CACHE_DIR", "/root/.jax_cache")
import contextlib
import numpy as np
import concourse.bass as bass
import concourse.mybir as mybir
from concourse.bass_utils import run_bass_kernel_spmd

F32 = mybir.dt.float32
F32R = mybir.dt.float32r
BF16 = mybir.dt.bfloat16
I32 = mybir.dt.int32
AF = mybir.ActivationFunctionType
ALU = mybir.AluOpType
AX = mybir.AxisListType

NCORES = 8
T = 1024
D = 2048
DC = D // 128
RG = [[0, 1, 2, 3, 4, 5, 6, 7]]
EPS = 1e-5
NONE = []


class Sem:
    def __init__(self, ctx, nc, name):
        self.h = ctx.enter_context(nc.semaphore(name))
        self.n = 0

    def inc(self, instr, k=1):
        instr.then_inc(self.h, k)
        self.n += k
        return self.n


class B:
    def __init__(self, name):
        self.nc = bass.Bass("TRN2", target_bir_lowering=False)
        self.ctx = contextlib.ExitStack()
        self.name = name
        self.block = self.ctx.enter_context(self.nc.Block())
        self.sems = {s: Sem(self.ctx, self.nc, "sem_" + s)
                     for s in ["ld", "ld2", "st", "t", "v", "s", "p", "cc"]}
        self.pool = [self.ctx.enter_context(
            self.nc.psum_tensor(f"bank{i}", [128, 512], F32)) for i in range(8)]
        self.bank_free = [NONE] * 8

    def sbuf(self, name, shape, dtype):
        return self.ctx.enter_context(self.nc.sbuf_tensor(name, shape, dtype))

    def on(self, eng):
        return getattr(self.block, eng)

    def waits(self, eng, *marks):
        flat = []
        for m in marks:
            if isinstance(m, list):
                flat.extend(m)
            elif m is not None:
                flat.append(m)
        best = {}
        for sem_name, cnt in flat:
            best[sem_name] = max(best.get(sem_name, 0), cnt)
        for sem_name, cnt in best.items():
            if cnt > 0:
                eng.wait_ge(self.sems[sem_name].h, cnt)

    def mark(self, sem_name):
        return (sem_name, self.sems[sem_name].n)

    def chain(self, eng, sem_name, ins):
        n = self.sems[sem_name].inc(ins)
        eng.wait_ge(self.sems[sem_name].h, n)
        return ins

    def close(self):
        self.ctx.close()


def fr(ap):
    return ap.bitcast(F32R)


class Arena:
    def __init__(self, b, name, cols):
        self.b = b
        self.name = name
        self.cols = cols
        self.n = 0
        used = b.nc.SBUF_PARTITION_SIZE_BYTES - b.nc.sbuf_bytes_remaining
        self.base = ((used + 31) // 32) * 32
        assert self.base + cols * 4 <= b.nc.SBUF_PARTITION_SIZE_BYTES, (
            self.base, cols)

    def t(self, col, shape, dtype):
        assert col % 8 == 0, col
        esz = 4 if dtype in (F32, F32R, I32) else 2
        ncols = (shape[-1] * esz + 3) // 4
        assert col + ncols <= self.cols, (col, shape, self.cols)
        self.n += 1
        h = self.b.nc.alloc_sbuf_tensor_at(
            f"{self.name}_{self.n}", list(shape), dtype,
            offset=self.base + col * 4)
        return h[tuple(slice(0, d) for d in shape)]

    def f32(self, col, n, p=128):
        return self.t(col, [p, n], F32)

    def f32r(self, col, n, p=128):
        return self.t(col, [p, n], F32R)



def host_tile_w_ocmajor(w, mt=128):
    K, M = w.shape
    assert K % 128 == 0 and M % mt == 0
    a = w.reshape(K // 128, 128, M // mt, mt)
    a = a.transpose(2, 1, 0, 3)
    return np.ascontiguousarray(a.reshape(M // mt, 128, (K // 128) * mt))


def host_consts(core):
    ident = np.eye(128, dtype=np.float32)
    ones_col = np.ones((128, 1), np.float32)
    ones_row = np.ones((1, 128), np.float32)
    lt = np.zeros((128, 8), np.float32)
    lt[:, :core] = 1.0
    prev = np.zeros((128, 8), np.float32)
    if core > 0:
        prev[:, core - 1] = 1.0
    return dict(c_ident=ident, c_ones_col=ones_col, c_ones_row=ones_row,
                c_lt=lt, c_ltinv=np.ascontiguousarray(1.0 - lt), c_prev=prev)


def declare_consts(nc):
    return dict(
        c_ident=nc.dram_tensor("c_ident", [128, 128], F32, kind="ExternalInput").ap(),
        c_ones_col=nc.dram_tensor("c_ones_col", [128, 1], F32R, kind="ExternalInput").ap(),
        c_ones_row=nc.dram_tensor("c_ones_row", [1, 128], F32R, kind="ExternalInput").ap(),
        c_lt=nc.dram_tensor("c_lt", [128, 8], F32, kind="ExternalInput").ap(),
        c_ltinv=nc.dram_tensor("c_ltinv", [128, 8], F32, kind="ExternalInput").ap(),
        c_prev=nc.dram_tensor("c_prev", [128, 8], F32, kind="ExternalInput").ap(),
    )


def load_consts(b, aps):
    C = {}
    C["ident"] = b.sbuf("cs_ident", [128, 128], F32)
    C["ones_col"] = b.sbuf("cs_ones_col", [128, 1], F32R)
    C["ones_row"] = b.sbuf("cs_ones_row", [1, 128], F32R)
    C["lt"] = b.sbuf("cs_lt", [128, 8], F32)
    C["ltinv"] = b.sbuf("cs_ltinv", [128, 8], F32)
    C["prev"] = b.sbuf("cs_prev", [128, 8], F32)
    ccols = b.sbuf("cs_cols", [128, 4], F32)

    @b.on("sync")
    def _(q):
        s = b.sems["ld"]
        for key, dst in [("c_ident", C["ident"]), ("c_ones_col", C["ones_col"]),
                         ("c_ones_row", C["ones_row"]), ("c_lt", C["lt"]),
                         ("c_ltinv", C["ltinv"]), ("c_prev", C["prev"])]:
            s.inc(q.dma_start(out=dst[:], in_=aps[key][:, :]), 16)

    @b.on("vector")
    def _(v):
        v.memset(ccols[:, 0:1], 1.0)
        v.memset(ccols[:, 1:2], -1.0)
        v.memset(ccols[:, 2:3], EPS)
        b.sems["v"].inc(v.memset(ccols[:, 3:4], 0.0))

    C["one"] = ccols[:, 0:1]
    C["negone"] = ccols[:, 1:2]
    C["eps"] = ccols[:, 2:3]
    C["zero"] = ccols[:, 3:4]
    C["ready"] = [b.mark("ld"), b.mark("v")]
    return C



def emit_ln(b, C, resid_tiles, scr, dep_marks, banks, out_aps,
            pre_wait_fn=None, post_fn=None, width=T):
    npc = (width + 511) // 512
    pieces = [(j * 512, min(width, (j + 1) * 512)) for j in range(npc)]
    sx = [b.pool[banks[0]][0:1, 0:512], b.pool[banks[1]][0:1, 0:512]]
    sqp = [b.pool[banks[2]][0:1, 0:512], b.pool[banks[3]][0:1, 0:512]]
    bcm, bcr = b.pool[banks[4]], b.pool[banks[5]]
    sq = scr["sq"]
    mean = scr["mean"][:, 0:width]
    var = scr["var"][:, 0:width]
    rstd = scr["rstd"][:, 0:width]

    sq_used = [NONE, NONE]
    for c in range(DC):
        for j, (n0, n1) in enumerate(pieces):
            @b.on("scalar")
            def _(s, c=c, j=j, n0=n0, n1=n1):
                if c == 0 and j == 0:
                    b.waits(s, *dep_marks)
                b.waits(s, sq_used[j])
                b.sems["s"].inc(s.activation(out=sq[j][:, 0:n1 - n0],
                                             in_=resid_tiles[c][:, n0:n1],
                                             func=AF.Square))
            sq_ready = b.mark("s")

            @b.on("tensor")
            def _(t, c=c, j=j, n0=n0, n1=n1, sq_ready=sq_ready):
                if c == 0 and j == 0:
                    b.waits(t, *dep_marks)
                    for bk in banks:
                        b.waits(t, b.bank_free[bk])
                b.waits(t, sq_ready)
                t.matmul(out=sx[j][:, 0:n1 - n0], lhsT=C["ones_col"][:, :],
                         rhs=resid_tiles[c][:, n0:n1], start=(c == 0),
                         stop=(c == DC - 1))
                ins = t.matmul(out=sqp[j][:, 0:n1 - n0], lhsT=C["ones_col"][:, :],
                               rhs=sq[j][:, 0:n1 - n0], start=(c == 0),
                               stop=(c == DC - 1))
                b.sems["t"].inc(ins)
            sq_used[j] = b.mark("t")

    mm_done = b.mark("t")

    @b.on("vector")
    def _(v):
        b.waits(v, mm_done)
        for j, (n0, n1) in enumerate(pieces):
            v.tensor_scalar_mul(out=mean[:, n0:n1], in0=sx[j][:, 0:n1 - n0],
                                scalar1=1.0 / D)
            v.tensor_scalar_mul(out=var[:, n0:n1], in0=sqp[j][:, 0:n1 - n0],
                                scalar1=1.0 / D)
        b.chain(v, "v", v.tensor_tensor(out=rstd, in0=mean, in1=mean,
                                        op=ALU.mult))
        b.sems["v"].inc(v.tensor_tensor(out=var, in0=var, in1=rstd,
                                        op=ALU.subtract))
    stats1 = b.mark("v")

    @b.on("scalar")
    def _(s):
        b.waits(s, stats1)
        b.sems["s"].inc(s.activation(out=rstd, in_=var, func=AF.Sqrt,
                                     bias=C["eps"][0:1, :]))
    stats2 = b.mark("s")

    @b.on("vector")
    def _(v):
        b.waits(v, stats2)
        with b.nc.allow_low_precision(reason="f32r LN stats"):
            b.sems["v"].inc(v.reciprocal(out=rstd, in_=rstd))
    stats3 = b.mark("v")

    for bk in banks[0:4]:
        b.bank_free[bk] = [stats1]

    prev_norm = None
    for j, (n0, n1) in enumerate(pieces):
        @b.on("tensor")
        def _(t, j=j, n0=n0, n1=n1, prev_norm=prev_norm):
            b.waits(t, stats3 if j == 0 else prev_norm)
            t.matmul(out=bcm[:, 0:n1 - n0], lhsT=C["ones_row"][:, :],
                     rhs=mean[:, n0:n1], start=True, stop=True)
            ins = t.matmul(out=bcr[:, 0:n1 - n0], lhsT=C["ones_row"][:, :],
                           rhs=rstd[:, n0:n1], start=True, stop=True)
            b.sems["t"].inc(ins)
        bc_done = b.mark("t")

        for c in range(DC):
            pw = pre_wait_fn(c, j) if pre_wait_fn else NONE

            @b.on("vector")
            def _(v, c=c, n0=n0, n1=n1, bc_done=bc_done, pw=pw):
                if c == 0:
                    b.waits(v, bc_done)
                b.waits(v, pw)
                v.tensor_tensor(out=out_aps[c][:, n0:n1],
                                in0=resid_tiles[c][:, n0:n1],
                                in1=bcm[:, 0:n1 - n0], op=ALU.subtract)
                ins = v.tensor_tensor(out=out_aps[c][:, n0:n1],
                                      in0=out_aps[c][:, n0:n1],
                                      in1=bcr[:, 0:n1 - n0], op=ALU.mult)
                b.sems["v"].inc(ins)
            if post_fn is not None:
                post_fn(c, j, b.mark("v"))
        prev_norm = b.mark("v")

    done = prev_norm
    for bk in banks[4:6]:
        b.bank_free[bk] = [done]
    return done



def declare_l0attn(nc, pfx="", out_kind="ExternalOutput"):
    io = {}
    io["x_loc"] = nc.dram_tensor(pfx + "x_loc", [T + 2, D], F32, kind="ExternalInput").ap()
    io["pcc"] = nc.dram_tensor(pfx + "pcc", [4096, 22], F32, kind="ExternalInput").ap()
    io["w_in_t"] = nc.dram_tensor(pfx + "w_in_t", [64, 128, DC * 128], F32R,
                                  kind="ExternalInput").ap()
    io["w_x_t"] = nc.dram_tensor(pfx + "w_x_t", [128, 32 * 160], F32R,
                                 kind="ExternalInput").ap()
    io["w_dt_t"] = nc.dram_tensor(pfx + "w_dt_t", [128, 4096], F32R,
                                  kind="ExternalInput").ap()
    io["w_out_t"] = nc.dram_tensor(pfx + "w_out_t", [16, 128, 32 * 128], F32R,
                                   kind="ExternalInput").ap()
    io["selE"] = nc.dram_tensor(pfx + "selE", [32, 4096], F32R,
                                kind="ExternalInput").ap()
    io["xfm1"] = nc.dram_tensor(pfx + "xfm1", [D, T], F32, kind=out_kind).ap()
    for nm, dt_ in [("xc_h", F32R), ("z_h", F32R), ("y_h", BF16), ("cds_h", F32),
                    ("yfin_h", F32R)]:
        io[nm] = nc.dram_tensor(pfx + nm, [4096, T], dt_, kind="Internal").ap()
    io["ccFD_in"] = nc.dram_tensor(pfx + "ccFD_in", [128, 1024], F32, kind="Internal").ap()
    io["ccFD_out"] = nc.dram_tensor(pfx + "ccFD_out", [1024, 1024], F32, kind="Internal").ap()
    return io


def host_l0attn(inp, core):
    x = np.asarray(inp["L0_attn_x"], np.float32).reshape(8192, D)
    t0 = core * T
    lo, hi = t0 - 1, t0 + T + 1
    xs = x[max(lo, 0):min(hi, 8192)]
    x_loc = np.pad(xs, ((max(0, -lo), max(0, hi - 8192)), (0, 0)))
    assert x_loc.shape == (T + 2, D)
    pcc = np.concatenate([
        np.asarray(inp["L0_attn_conv_w"], np.float32).reshape(4096, 3),
        np.asarray(inp["L0_attn_conv_b"], np.float32).reshape(4096, 1),
        np.asarray(inp["L0_attn_b_dt"], np.float32).reshape(4096, 1),
        np.asarray(inp["L0_attn_D"], np.float32).reshape(4096, 1),
        np.asarray(inp["L0_attn_A_log"], np.float32).reshape(4096, 16),
    ], axis=1)
    return dict(x_loc=np.ascontiguousarray(x_loc), pcc=np.ascontiguousarray(pcc))


def host_l0attn_weights(inp):
    w_in = np.asarray(inp["L0_attn_W_in"], np.float32)
    w_x = np.asarray(inp["L0_attn_W_x"], np.float32)
    w_dt = np.asarray(inp["L0_attn_W_dt"], np.float32)
    w_out = np.asarray(inp["L0_attn_W_out"], np.float32)
    w_in_t = host_tile_w_ocmajor(w_in)
    w_x_t = np.ascontiguousarray(
        w_x.reshape(32, 128, 160).transpose(1, 0, 2).reshape(128, 32 * 160))
    w_out_t = host_tile_w_ocmajor(w_out)
    selE = np.zeros((32, 32 * 128), np.float32)
    for r in range(32):
        selE[r, r * 128:(r + 1) * 128] = 1.0
    return dict(w_in_t=w_in_t, w_x_t=w_x_t, w_dt_t=np.ascontiguousarray(w_dt),
                w_out_t=w_out_t, selE=selE)


def build_l0attn(b, C, io, start_dep=()):
    ar = Arena(b, "ar1", 47936)
    pcc_all = ar.f32(0, 704)
    A_all = ar.f32(704, 512)
    F_all = ar.f32(1216, 512)
    D_all = ar.f32(1728, 512)
    H_all = ar.f32(2240, 512)
    ones_t = ar.f32(2752, 1024)
    dtin_fr = ar.f32r(3776, 1024)
    bc_fr = ar.f32r(4800, 1024, p=32)
    LNS = 5824
    ln_scr = dict(sq=[ar.f32r(LNS, 512), ar.f32r(LNS + 512, 512)],
                  mean=ar.f32r(LNS + 1024, 1024, p=1),
                  var=ar.f32r(LNS + 2048, 1024, p=1),
                  rstd=ar.f32r(LNS + 3072, 1024, p=1))
    OA = LNS + 4096
    out_acc = [ar.f32r(OA + c * 1024, 1024) for c in range(DC)]
    RS = OA + 16384

    xfm = [ar.f32r(RS + c * 1032, 1026) for c in range(DC)]
    xrow = [ar.f32(OA + i * 2048, 2048) for i in range(2)]
    win_wb = [ar.f32r(OA + 4096 + i * 2048, 2048) for i in range(2)]
    xp_sb = ar.f32(OA + 8192, 1026)
    xc_st = ar.f32(OA + 9224, 1024)
    xc_frt = ar.f32r(OA + 10248, 1024)
    z_stg = ar.f32r(OA + 11272, 512)
    spB = ar.f32(OA + 9216, 1024)
    cds_j = ar.f32(OA + 10240, 1024)
    FD_r = ar.f32(RS + 0, 1024)
    Dm_s = ar.f32(RS + 1024, 512)
    selE_s = ar.f32r(42816, 4096, p=32)
    ybufs = [ar.f32r(RS + i * 1024, 1024) for i in range(8)]
    wout_wb = [ar.f32r(RS + 8192 + i * 4096, 4096) for i in range(2)]
    xrow2 = [ar.f32(RS + 10240 + i * 2048, 2048) for i in range(2)]
    lnout = [ar.f32r(RS + 14336, 1024), ar.f32r(RS + 15360, 1024)]

    sld = b.sems["ld"]
    sld2 = b.sems["ld2"]
    sst = b.sems["st"]

    wx_all = ar.f32r(42816, 5120)
    wdt_all = ar.f32r(OA + 11784, 4096)

    @b.on("sync")
    def _(q):
        b.waits(q, *C["ready"], *start_dep)
        sld.inc(q.dma_start(out=wx_all[:], in_=io["w_x_t"][:, :]), 16)
        sld.inc(q.dma_start(out=wdt_all[:], in_=io["w_dt_t"][:, :]), 16)
        for j in range(32):
            sld.inc(q.dma_start(out=pcc_all[:, j * 22:(j + 1) * 22],
                                in_=io["pcc"][j * 128:(j + 1) * 128, :]), 16)
    pcc_ld = b.mark("ld")

    @b.on("scalar")
    def _(s):
        b.waits(s, pcc_ld)
        for j in range(32):
            ins = s.activation(out=A_all[:, j * 16:(j + 1) * 16],
                               in_=pcc_all[:, j * 22 + 6:j * 22 + 22], func=AF.Exp)
        b.sems["s"].inc(ins)
    aexp = b.mark("s")

    @b.on("vector")
    def _(v):
        b.waits(v, aexp)
        v.tensor_scalar_mul(out=A_all, in0=A_all, scalar1=-1.0)
        b.sems["v"].inc(v.memset(ones_t, 1.0))
    a_ready = b.mark("v")

    row_sizes = [(i * 128, min(T + 2, (i + 1) * 128)) for i in range(9)]
    xrow_free = [NONE, NONE]
    for i, (r0, r1) in enumerate(row_sizes):
        nr = r1 - r0

        @b.on("sync")
        def _(q, i=i, r0=r0, r1=r1):
            b.waits(q, xrow_free[i % 2])
            if i == 0:
                b.waits(q, *C["ready"])
            sld.inc(q.dma_start(out=xrow[i % 2][0:r1 - r0, 0:D],
                                in_=io["x_loc"][r0:r1, :]), 16)
        xld = b.mark("ld")

        for c in range(DC):
            bk = 6 + (i * DC + c) % 2

            @b.on("tensor")
            def _(t, i=i, c=c, nr=nr, bk=bk, xld=xld):
                if c == 0:
                    b.waits(t, xld)
                b.waits(t, b.bank_free[bk])
                ins = t.transpose(out=b.pool[bk][0:128, 0:nr],
                                  in_=xrow[i % 2][0:nr, c * 128:(c + 1) * 128],
                                  identity=C["ident"][0:nr, 0:nr])
                b.sems["t"].inc(ins)
            tr_mark = b.mark("t")

            @b.on("scalar")
            def _(s, c=c, r0=r0, nr=nr, bk=bk, tr_mark=tr_mark):
                b.waits(s, tr_mark)
                ins = s.activation(out=xfm[c][:, r0:r0 + nr],
                                   in_=b.pool[bk][0:128, 0:nr], func=AF.Copy)
                b.sems["s"].inc(ins)
            b.bank_free[bk] = [b.mark("s")]
        xrow_free[i % 2] = [b.mark("t")]
    xfm_ready = b.mark("s")

    win_free = [NONE, NONE]
    xp_free = NONE
    xcst_free = NONE
    xcfr_free = NONE
    zstg_free = NONE
    xp_tts = [(0, 512), (512, 1024), (1024, 1026)]
    z_tts = [(1, 513), (513, 1025)]
    wx_last = None

    for oc in range(64):
        is_xp = oc < 32
        j = oc if is_xp else oc - 32
        tts = xp_tts if is_xp else z_tts

        @b.on("sync")
        def _(q, oc=oc):
            b.waits(q, win_free[oc % 2])
            if oc == 0:
                b.waits(q, xfm_ready)
            sld.inc(q.dma_start(out=win_wb[oc % 2][:], in_=io["w_in_t"][oc, :, :]), 16)
        wld = b.mark("ld")

        evac_last = None
        for ti, (n0, n1) in enumerate(tts):
            bk = 4 + (oc * 3 + ti) % 2

            @b.on("tensor")
            def _(t, oc=oc, ti=ti, n0=n0, n1=n1, bk=bk, wld=wld):
                if ti == 0:
                    b.waits(t, wld)
                    if oc == 0:
                        b.waits(t, xfm_ready)
                b.waits(t, b.bank_free[bk])
                for k in range(DC):
                    ins = t.matmul(out=b.pool[bk][:, 0:n1 - n0],
                                   lhsT=win_wb[oc % 2][:, k * 128:(k + 1) * 128],
                                   rhs=xfm[k][:, n0:n1],
                                   start=(k == 0), stop=(k == DC - 1))
                b.sems["t"].inc(ins)
            mm_mark = b.mark("t")

            if is_xp:
                @b.on("scalar")
                def _(s, n0=n0, n1=n1, bk=bk, mm_mark=mm_mark):
                    b.waits(s, mm_mark, xp_free)
                    ins = s.copy(out=xp_sb[:, n0:n1], in_=b.pool[bk][:, 0:n1 - n0])
                    b.sems["s"].inc(ins)
                b.bank_free[bk] = [b.mark("s")]
                evac_last = b.mark("s")
            else:
                @b.on("scalar")
                def _(s, n0=n0, n1=n1, bk=bk, mm_mark=mm_mark):
                    b.waits(s, mm_mark, zstg_free)
                    ins = s.activation(out=z_stg[:, 0:n1 - n0],
                                       in_=b.pool[bk][:, 0:n1 - n0], func=AF.Silu)
                    b.sems["s"].inc(ins)
                b.bank_free[bk] = [b.mark("s")]
                zmk = b.mark("s")

                @b.on("gpsimd")
                def _(p, j=j, n0=n0, n1=n1, zmk=zmk):
                    b.waits(p, zmk)
                    sst.inc(p.dma_start(
                        out=io["z_h"][j * 128:(j + 1) * 128, n0 - 1:n1 - 1],
                        in_=z_stg[:, 0:n1 - n0]), 16)
                zstg_free = [("st", b.sems["st"].n)]

        if is_xp:
            @b.on("vector")
            def _(v, j=j, evac_last=evac_last):
                b.waits(v, evac_last, xcst_free)
                if j == 0:
                    b.waits(v, a_ready)
                cw = pcc_all[:, j * 22:j * 22 + 3]
                v.tensor_scalar_mul(out=xc_st, in0=xp_sb[:, 0:1024],
                                    scalar1=cw[:, 0:1])
                v.scalar_tensor_tensor(out=xc_st, in0=xp_sb[:, 1:1025],
                                       scalar=cw[:, 1:2], in1=xc_st,
                                       op0=ALU.mult, op1=ALU.add)
                ins = v.scalar_tensor_tensor(out=xc_st, in0=xp_sb[:, 2:1026],
                                             scalar=cw[:, 2:3], in1=xc_st,
                                             op0=ALU.mult, op1=ALU.add)
                b.sems["v"].inc(ins)
            xp_free = [b.mark("v")]
            conv_done = b.mark("v")

            @b.on("scalar")
            def _(s, j=j, conv_done=conv_done):
                b.waits(s, conv_done, xcfr_free)
                ins = s.activation(out=xc_frt, in_=xc_st, func=AF.Silu,
                                   bias=pcc_all[:, j * 22 + 3:j * 22 + 4])
                b.sems["s"].inc(ins)
            xcst_free = [b.mark("s")]
            silu_done = b.mark("s")

            @b.on("tensor")
            def _(t, j=j, silu_done=silu_done):
                b.waits(t, silu_done)
                for tt in range(2):
                    m0 = tt * 512
                    t.matmul(out=b.pool[tt][:, :],
                             lhsT=wx_all[:, j * 160:j * 160 + 128],
                             rhs=xc_frt[:, m0:m0 + 512],
                             start=(j == 0), stop=(j == 31))
                    ins = t.matmul(out=b.pool[2 + tt][0:32, :],
                                   lhsT=wx_all[:, j * 160 + 128:j * 160 + 160],
                                   rhs=xc_frt[:, m0:m0 + 512],
                                   start=(j == 0), stop=(j == 31))
                b.sems["t"].inc(ins)
            wx_mark = b.mark("t")
            wx_last = wx_mark

            @b.on("gpsimd")
            def _(p, j=j, silu_done=silu_done):
                b.waits(p, silu_done)
                sst.inc(p.dma_start(out=io["xc_h"][j * 128:(j + 1) * 128, :],
                                    in_=xc_frt[:, :]), 16)
            xcfr_free = [wx_mark, ("st", b.sems["st"].n)]

        win_free[oc % 2] = [b.mark("t")]

    xc_all_stored = b.mark("st")
    wx_all_done = wx_last

    @b.on("scalar")
    def _(s):
        b.waits(s, wx_all_done)
        for tt in range(2):
            s.activation(out=dtin_fr[:, tt * 512:(tt + 1) * 512],
                         in_=b.pool[tt][:, :], func=AF.Copy)
            ins = s.activation(out=bc_fr[:, tt * 512:(tt + 1) * 512],
                               in_=b.pool[2 + tt][0:32, :], func=AF.Copy)
        b.sems["s"].inc(ins)
    xdbl_done = b.mark("s")
    for bk in range(4):
        b.bank_free[bk] = [xdbl_done]

    @b.on("sync")
    def _(q):
        b.waits(q, wx_all_done)
        sld.inc(q.dma_start(out=selE_s[:, :], in_=io["selE"][:, :]), 16)
    selE_ld = b.mark("ld")

    dt8 = [ar.f32(RS + jj * 1024, 1024) for jj in range(8)]
    v8b = [ar.t(RS + 8192 + jj * 512, [128, 1024], BF16) for jj in range(8)]
    y8b = [ar.t(RS + 12288 + jj * 512, [128, 1024], BF16) for jj in range(8)]
    dA2 = [ar.f32(OA + i * 1024, 1024) for i in range(2)]
    w2b = [ar.t(OA + 2048 + i * 512, [128, 1024], BF16) for i in range(2)]
    h2b = [ar.t(OA + 3072 + i * 512, [128, 1024], BF16) for i in range(2)]
    t2b = [ar.t(OA + 4096 + i * 512, [128, 1024], BF16) for i in range(2)]
    B2b = [ar.t(OA + 5120 + i * 512, [128, 1024], BF16) for i in range(2)]
    C2b = [ar.t(OA + 6144 + i * 512, [128, 1024], BF16) for i in range(2)]
    xcl2 = [ar.f32r(OA + 7168 + i * 1024, 1024) for i in range(2)]
    zl2 = [ar.f32r(OA + 2048, 1024), ar.f32r(OA + 5120, 1024)]
    yfin32 = ar.f32(OA + 3072, 1024)
    cds8 = dt8

    brow_free = [NONE, NONE]
    crow_free = [NONE, NONE]
    xcl2_free = [NONE, NONE]
    spB_free = NONE
    cdsj_free = NONE
    dA_free = [NONE, NONE]
    w_free = [NONE, NONE]
    h_free = [NONE, NONE]
    t_free = [NONE, NONE]
    Bb_free = [NONE, NONE]
    Cb_free = [NONE, NONE]
    dt8_free = [NONE] * 8
    v8_free = [NONE] * 8
    y8_free = [NONE] * 8

    ui = 0
    for jb in range(4):
        for jj in range(8):
            j = jb * 8 + jj

            @b.on("tensor")
            def _(t, j=j, jj=jj):
                if j == 0:
                    b.waits(t, xdbl_done)
                for tt in range(2):
                    b.waits(t, b.bank_free[tt])
                    ins = t.matmul(out=b.pool[tt][:, :],
                                   lhsT=wdt_all[:, j * 128:(j + 1) * 128],
                                   rhs=dtin_fr[:, tt * 512:(tt + 1) * 512],
                                   start=True, stop=True)
                b.sems["t"].inc(ins)
            dtp_mark = b.mark("t")

            bcol = pcc_all[:, j * 22 + 4:j * 22 + 5]

            @b.on("vector")
            def _(v, jj=jj, dtp_mark=dtp_mark, bcol=bcol):
                b.waits(v, dtp_mark, dt8_free[jj])
                for tt in range(2):
                    n0 = tt * 512
                    ins = v.tensor_scalar(out=dt8[jj][:, n0:n0 + 512],
                                          in0=b.pool[tt][:, :], scalar1=bcol,
                                          scalar2=0.0, op0=ALU.add, op1=ALU.max)
                b.sems["v"].inc(ins)
            m_done = b.mark("v")

            @b.on("scalar")
            def _(s, dtp_mark=dtp_mark, bcol=bcol):
                b.waits(s, dtp_mark, spB_free)
                for tt in range(2):
                    n0 = tt * 512
                    s.activation(out=spB[:, n0:n0 + 512], in_=b.pool[tt][:, :],
                                 func=AF.Abs, bias=bcol)
                s.activation(out=spB, in_=spB, func=AF.Exp, scale=C["negone"])
                ins = s.activation(out=spB, in_=spB, func=AF.Ln, bias=C["one"])
                b.sems["s"].inc(ins)
            sp_done = b.mark("s")
            b.bank_free[0] = [m_done, sp_done]
            b.bank_free[1] = [m_done, sp_done]

            @b.on("vector")
            def _(v, jj=jj, sp_done=sp_done):
                b.waits(v, sp_done)
                v.tensor_tensor(out=dt8[jj], in0=dt8[jj], in1=spB, op=ALU.add)
                b.waits(v, cdsj_free)
                ins = v.tensor_tensor_scan(out=cds_j, data0=ones_t, data1=dt8[jj],
                                           initial=0.0, op0=ALU.mult, op1=ALU.add)
                b.sems["v"].inc(ins)
            spB_free = [b.mark("v")]
            cds_done = b.mark("v")

            @b.on("sync")
            def _(q, j=j, jj=jj):
                if j == 0:
                    b.waits(q, xc_all_stored)
                b.waits(q, xcl2_free[jj % 2])
                sld2.inc(q.dma_start(out=xcl2[jj % 2][:, :],
                                     in_=io["xc_h"][j * 128:(j + 1) * 128, :]), 16)
            xc_ld = b.mark("ld2")

            @b.on("vector")
            def _(v, jj=jj, xc_ld=xc_ld):
                b.waits(v, xc_ld, v8_free[jj])
                ins = v.tensor_tensor(out=v8b[jj][:, :], in0=dt8[jj],
                                      in1=xcl2[jj % 2], op=ALU.mult)
                b.sems["v"].inc(ins)
            vj_done = b.mark("v")
            xcl2_free[jj % 2] = [vj_done]

            @b.on("gpsimd")
            def _(p, jj=jj, vj_done=vj_done):
                b.waits(p, y8_free[jj])
                ins = p.memset(y8b[jj][:, :], 0.0)
                b.sems["p"].inc(ins)
            y8_free[jj] = NONE

            @b.on("gpsimd")
            def _(p, j=j, cds_done=cds_done):
                b.waits(p, cds_done)
                sst.inc(p.dma_start(out=io["cds_h"][j * 128:(j + 1) * 128, :],
                                    in_=cds_j[:, :]), 16)
            cds_st = ("st", b.sems["st"].n)

            @b.on("scalar")
            def _(s, j=j, cds_done=cds_done):
                b.waits(s, cds_done)
                ins = s.activation(out=D_all[:, j * 16:(j + 1) * 16],
                                   in_=A_all[:, j * 16:(j + 1) * 16], func=AF.Exp,
                                   scale=cds_j[:, 1023:1024])
                b.sems["s"].inc(ins)
            cdsj_free = [cds_st, b.mark("s")]
        blk_pre = [b.mark("v"), b.mark("s"), b.mark("p")]

        for si in range(16):
            pr = si % 2

            @b.on("tensor")
            def _(t, si=si):
                if si == 0:
                    b.waits(t, selE_ld, xdbl_done)
                for tt in range(2):
                    n0 = tt * 512
                    b.waits(t, b.bank_free[2 + tt])
                    t.matmul(out=b.pool[2 + tt][:, :],
                             lhsT=selE_s[:, si * 128:(si + 1) * 128],
                             rhs=bc_fr[0:32, n0:n0 + 512], start=True, stop=True)
                    b.waits(t, b.bank_free[4 + tt])
                    ins = t.matmul(out=b.pool[4 + tt][:, :],
                                   lhsT=selE_s[:, (16 + si) * 128:(17 + si) * 128],
                                   rhs=bc_fr[0:32, n0:n0 + 512], start=True,
                                   stop=True)
                b.sems["t"].inc(ins)
            bc_mm = b.mark("t")

            @b.on("scalar")
            def _(s, pr=pr, bc_mm=bc_mm):
                b.waits(s, bc_mm, Bb_free[pr], Cb_free[pr])
                for tt in range(2):
                    n0 = tt * 512
                    s.activation(out=B2b[pr][:, n0:n0 + 512],
                                 in_=b.pool[2 + tt][:, :], func=AF.Copy)
                    ins = s.activation(out=C2b[pr][:, n0:n0 + 512],
                                       in_=b.pool[4 + tt][:, :], func=AF.Copy)
                b.sems["s"].inc(ins)
            bc_evac = b.mark("s")
            for bk in [2, 3, 4, 5]:
                b.bank_free[bk] = [bc_evac]

            for jj in range(8):
                j = jb * 8 + jj
                up = ui % 2
                ui += 1

                @b.on("scalar")
                def _(s, j=j, jj=jj, si=si, up=up, blk_pre=blk_pre):
                    if si == 0 and jj == 0:
                        b.waits(s, *blk_pre)
                    b.waits(s, dA_free[up])
                    ins = s.activation(out=dA2[up], in_=dt8[jj], func=AF.Exp,
                                       scale=A_all[:, j * 16 + si:j * 16 + si + 1])
                    b.sems["s"].inc(ins)
                dA_done = b.mark("s")

                @b.on("vector")
                def _(v, jj=jj, si=si, pr=pr, up=up, dA_done=dA_done,
                      bc_evac=bc_evac, j=j):
                    b.waits(v, dA_done, bc_evac, w_free[up])
                    v.tensor_tensor(out=w2b[up][:, :], in0=v8b[jj][:, :],
                                    in1=B2b[pr][:, :], op=ALU.mult)
                    b.waits(v, h_free[up])
                    v.tensor_tensor_scan(out=h2b[up][:, :], data0=dA2[up],
                                         data1=w2b[up][:, :], initial=0.0,
                                         op0=ALU.mult, op1=ALU.add)
                    b.waits(v, t_free[up])
                    v.tensor_copy(out=F_all[:, j * 16 + si:j * 16 + si + 1],
                                  in_=h2b[up][:, 1023:1024])
                    ins = v.tensor_tensor(out=t2b[up][:, :], in0=h2b[up][:, :],
                                          in1=C2b[pr][:, :], op=ALU.mult)
                    b.sems["v"].inc(ins)
                unit_v = b.mark("v")
                dA_free[up] = [unit_v]
                w_free[up] = [unit_v]
                h_free[up] = [unit_v]
                if jj == 7:
                    Bb_free[pr] = [unit_v]
                    Cb_free[pr] = [unit_v]

                @b.on("gpsimd")
                def _(p, jj=jj, up=up, unit_v=unit_v):
                    b.waits(p, unit_v)
                    ins = p.tensor_tensor(out=y8b[jj][:, :], in0=y8b[jj][:, :],
                                          in1=t2b[up][:, :], op=ALU.add)
                    b.sems["p"].inc(ins)
                t_free[up] = [b.mark("p")]
        blk_scan = b.mark("p")

        @b.on("gpsimd")
        def _(p, jb=jb):
            for jj in range(8):
                j = jb * 8 + jj
                sst.inc(p.dma_start(out=io["y_h"][j * 128:(j + 1) * 128, :],
                                    in_=y8b[jj][:, :]), 16)
        ymk = ("st", b.sems["st"].n)
        for jj in range(8):
            y8_free[jj] = [ymk]
            dt8_free[jj] = [b.mark("s")]
            v8_free[jj] = [b.mark("v")]
    scan1_done = [b.mark("p"), b.mark("s"), b.mark("st"), b.mark("v"),
                  b.mark("t"), b.mark("ld2")]

    @b.on("gpsimd")
    def _(p):
        b.waits(p, *scan1_done)
        sst.inc(p.dma_start(out=io["ccFD_in"][:, 0:512], in_=F_all[:, :]), 16)
        sst.inc(p.dma_start(out=io["ccFD_in"][:, 512:1024], in_=D_all[:, :]), 16)
        p.wait_ge(b.sems["st"].h, b.sems["st"].n)
        b.sems["cc"].inc(p.collective_compute(
            "AllGather", ALU.bypass, replica_groups=RG,
            ins=[io["ccFD_in"].opt()], outs=[io["ccFD_out"].opt()]))
    cc_done = b.mark("cc")

    @b.on("vector")
    def _(v):
        b.waits(v, cc_done)
        b.sems["v"].inc(v.memset(H_all, 0.0))
    fd_prev = b.mark("v")
    for r in range(8):
        @b.on("sync")
        def _(q, r=r, fd_prev=fd_prev):
            b.waits(q, fd_prev)
            sld2.inc(q.dma_start(out=FD_r[:, :],
                                 in_=io["ccFD_out"][r * 128:(r + 1) * 128, :]), 16)
        fd_ld = b.mark("ld2")

        @b.on("vector")
        def _(v, r=r, fd_ld=fd_ld):
            b.waits(v, fd_ld)
            v.tensor_scalar(out=Dm_s, in0=FD_r[:, 512:1024],
                            scalar1=C["lt"][:, r:r + 1],
                            scalar2=C["ltinv"][:, r:r + 1],
                            op0=ALU.mult, op1=ALU.add)
            v.tensor_tensor(out=H_all, in0=H_all, in1=Dm_s, op=ALU.mult)
            v.tensor_scalar_mul(out=Dm_s, in0=FD_r[:, 0:512],
                                scalar1=C["lt"][:, r:r + 1])
            ins = v.tensor_tensor(out=H_all, in0=H_all, in1=Dm_s, op=ALU.add)
            b.sems["v"].inc(ins)
        fd_prev = b.mark("v")
    H_done = fd_prev

    e2 = dA2
    t92 = t2b
    e_free = dA_free
    t9_free = t_free
    C2_free = Cb_free
    cds8_free = [NONE] * 8
    y8p2_free = [NONE] * 8
    zl2_free = [NONE, NONE]
    yfin32_free = NONE

    for jb in range(4):
        for jj in range(8):
            j = jb * 8 + jj

            @b.on("sync")
            def _(q, j=j, jj=jj):
                if j == 0:
                    b.waits(q, H_done, *scan1_done)
                b.waits(q, cds8_free[jj])
                sld2.inc(q.dma_start(out=cds8[jj][:, :],
                                     in_=io["cds_h"][j * 128:(j + 1) * 128, :]), 16)

            @b.on("gpsimd")
            def _(p, j=j, jj=jj):
                if j == 0:
                    b.waits(p, H_done)
                b.waits(p, y8p2_free[jj])
                sst.inc(p.dma_start(out=y8b[jj][:, :],
                                    in_=io["y_h"][j * 128:(j + 1) * 128, :]), 16)
        ph2_ld = [b.mark("ld2"), b.mark("st")]

        for si in range(16):
            pr = si % 2

            @b.on("tensor")
            def _(t, si=si):
                for tt in range(2):
                    n0 = tt * 512
                    b.waits(t, b.bank_free[2 + tt])
                    ins = t.matmul(out=b.pool[2 + tt][:, :],
                                   lhsT=selE_s[:, (16 + si) * 128:(17 + si) * 128],
                                   rhs=bc_fr[0:32, n0:n0 + 512],
                                   start=True, stop=True)
                b.sems["t"].inc(ins)
            cmm = b.mark("t")

            @b.on("scalar")
            def _(s, pr=pr, cmm=cmm):
                b.waits(s, cmm, C2_free[pr])
                for tt in range(2):
                    n0 = tt * 512
                    ins = s.activation(out=C2b[pr][:, n0:n0 + 512],
                                       in_=b.pool[2 + tt][:, :], func=AF.Copy)
                b.sems["s"].inc(ins)
            cevac = b.mark("s")
            b.bank_free[2] = [cevac]
            b.bank_free[3] = [cevac]

            for jj in range(8):
                j = jb * 8 + jj
                up = ui % 2
                ui += 1

                @b.on("scalar")
                def _(s, j=j, jj=jj, si=si, up=up, ph2_ld=ph2_ld):
                    if si == 0 and jj == 0:
                        b.waits(s, *ph2_ld)
                    b.waits(s, e_free[up])
                    ins = s.activation(out=e2[up], in_=cds8[jj], func=AF.Exp,
                                       scale=A_all[:, j * 16 + si:j * 16 + si + 1])
                    b.sems["s"].inc(ins)
                e_done = b.mark("s")

                @b.on("vector")
                def _(v, j=j, si=si, pr=pr, up=up, e_done=e_done, cevac=cevac):
                    b.waits(v, e_done, cevac, t9_free[up])
                    ins = v.scalar_tensor_tensor(
                        out=t92[up][:, :], in0=e2[up],
                        scalar=H_all[:, j * 16 + si:j * 16 + si + 1],
                        in1=C2b[pr][:, :], op0=ALU.mult, op1=ALU.mult)
                    b.sems["v"].inc(ins)
                t9_done = b.mark("v")
                e_free[up] = [t9_done]
                if jj == 7:
                    C2_free[pr] = [t9_done]

                @b.on("gpsimd")
                def _(p, jj=jj, up=up, t9_done=t9_done):
                    b.waits(p, t9_done)
                    ins = p.tensor_tensor(out=y8b[jj][:, :], in0=y8b[jj][:, :],
                                          in1=t92[up][:, :], op=ALU.add)
                    b.sems["p"].inc(ins)
                t9_free[up] = [b.mark("p")]
        corr_blk = b.mark("p")
        sblk = b.mark("s")
        for jj in range(8):
            cds8_free[jj] = [sblk]

        for jj in range(8):
            j = jb * 8 + jj

            @b.on("sync")
            def _(q, j=j, jj=jj):
                b.waits(q, xcl2_free[jj % 2], zl2_free[jj % 2])
                sld2.inc(q.dma_start(out=xcl2[jj % 2][:, :],
                                     in_=io["xc_h"][j * 128:(j + 1) * 128, :]), 16)
                sld2.inc(q.dma_start(out=zl2[jj % 2][:, :],
                                     in_=io["z_h"][j * 128:(j + 1) * 128, :]), 16)
            fin_ld = b.mark("ld2")

            @b.on("vector")
            def _(v, j=j, jj=jj, fin_ld=fin_ld, corr_blk=corr_blk):
                b.waits(v, fin_ld, yfin32_free)
                if jj == 0:
                    b.waits(v, corr_blk)
                v.scalar_tensor_tensor(out=yfin32, in0=xcl2[jj % 2],
                                       scalar=pcc_all[:, j * 22 + 5:j * 22 + 6],
                                       in1=y8b[jj][:, :], op0=ALU.mult,
                                       op1=ALU.add)
                ins = v.tensor_tensor(out=fr(yfin32), in0=yfin32,
                                      in1=zl2[jj % 2], op=ALU.mult)
                b.sems["v"].inc(ins)
            yfin_done = b.mark("v")
            xcl2_free[jj % 2] = [yfin_done]
            zl2_free[jj % 2] = [yfin_done]
            y8p2_free[jj] = [yfin_done]

            @b.on("gpsimd")
            def _(p, j=j, yfin_done=yfin_done):
                b.waits(p, yfin_done)
                sst.inc(p.dma_start(out=io["yfin_h"][j * 128:(j + 1) * 128, :],
                                    in_=fr(yfin32)[:, :]), 16)
            yfin32_free = [("st", b.sems["st"].n)]
    yfin_all = b.mark("st")
    ph2_all = [b.mark("p"), b.mark("s"), b.mark("v"), b.mark("t"), yfin_all,
               b.mark("ld2")]


    ybuf_free = [ph2_all] * 8
    wout_free = [ph2_all, ph2_all]
    for oc in range(16):
        @b.on("sync")
        def _(q, oc=oc):
            if oc == 0:
                b.waits(q, yfin_all)
            b.waits(q, wout_free[oc % 2])
            sld.inc(q.dma_start(out=wout_wb[oc % 2][:],
                                in_=io["w_out_t"][oc, :, :]), 16)
        wo_ld = b.mark("ld")

        grp_marks = []
        for grp in range(4):
            @b.on("sync")
            def _(q, oc=oc, grp=grp):
                for k in range(8):
                    b.waits(q, ybuf_free[k])
                    kk = grp * 8 + k
                    sld2.inc(q.dma_start(
                        out=ybufs[k][:, :],
                        in_=io["yfin_h"][kk * 128:(kk + 1) * 128, :]), 16)
            y_ld = b.mark("ld2")

            for tt in range(2):
                bk = 6 + tt

                @b.on("tensor")
                def _(t, oc=oc, grp=grp, tt=tt, bk=bk, y_ld=y_ld, wo_ld=wo_ld):
                    b.waits(t, y_ld)
                    if grp == 0:
                        b.waits(t, wo_ld, b.bank_free[bk])
                    n0 = tt * 512
                    for k in range(8):
                        kk = grp * 8 + k
                        ins = t.matmul(out=b.pool[bk][:, :],
                                       lhsT=wout_wb[oc % 2][:, kk * 128:(kk + 1) * 128],
                                       rhs=ybufs[k][:, n0:n0 + 512],
                                       start=(grp == 0 and k == 0),
                                       stop=(grp == 3 and k == 7))
                    b.sems["t"].inc(ins)
            gm = b.mark("t")
            for k in range(8):
                ybuf_free[k] = [gm]
        wout_free[oc % 2] = [b.mark("t")]
        oc_mm = b.mark("t")

        @b.on("vector")
        def _(v, oc=oc, oc_mm=oc_mm):
            b.waits(v, oc_mm)
            for tt in range(2):
                n0 = tt * 512
                ins = v.tensor_copy(out=out_acc[oc][:, n0:n0 + 512],
                                    in_=b.pool[6 + tt][:, :])
            b.sems["v"].inc(ins)
        b.bank_free[6] = [b.mark("v")]
        b.bank_free[7] = [b.mark("v")]
    wout_done = b.mark("v")

    xrow2_free = [NONE, NONE]
    for i in range(8):
        @b.on("sync")
        def _(q, i=i):
            if i < 2:
                b.waits(q, wout_done)
            b.waits(q, xrow2_free[i % 2])
            sld.inc(q.dma_start(out=xrow2[i % 2][0:128, 0:D],
                                in_=io["x_loc"][1 + i * 128:1 + (i + 1) * 128, :]),
                    16)
        x2_ld = b.mark("ld")

        for c in range(DC):
            bk = (i * DC + c) % 2

            @b.on("tensor")
            def _(t, i=i, c=c, bk=bk, x2_ld=x2_ld):
                if c == 0:
                    b.waits(t, x2_ld)
                b.waits(t, b.bank_free[bk])
                if i == 0 and c == 0:
                    b.waits(t, wout_done)
                ins = t.transpose(out=b.pool[bk][0:128, 0:128],
                                  in_=xrow2[i % 2][0:128, c * 128:(c + 1) * 128],
                                  identity=C["ident"][:, :])
                b.sems["t"].inc(ins)
            tr_mark = b.mark("t")

            @b.on("vector")
            def _(v, i=i, c=c, bk=bk, tr_mark=tr_mark):
                b.waits(v, tr_mark)
                ins = v.tensor_tensor(out=out_acc[c][:, i * 128:(i + 1) * 128],
                                      in0=out_acc[c][:, i * 128:(i + 1) * 128],
                                      in1=b.pool[bk][0:128, 0:128], op=ALU.add)
                b.sems["v"].inc(ins)
            b.bank_free[bk] = [b.mark("v")]
        xrow2_free[i % 2] = [b.mark("t")]
    resid_done = b.mark("v")

    resid_tiles = out_acc
    lnout_free = [NONE, NONE]

    def pre_wait_fn(c, j):
        return lnout_free[c % 2]

    def post_fn(c, j, vmark):
        @b.on("gpsimd")
        def _(p, c=c, j=j, vmark=vmark):
            b.waits(p, vmark)
            n0, n1 = j * 512, (j + 1) * 512
            sst.inc(p.dma_start(out=io["xfm1"][c * 128:(c + 1) * 128, n0:n1],
                                in_=lnout[c % 2].bitcast(F32)[:, n0:n1]), 16)
        lnout_free[c % 2] = [("st", b.sems["st"].n)]

    out_aps = {c: lnout[c % 2] for c in range(DC)}
    emit_ln(b, C, resid_tiles, ln_scr, [resid_done], [0, 1, 2, 3, 4, 5],
            out_aps, pre_wait_fn, post_fn)

    @b.on("gpsimd")
    def _(p):
        p.wait_ge(b.sems["st"].h, b.sems["st"].n)
    return io



def declare_l0ffn(nc, pfx="", xin=None, out_kind="ExternalOutput"):
    io = {}
    io["xin"] = xin if xin is not None else nc.dram_tensor(
        pfx + "xin", [D, T], F32R, kind="ExternalInput").ap()
    io["gate_t"] = nc.dram_tensor(pfx + "gate_t", [128, 256], F32R, kind="ExternalInput").ap()
    io["ebias"] = nc.dram_tensor(pfx + "ebias", [16, 1], F32, kind="ExternalInput").ap()
    io["wg_t"] = nc.dram_tensor(pfx + "wg_t", [136, 128, 2048], F32R, kind="ExternalInput").ap()
    io["wu_t"] = nc.dram_tensor(pfx + "wu_t", [136, 128, 2048], F32R, kind="ExternalInput").ap()
    io["wd_t"] = nc.dram_tensor(pfx + "wd_t", [272, 128, 1024], F32R, kind="ExternalInput").ap()
    io["xfm2"] = nc.dram_tensor(pfx + "xfm2", [D, T], F32, kind=out_kind).ap()
    return io


def host_l0ffn_weights(inp):
    wg = np.asarray(inp["L0_ffn_w_gate"], np.float32)
    wu = np.asarray(inp["L0_ffn_w_up"], np.float32)
    wd = np.asarray(inp["L0_ffn_w_down"], np.float32)
    swg = np.asarray(inp["L0_ffn_sw_gate"], np.float32)
    swu = np.asarray(inp["L0_ffn_sw_up"], np.float32)
    swd = np.asarray(inp["L0_ffn_sw_down"], np.float32)
    gate_w = np.asarray(inp["L0_ffn_gate_w"], np.float32)

    def tile_gu(w):
        return host_tile_w_ocmajor(np.ascontiguousarray(w.T))

    wg_t = np.concatenate([tile_gu(wg[e]) for e in range(16)] + [tile_gu(swg)])
    wu_t = np.concatenate([tile_gu(wu[e]) for e in range(16)] + [tile_gu(swu)])

    def tile_d(w):
        return host_tile_w_ocmajor(np.ascontiguousarray(w.T))

    wd_t = np.concatenate([tile_d(wd[e]) for e in range(16)] + [tile_d(swd)])
    gt = np.ascontiguousarray(
        gate_w.T.reshape(16, 128, 16).transpose(1, 0, 2).reshape(128, 256))
    eb = np.ascontiguousarray(np.asarray(inp["L0_ffn_expert_bias"],
                                         np.float32).reshape(16, 1))
    return dict(gate_t=gt, ebias=eb,
                wg_t=np.ascontiguousarray(wg_t),
                wu_t=np.ascontiguousarray(wu_t),
                wd_t=np.ascontiguousarray(wd_t))


def build_l0ffn(b, C, io, start_dep=()):
    ar = Arena(b, "ar2", 49500)
    xn = [ar.f32r(c * 1024, 1024) for c in range(DC)]
    dn_acc = [ar.f32(16384 + c * 1024, 1024) for c in range(DC)]
    HG = 32768
    h_half = [ar.f32r(HG + i * 512, 512) for i in range(8)]
    lnout = [ar.f32r(HG, 1024), ar.f32r(HG + 1024, 1024)]
    WB = 36864
    wgb = [ar.f32r(WB, 2048)]
    wub = [ar.f32r(WB + 2048, 2048)]
    wdb = [ar.f32r(WB + 4096 + i * 1024, 1024) for i in range(2)]
    ln_scr = dict(sq=[ar.f32r(WB, 512), ar.f32r(WB + 512, 512)],
                  mean=ar.f32r(WB + 1024, 1024, p=1),
                  var=ar.f32r(WB + 2048, 1024, p=1),
                  rstd=ar.f32r(WB + 3072, 1024, p=1))
    MS = 43008
    gate_all = ar.f32r(MS, 256)
    sc_E = ar.f32(MS + 256, 1024, p=16)
    bb_E = ar.f32(MS + 1280, 1024, p=16)
    cwE = ar.f32r(MS + 2304, 1024, p=16)
    ebias_s = ar.f32(MS + 3328, 8, p=16)
    smallT = ar.f32(MS + 3336, 80)
    bb_T = smallT[:, 0:16]
    scT = smallT[:, 16:32]
    indacc = smallT[:, 32:48]
    tmp16 = smallT[:, 48:64]
    mxcol = smallT[:, 64:65]
    rscol = smallT[:, 65:66]
    cwrow = ar.f32r(MS + 3416, 1024, p=1)
    cwb = ar.f32(MS + 4440, 1024)
    gsb = ar.f32(MS + 5464, 512)
    hh = ar.f32(MS + 5976, 512)

    sld = b.sems["ld"]
    sld2 = b.sems["ld2"]
    sst = b.sems["st"]

    @b.on("sync")
    def _(q):
        b.waits(q, *C["ready"], *start_dep)
        for c in range(DC):
            sld.inc(q.dma_start(out=xn[c][:, :],
                                in_=io["xin"][c * 128:(c + 1) * 128, :]), 16)
        sld.inc(q.dma_start(out=gate_all[:, :], in_=io["gate_t"][:, :]), 16)
        sld.inc(q.dma_start(out=ebias_s[:, 0:1], in_=io["ebias"][:, :]), 16)
    xn_ld = b.mark("ld")

    @b.on("tensor")
    def _(t):
        b.waits(t, xn_ld)
        for tt in range(2):
            n0 = tt * 512
            for k in range(DC):
                ins = t.matmul(out=b.pool[tt][0:16, :],
                               lhsT=gate_all[:, k * 16:(k + 1) * 16],
                               rhs=xn[k][:, n0:n0 + 512],
                               start=(k == 0), stop=(k == DC - 1))
        b.sems["t"].inc(ins)
    lg_mm = b.mark("t")

    @b.on("scalar")
    def _(s):
        b.waits(s, lg_mm)
        for tt in range(2):
            n0 = tt * 512
            ins = s.activation(out=sc_E[:, n0:n0 + 512], in_=b.pool[tt][0:16, :],
                               func=AF.Sigmoid)
        b.sems["s"].inc(ins)
    sc_done = b.mark("s")
    b.bank_free[0] = [sc_done]
    b.bank_free[1] = [sc_done]

    @b.on("vector")
    def _(v):
        b.waits(v, sc_done)
        ins = v.tensor_scalar_add(out=bb_E, in0=sc_E, scalar1=ebias_s[:, 0:1])
        b.sems["v"].inc(ins)
    bb_done = b.mark("v")

    for tg in range(8):
        m0 = tg * 128

        @b.on("tensor")
        def _(t, tg=tg, m0=m0):
            if tg == 0:
                b.waits(t, bb_done)
            b.waits(t, b.bank_free[2], b.bank_free[3])
            t.transpose(out=b.pool[2][0:128, 0:16], in_=bb_E[0:16, m0:m0 + 128],
                        identity=C["ident"][0:16, 0:16])
            ins = t.transpose(out=b.pool[3][0:128, 0:16],
                              in_=sc_E[0:16, m0:m0 + 128],
                              identity=C["ident"][0:16, 0:16])
            b.sems["t"].inc(ins)
        tr_done = b.mark("t")

        @b.on("vector")
        def _(v, tg=tg, tr_done=tr_done):
            b.waits(v, tr_done)
            b.chain(v, "v", v.tensor_copy(out=bb_T, in_=b.pool[2][0:128, 0:16]))
            b.chain(v, "v", v.tensor_copy(out=scT, in_=b.pool[3][0:128, 0:16]))
            b.chain(v, "v", v.memset(indacc, 0.0))
            for it in range(4):
                b.chain(v, "v", v.reduce_max(out=mxcol, in_=bb_T, axis=AX.X))
                b.chain(v, "v", v.tensor_single_scalar(out=tmp16, in_=bb_T,
                                                       scalar=mxcol,
                                                       op=ALU.is_ge))
                b.chain(v, "v", v.tensor_tensor(out=indacc, in0=indacc,
                                                in1=tmp16, op=ALU.max))
                b.chain(v, "v", v.scalar_tensor_tensor(out=bb_T, in0=tmp16,
                                                       scalar=-1e30, in1=bb_T,
                                                       op0=ALU.mult, op1=ALU.add))
            b.chain(v, "v", v.tensor_tensor(out=tmp16, in0=indacc, in1=scT,
                                            op=ALU.mult))
            b.chain(v, "v", v.reduce_sum(out=rscol, in_=tmp16, axis=AX.X))
            b.chain(v, "v", v.reciprocal(out=rscol, in_=rscol))
            b.chain(v, "v", v.tensor_scalar_mul(out=tmp16, in0=tmp16,
                                                scalar1=rscol))
        cw_done = b.mark("v")
        b.bank_free[2] = [cw_done]
        b.bank_free[3] = [cw_done]

        @b.on("tensor")
        def _(t, tg=tg, cw_done=cw_done):
            b.waits(t, cw_done, b.bank_free[4])
            ins = t.transpose(out=b.pool[4][0:16, 0:128], in_=tmp16[:, :],
                              identity=C["ident"][:, :])
            b.sems["t"].inc(ins)
        cwT_done = b.mark("t")

        @b.on("scalar")
        def _(s, tg=tg, m0=m0, cwT_done=cwT_done):
            b.waits(s, cwT_done)
            ins = s.activation(out=cwE[:, m0:m0 + 128], in_=b.pool[4][0:16, 0:128],
                               func=AF.Copy)
            b.sems["s"].inc(ins)
        b.bank_free[4] = [b.mark("s")]
    cwE_done = b.mark("s")

    wg_free = [NONE]
    wu_free = [NONE]
    wd_free = [NONE, NONE]
    cwrow_free = NONE
    cwb_free = NONE
    gsb_free = NONE
    hh_free = NONE
    hhalf_free = [NONE] * 8
    dn_marks = [NONE] * DC

    for e in range(17):
        routed = e < 16
        if routed:
            @b.on("gpsimd")
            def _(p, e=e):
                if e == 0:
                    b.waits(p, cwE_done)
                b.waits(p, cwrow_free)
                sst.inc(p.dma_start(out=cwrow[:, :], in_=cwE[e:e + 1, :]), 16)
            cwrow_ld = b.mark("st")

            @b.on("tensor")
            def _(t, cwrow_ld=cwrow_ld):
                b.waits(t, cwrow_ld, b.bank_free[0], b.bank_free[1])
                for tt in range(2):
                    n0 = tt * 512
                    ins = t.matmul(out=b.pool[tt][:, :], lhsT=C["ones_row"][:, :],
                                   rhs=cwrow[:, n0:n0 + 512], start=True,
                                   stop=True)
                b.sems["t"].inc(ins)
            cwmm = b.mark("t")
            cwrow_free = [cwmm]

            @b.on("vector")
            def _(v, cwmm=cwmm):
                b.waits(v, cwmm, cwb_free)
                for tt in range(2):
                    n0 = tt * 512
                    ins = v.tensor_copy(out=cwb[:, n0:n0 + 512],
                                        in_=b.pool[tt][:, :])
                b.sems["v"].inc(ins)
            cwb_done = b.mark("v")
            b.bank_free[0] = [cwb_done]
            b.bank_free[1] = [cwb_done]
        else:
            cwb_done = NONE

        for tt in range(2):
            n0 = tt * 512
            for ic in range(8):
                wslot = e * 8 + ic

                @b.on("sync")
                def _(q, e=e, tt=tt, ic=ic, wslot=wslot):
                    b.waits(q, wg_free[0], wu_free[0])
                    sld.inc(q.dma_start(out=wgb[0][:, :],
                                        in_=io["wg_t"][wslot, :, :]), 16)
                    sld.inc(q.dma_start(out=wub[0][:, :],
                                        in_=io["wu_t"][wslot, :, :]), 16)
                w_ld = b.mark("ld")

                gbk = 2 + ic % 2
                ubk = 4 + ic % 2

                @b.on("tensor")
                def _(t, e=e, tt=tt, ic=ic, n0=n0, gbk=gbk, ubk=ubk, w_ld=w_ld):
                    b.waits(t, w_ld, b.bank_free[gbk], b.bank_free[ubk])
                    for k in range(DC):
                        t.matmul(out=b.pool[gbk][:, :],
                                 lhsT=wgb[0][:, k * 128:(k + 1) * 128],
                                 rhs=xn[k][:, n0:n0 + 512],
                                 start=(k == 0), stop=(k == DC - 1))
                    for k in range(DC):
                        ins = t.matmul(out=b.pool[ubk][:, :],
                                       lhsT=wub[0][:, k * 128:(k + 1) * 128],
                                       rhs=xn[k][:, n0:n0 + 512],
                                       start=(k == 0), stop=(k == DC - 1))
                    b.sems["t"].inc(ins)
                gu_mm = b.mark("t")
                wg_free[0] = [gu_mm]
                wu_free[0] = [gu_mm]

                @b.on("scalar")
                def _(s, gbk=gbk, gu_mm=gu_mm):
                    b.waits(s, gu_mm, gsb_free)
                    ins = s.activation(out=gsb[:, :], in_=b.pool[gbk][:, :],
                                       func=AF.Silu)
                    b.sems["s"].inc(ins)
                gs_done = b.mark("s")
                b.bank_free[gbk] = [gs_done]

                @b.on("vector")
                def _(v, ubk=ubk, gs_done=gs_done, routed=routed, ic=ic,
                      n0=n0, cwb_done=cwb_done):
                    b.waits(v, gs_done, hh_free, hhalf_free[ic])
                    if routed:
                        v.tensor_tensor(out=hh, in0=gsb, in1=b.pool[ubk][:, :],
                                        op=ALU.mult)
                        ins = v.tensor_tensor(out=h_half[ic], in0=hh,
                                              in1=cwb[:, n0:n0 + 512],
                                              op=ALU.mult)
                    else:
                        ins = v.tensor_tensor(out=h_half[ic], in0=gsb,
                                              in1=b.pool[ubk][:, :], op=ALU.mult)
                    b.sems["v"].inc(ins)
                hdone = b.mark("v")
                gsb_free = [hdone]
                hh_free = [hdone]
                b.bank_free[ubk] = [hdone]
                hhalf_free[ic] = NONE

            for hc in range(DC):
                wslot = e * 16 + hc

                @b.on("sync")
                def _(q, wslot=wslot, hc=hc):
                    b.waits(q, wd_free[hc % 2])
                    sld2.inc(q.dma_start(out=wdb[hc % 2][:, :],
                                         in_=io["wd_t"][wslot, :, :]), 16)
                wd_ld = b.mark("ld2")

                dbk = 6 + hc % 2
                hmark = b.mark("v")

                @b.on("tensor")
                def _(t, hc=hc, wd_ld=wd_ld, dbk=dbk, hmark=hmark):
                    b.waits(t, wd_ld, b.bank_free[dbk])
                    if hc == 0:
                        b.waits(t, hmark)
                    for k in range(8):
                        ins = t.matmul(out=b.pool[dbk][:, :],
                                       lhsT=wdb[hc % 2][:, k * 128:(k + 1) * 128],
                                       rhs=h_half[k][:, :],
                                       start=(k == 0), stop=(k == 7))
                    b.sems["t"].inc(ins)
                dn_mm = b.mark("t")
                wd_free[hc % 2] = [dn_mm]

                @b.on("vector")
                def _(v, e=e, tt=tt, hc=hc, n0=n0, dbk=dbk, dn_mm=dn_mm):
                    b.waits(v, dn_mm)
                    if e == 0:
                        ins = v.tensor_copy(out=dn_acc[hc][:, n0:n0 + 512],
                                            in_=b.pool[dbk][:, :])
                    else:
                        ins = v.tensor_tensor(out=dn_acc[hc][:, n0:n0 + 512],
                                              in0=dn_acc[hc][:, n0:n0 + 512],
                                              in1=b.pool[dbk][:, :], op=ALU.add)
                    b.sems["v"].inc(ins)
                b.bank_free[dbk] = [b.mark("v")]
            down_done = b.mark("t")
            for ic in range(8):
                hhalf_free[ic] = [down_done]
    dn_done = b.mark("v")

    @b.on("vector")
    def _(v):
        b.waits(v, dn_done)
        for c in range(DC):
            ins = v.tensor_tensor(out=xn[c][:, :], in0=xn[c][:, :],
                                  in1=dn_acc[c][:, :], op=ALU.add)
        b.sems["v"].inc(ins)
    resid_done = b.mark("v")

    lnout_free = [NONE, NONE]

    def pre_wait_fn(c, j):
        return lnout_free[c % 2]

    def post_fn(c, j, vmark):
        @b.on("gpsimd")
        def _(p, c=c, j=j, vmark=vmark):
            b.waits(p, vmark)
            n0, n1 = j * 512, (j + 1) * 512
            sst.inc(p.dma_start(out=io["xfm2"][c * 128:(c + 1) * 128, n0:n1],
                                in_=lnout[c % 2].bitcast(F32)[:, n0:n1]), 16)
        lnout_free[c % 2] = [("st", b.sems["st"].n)]

    out_aps = {c: lnout[c % 2] for c in range(DC)}
    emit_ln(b, C, xn, ln_scr, [resid_done], [0, 1, 2, 3, 4, 5],
            out_aps, pre_wait_fn, post_fn)

    @b.on("gpsimd")
    def _(p):
        p.wait_ge(b.sems["st"].h, b.sems["st"].n)
    return io



def declare_l1attn(nc, pfx="", xin=None, out_kind="ExternalOutput"):
    io = {}
    io["xin"] = xin if xin is not None else nc.dram_tensor(
        pfx + "xin", [D, T], F32R, kind="ExternalInput").ap()
    io["wact_t"] = nc.dram_tensor(pfx + "wact_t", [16, 128, 4096], F32R,
                                  kind="ExternalInput").ap()
    io["wrec_t"] = nc.dram_tensor(pfx + "wrec_t", [16, 128, 4096], F32R,
                                  kind="ExternalInput").ap()
    io["wval_t"] = nc.dram_tensor(pfx + "wval_t", [16, 128, 2048], F32R,
                                  kind="ExternalInput").ap()
    io["xfm3"] = nc.dram_tensor(pfx + "xfm3", [D, T], F32, kind=out_kind).ap()
    io["cc_sh_in"] = nc.dram_tensor(pfx + "cc_sh_in", [2048, 1], F32, kind="Internal").ap()
    io["cc_sh_out"] = nc.dram_tensor(pfx + "cc_sh_out", [16384, 1], F32, kind="Internal").ap()
    io["cc_mx_in"] = nc.dram_tensor(pfx + "cc_mx_in", [1024, 1], F32, kind="Internal").ap()
    io["cc_mx_out"] = nc.dram_tensor(pfx + "cc_mx_out", [8192, 1], F32, kind="Internal").ap()
    io["cc_q_in"] = nc.dram_tensor(pfx + "cc_q_in", [2048, 1], F32, kind="Internal").ap()
    io["cc_q_out"] = nc.dram_tensor(pfx + "cc_q_out", [16384, 1], F32, kind="Internal").ap()
    return io


def host_l1attn_weights(inp):
    W_act = np.asarray(inp["L1_attn_W_act"], np.float32)
    W_rec = np.asarray(inp["L1_attn_W_rec"], np.float32)
    W_val = np.asarray(inp["L1_attn_W_val"], np.float32)
    perm = np.concatenate([np.arange(0, 2048, 2), np.arange(1, 2048, 2)])
    wact = np.ascontiguousarray(W_act.T[:, perm])
    wrec = np.ascontiguousarray(W_rec.T)
    wval = np.ascontiguousarray(W_val.T[perm, :])
    return dict(wact_t=host_tile_w_ocmajor(wact),
                wrec_t=host_tile_w_ocmajor(wrec),
                wval_t=host_tile_w_ocmajor(wval))


def build_l1attn(b, C, io, start_dep=()):
    ar = Arena(b, "ar3", 50400)
    x1e = [ar.f32r(c * 1032, 1032) for c in range(DC)]
    XV = 16512
    xv = [ar.f32(XV + c * 1024, 1024) for c in range(DC)]
    kv = [ar.f32r(XV + c * 1024, 1024) for c in range(DC)]
    resid = kv
    M = 32896
    m_t = [ar.f32(M + i * 1024, 1024) for i in range(8)]
    W = 41088
    wactb = ar.f32r(W, 4096)
    xvtmp = ar.f32(W + 4096, 512)
    pong = [ar.f32(W + i * 1024, 1024) for i in range(2)]
    tsc = [ar.f32(W + 2048 + i * 1024, 1024) for i in range(4)]
    mscr = ar.f32(W + 6144, 1024)
    wvalb = ar.f32r(W, 2048)
    wrecb = ar.f32r(W + 2048, 4096)
    rr = ar.f32(W + 6144, 512)
    r2 = ar.f32(W + 6656, 512)
    r4 = ar.f32(W + 7168, 512)
    ln_scr = dict(sq=[ar.f32r(W, 512), ar.f32r(W + 512, 512)],
                  mean=ar.f32r(W + 1024, 1024, p=1),
                  var=ar.f32r(W + 2048, 1024, p=1),
                  rstd=ar.f32r(W + 3072, 1024, p=1))
    lnout = [ar.f32r(W + 4096, 1024), ar.f32r(W + 5120, 1024)]
    SM = 49280
    shtab = ar.f32(SM, 8)
    mtab = ar.f32(SM + 8, 8)
    rtab = ar.f32(SM + 16, 8)
    itab = ar.f32(SM + 24, 8)
    maxoff = ar.f32(SM + 32, 8)
    pcol = ar.f32(SM + 40, 8)
    shcol = ar.f32(SM + 48, 8)
    assert SM + 56 <= 50400

    sld = b.sems["ld"]
    sld2 = b.sems["ld2"]
    sst = b.sems["st"]

    @b.on("sync")
    def _(q):
        b.waits(q, *C["ready"], *start_dep)
        for c in range(DC):
            sld.inc(q.dma_start(out=x1e[c][:, 1:1025],
                                in_=io["xin"][c * 128:(c + 1) * 128, :]), 16)
    x_ld = b.mark("ld")

    @b.on("gpsimd")
    def _(p):
        b.waits(p, x_ld)
        for c in range(DC):
            sst.inc(p.dma_start(out=io["cc_sh_in"][c * 128:(c + 1) * 128, :],
                                in_=x1e[c][:, 1024:1025].bitcast(F32)), 16)
        p.wait_ge(b.sems["st"].h, b.sems["st"].n)
        b.sems["cc"].inc(p.collective_compute(
            "AllGather", ALU.bypass, replica_groups=RG,
            ins=[io["cc_sh_in"].opt()], outs=[io["cc_sh_out"].opt()]))
    sh_cc = b.mark("cc")

    prev_comb = NONE
    for c in range(DC):
        @b.on("sync")
        def _(q, c=c, prev_comb=prev_comb):
            if c == 0:
                b.waits(q, sh_cc)
            b.waits(q, prev_comb)
            for r in range(8):
                sld2.inc(q.dma_start(
                    out=shtab[:, r:r + 1],
                    in_=io["cc_sh_out"][r * 2048 + c * 128:r * 2048 + (c + 1) * 128, :]),
                    16)
        sh_ld = b.mark("ld2")

        @b.on("vector")
        def _(v, c=c, sh_ld=sh_ld):
            b.waits(v, sh_ld)
            b.chain(v, "v", v.tensor_tensor(out=shcol, in0=shtab,
                                            in1=C["prev"][:, :], op=ALU.mult))
            with b.nc.allow_low_precision(reason="f32r shift column"):
                b.chain(v, "v", v.reduce_sum(out=x1e[c][:, 0:1], in_=shcol,
                                             axis=AX.X))
        prev_comb = b.mark("v")
    shift_done = b.mark("v")

    wact_free = NONE
    xvtmp_free = NONE

    def ct_rhs(k, n0, n1):
        if k < DC:
            return x1e[k][:, n0:n1]
        return x1e[k - DC][:, n0 + 1:n1 + 1]

    for oc in range(DC):
        @b.on("sync")
        def _(q, oc=oc):
            if oc == 0:
                b.waits(q, shift_done)
            b.waits(q, wact_free)
            sld.inc(q.dma_start(out=wactb[:, :], in_=io["wact_t"][oc, :, :]), 16)
        wld = b.mark("ld")

        for tt in range(2):
            n0 = tt * 512
            bk = tt

            @b.on("tensor")
            def _(t, oc=oc, tt=tt, n0=n0, bk=bk, wld=wld):
                if tt == 0:
                    b.waits(t, wld)
                    if oc == 0:
                        b.waits(t, shift_done)
                b.waits(t, b.bank_free[bk])
                for k in range(32):
                    ins = t.matmul(out=b.pool[bk][:, :],
                                   lhsT=wactb[:, k * 128:(k + 1) * 128],
                                   rhs=ct_rhs(k, n0, n0 + 512),
                                   start=(k == 0), stop=(k == 31))
                b.sems["t"].inc(ins)
            mm = b.mark("t")

            @b.on("vector")
            def _(v, bk=bk, mm=mm):
                b.waits(v, mm, xvtmp_free)
                ins = v.tensor_scalar_max(out=xvtmp, in0=b.pool[bk][:, :],
                                          scalar1=0.0)
                b.sems["v"].inc(ins)
            relu_done = b.mark("v")
            b.bank_free[bk] = [relu_done]

            @b.on("scalar")
            def _(s, oc=oc, n0=n0, relu_done=relu_done):
                b.waits(s, relu_done)
                ins = s.activation(out=xv[oc][:, n0:n0 + 512], in_=xvtmp,
                                   func=AF.Sigmoid)
                b.sems["s"].inc(ins)
            xvtmp_free = [b.mark("s")]
        wact_free = [b.mark("t")]
    xv_done = b.mark("s")

    mag_marks = []
    for pc in range(8):
        @b.on("scalar")
        def _(s, pc=pc):
            if pc == 0:
                b.waits(s, xv_done)
            b.waits(s, b.mark("v"))
            s.activation(out=mscr, in_=xv[pc][:, :], func=AF.Square)
            ins = s.activation(out=m_t[pc][:, :], in_=xv[8 + pc][:, :],
                               func=AF.Square)
            b.sems["s"].inc(ins)
        sq_done = b.mark("s")

        @b.on("vector")
        def _(v, pc=pc, sq_done=sq_done):
            b.waits(v, sq_done)
            ins = v.tensor_tensor(out=m_t[pc][:, :], in0=m_t[pc][:, :],
                                  in1=mscr, op=ALU.add)
            b.sems["v"].inc(ins)
        add_done = b.mark("v")

        @b.on("scalar")
        def _(s, pc=pc, add_done=add_done):
            b.waits(s, add_done)
            ins = s.activation(out=m_t[pc][:, :], in_=m_t[pc][:, :], func=AF.Sqrt)
            b.sems["s"].inc(ins)
        mag_done = b.mark("s")

        @b.on("vector")
        def _(v, pc=pc, mag_done=mag_done):
            b.waits(v, mag_done)
            cur, oth = m_t[pc], mscr
            for k in [1, 2, 4, 8, 16, 32, 64, 128, 256, 512]:
                b.chain(v, "v", v.tensor_copy(out=oth[:, 0:k], in_=cur[:, 0:k]))
                b.chain(v, "v", v.tensor_max(out=oth[:, k:1024], in0=cur[:, k:1024],
                                             in1=cur[:, 0:1024 - k]))
                cur, oth = oth, cur
            assert cur is m_t[pc]
        cm_done = b.mark("v")

        @b.on("gpsimd")
        def _(p, pc=pc, cm_done=cm_done):
            b.waits(p, cm_done)
            sst.inc(p.dma_start(out=io["cc_mx_in"][pc * 128:(pc + 1) * 128, :],
                                in_=m_t[pc][:, 1023:1024]), 16)
    mx_st = b.mark("st")

    @b.on("gpsimd")
    def _(p):
        p.wait_ge(b.sems["st"].h, b.sems["st"].n)
        b.sems["cc"].inc(p.collective_compute(
            "AllGather", ALU.bypass, replica_groups=RG,
            ins=[io["cc_mx_in"].opt()], outs=[io["cc_mx_out"].opt()]))
    mx_cc = b.mark("cc")

    @b.on("vector")
    def _(v):
        b.waits(v, mx_cc)
        ins = v.tensor_scalar_mul(out=maxoff, in0=C["ltinv"][:, :], scalar1=-1e30)
        b.sems["v"].inc(ins)
    moff_done = b.mark("v")

    q_marks = []
    for pc in range(8):
        @b.on("sync")
        def _(q, pc=pc):
            b.waits(q, mx_cc)
            if pc > 0:
                b.waits(q, q_marks[-1])
            for r in range(8):
                sld2.inc(q.dma_start(
                    out=mtab[:, r:r + 1],
                    in_=io["cc_mx_out"][r * 1024 + pc * 128:r * 1024 + (pc + 1) * 128, :]),
                    16)
        mt_ld = b.mark("ld2")

        @b.on("vector")
        def _(v, pc=pc, mt_ld=mt_ld):
            b.waits(v, mt_ld, moff_done)
            b.chain(v, "v", v.tensor_tensor(out=shcol, in0=mtab, in1=maxoff,
                                            op=ALU.add))
            b.chain(v, "v", v.reduce_max(out=pcol[:, 0:1], in_=shcol, axis=AX.X))
            b.chain(v, "v", v.tensor_scalar_max(out=m_t[pc][:, :],
                                                in0=m_t[pc][:, :],
                                                scalar1=pcol[:, 0:1]))
            v.reciprocal(out=m_t[pc][:, :], in_=m_t[pc][:, :])
            v.tensor_tensor(out=xv[pc][:, :], in0=xv[pc][:, :], in1=m_t[pc][:, :],
                            op=ALU.mult)
            ins = v.tensor_tensor(out=xv[8 + pc][:, :], in0=xv[8 + pc][:, :],
                                  in1=m_t[pc][:, :], op=ALU.mult)
            b.sems["v"].inc(ins)
        norm_done = b.mark("v")

        @b.on("vector")
        def _(v, pc=pc):
            ra, ia = xv[pc], xv[8 + pc]
            rb, ib = pong[0], pong[1]
            for k in [1, 2, 4, 8, 16, 32, 64, 128, 256, 512]:
                b.chain(v, "v", v.tensor_copy(out=rb[:, 0:k], in_=ra[:, 0:k]))
                b.chain(v, "v", v.tensor_copy(out=ib[:, 0:k], in_=ia[:, 0:k]))
                v.tensor_tensor(out=tsc[0][:, 0:1024 - k], in0=ra[:, k:1024],
                                in1=ra[:, 0:1024 - k], op=ALU.mult)
                v.tensor_tensor(out=tsc[1][:, 0:1024 - k], in0=ia[:, k:1024],
                                in1=ia[:, 0:1024 - k], op=ALU.mult)
                v.tensor_tensor(out=tsc[2][:, 0:1024 - k], in0=ra[:, k:1024],
                                in1=ia[:, 0:1024 - k], op=ALU.mult)
                v.tensor_tensor(out=tsc[3][:, 0:1024 - k], in0=ia[:, k:1024],
                                in1=ra[:, 0:1024 - k], op=ALU.mult)
                v.tensor_tensor(out=rb[:, k:1024], in0=tsc[0][:, 0:1024 - k],
                                in1=tsc[1][:, 0:1024 - k], op=ALU.subtract)
                b.chain(v, "v", v.tensor_tensor(out=ib[:, k:1024],
                                                in0=tsc[2][:, 0:1024 - k],
                                                in1=tsc[3][:, 0:1024 - k],
                                                op=ALU.add))
                ra, ia = rb, ib
                rb, ib = (xv[pc], xv[8 + pc]) if ra is pong[0] else (pong[0], pong[1])
            assert ra is xv[pc]
        cp_done = b.mark("v")

        @b.on("gpsimd")
        def _(p, pc=pc, cp_done=cp_done):
            b.waits(p, cp_done)
            sst.inc(p.dma_start(out=io["cc_q_in"][pc * 128:(pc + 1) * 128, :],
                                in_=xv[pc][:, 1023:1024]), 16)
            sst.inc(p.dma_start(out=io["cc_q_in"][1024 + pc * 128:1024 + (pc + 1) * 128, :],
                                in_=xv[8 + pc][:, 1023:1024]), 16)
        q_marks.append(("st", b.sems["st"].n))
    q_st = b.mark("st")

    @b.on("gpsimd")
    def _(p):
        p.wait_ge(b.sems["st"].h, b.sems["st"].n)
        b.sems["cc"].inc(p.collective_compute(
            "AllGather", ALU.bypass, replica_groups=RG,
            ins=[io["cc_q_in"].opt()], outs=[io["cc_q_out"].opt()]))
    q_cc = b.mark("cc")

    prev_q = NONE
    kvdone = []
    for pc in range(8):
        @b.on("sync")
        def _(q, pc=pc, prev_q=prev_q):
            if pc == 0:
                b.waits(q, q_cc)
            b.waits(q, prev_q)
            for r in range(8):
                sld2.inc(q.dma_start(
                    out=rtab[:, r:r + 1],
                    in_=io["cc_q_out"][r * 2048 + pc * 128:r * 2048 + (pc + 1) * 128, :]),
                    16)
                sld2.inc(q.dma_start(
                    out=itab[:, r:r + 1],
                    in_=io["cc_q_out"][r * 2048 + 1024 + pc * 128:
                                       r * 2048 + 1024 + (pc + 1) * 128, :]), 16)
        qt_ld = b.mark("ld2")

        @b.on("vector")
        def _(v, pc=pc, qt_ld=qt_ld):
            b.waits(v, qt_ld)
            Pr, Pi = pcol[:, 1:2], pcol[:, 2:3]
            qr, qi = pcol[:, 3:4], pcol[:, 4:5]
            t0c, t1c = pcol[:, 5:6], pcol[:, 6:7]
            b.chain(v, "v", v.memset(Pr, 1.0))
            b.chain(v, "v", v.memset(Pi, 0.0))
            for r in range(8):
                b.chain(v, "v", v.tensor_scalar(out=qr, in0=rtab[:, r:r + 1],
                                                scalar1=C["lt"][:, r:r + 1],
                                                scalar2=C["ltinv"][:, r:r + 1],
                                                op0=ALU.mult, op1=ALU.add))
                b.chain(v, "v", v.tensor_tensor(out=qi, in0=itab[:, r:r + 1],
                                                in1=C["lt"][:, r:r + 1],
                                                op=ALU.mult))
                b.chain(v, "v", v.tensor_tensor(out=t0c, in0=Pr, in1=qr,
                                                op=ALU.mult))
                b.chain(v, "v", v.tensor_tensor(out=t1c, in0=Pi, in1=qi,
                                                op=ALU.mult))
                b.chain(v, "v", v.tensor_tensor(out=t1c, in0=t0c, in1=t1c,
                                                op=ALU.subtract))
                b.chain(v, "v", v.tensor_tensor(out=t0c, in0=Pr, in1=qi,
                                                op=ALU.mult))
                b.chain(v, "v", v.tensor_tensor(out=Pi, in0=Pi, in1=qr,
                                                op=ALU.mult))
                b.chain(v, "v", v.tensor_tensor(out=Pi, in0=Pi, in1=t0c,
                                                op=ALU.add))
                b.chain(v, "v", v.tensor_copy(out=Pr, in_=t1c))
            v.tensor_scalar_mul(out=pong[0], in0=xv[pc][:, :], scalar1=Pr)
            v.tensor_scalar_mul(out=pong[1], in0=xv[8 + pc][:, :], scalar1=Pi)
            v.tensor_scalar_mul(out=mscr, in0=xv[8 + pc][:, :], scalar1=Pr)
            b.chain(v, "v", v.tensor_scalar_mul(out=tsc[0], in0=xv[pc][:, :],
                                                scalar1=Pi))
            v.tensor_tensor(out=kv[pc][:, :], in0=pong[0], in1=pong[1],
                            op=ALU.subtract)
            ins = v.tensor_tensor(out=kv[8 + pc][:, :], in0=mscr, in1=tsc[0],
                                  op=ALU.add)
            b.sems["v"].inc(ins)
        prev_q = b.mark("v")
        kvdone.append(prev_q)
    kv_done = b.mark("v")

    resid_h = [ar.f32r(M + i * 512, 512) for i in range(DC)]
    wv_free = NONE
    rr_free = NONE
    ln_done_prev = NONE
    for tt in range(2):
        n0 = tt * 512
        oc_marks = []
        for oc in range(DC):
            @b.on("sync")
            def _(q, tt=tt, oc=oc, ln_done_prev=ln_done_prev):
                if oc == 0:
                    if tt == 0:
                        b.waits(q, kv_done)
                    else:
                        b.waits(q, ln_done_prev)
                b.waits(q, wv_free)
                sld.inc(q.dma_start(out=wvalb[:, :], in_=io["wval_t"][oc, :, :]), 16)
                sld.inc(q.dma_start(out=wrecb[:, :], in_=io["wrec_t"][oc, :, :]), 16)
            wld = b.mark("ld")

            vbk = 2 + oc % 2
            rbk = 4 + oc % 2

            @b.on("tensor")
            def _(t, tt=tt, oc=oc, n0=n0, vbk=vbk, rbk=rbk, wld=wld,
                  ln_done_prev=ln_done_prev):
                b.waits(t, wld)
                if oc == 0 and tt == 0:
                    b.waits(t, kv_done)
                b.waits(t, b.bank_free[vbk], b.bank_free[rbk])
                for k in range(DC):
                    t.matmul(out=b.pool[vbk][:, :],
                             lhsT=wvalb[:, k * 128:(k + 1) * 128],
                             rhs=kv[k][:, n0:n0 + 512],
                             start=(k == 0), stop=(k == DC - 1))
                for k in range(32):
                    ins = t.matmul(out=b.pool[rbk][:, :],
                                   lhsT=wrecb[:, k * 128:(k + 1) * 128],
                                   rhs=ct_rhs(k, n0, n0 + 512),
                                   start=(k == 0), stop=(k == 31))
                b.sems["t"].inc(ins)
            mm = b.mark("t")
            wv_free = [mm]

            @b.on("vector")
            def _(v, rbk=rbk, mm=mm):
                b.waits(v, mm, rr_free)
                ins = v.tensor_scalar_max(out=rr, in0=b.pool[rbk][:, :],
                                          scalar1=0.0)
                b.sems["v"].inc(ins)
            relu_done = b.mark("v")
            b.bank_free[rbk] = [relu_done]

            @b.on("scalar")
            def _(s, relu_done=relu_done):
                b.waits(s, relu_done)
                ins = s.activation(out=rr, in_=rr, func=AF.Sigmoid)
                b.sems["s"].inc(ins)
            sig_done = b.mark("s")

            @b.on("vector")
            def _(v, vbk=vbk, sig_done=sig_done):
                b.waits(v, sig_done)
                v.tensor_tensor(out=r2, in0=rr, in1=rr, op=ALU.mult)
                b.chain(v, "v", v.tensor_tensor(out=r4, in0=r2, in1=r2,
                                                op=ALU.mult))
                ins = v.tensor_tensor(out=r2, in0=b.pool[vbk][:, :], in1=r4,
                                      op=ALU.mult)
                b.sems["v"].inc(ins)
            o_done = b.mark("v")
            rr_free = [o_done]
            b.bank_free[vbk] = [o_done]

            @b.on("gpsimd")
            def _(p, oc=oc, n0=n0, o_done=o_done):
                b.waits(p, o_done)
                ins = p.tensor_tensor(out=resid_h[oc][:, :], in0=r2,
                                      in1=x1e[oc][:, 1 + n0:513 + n0],
                                      op=ALU.add)
                b.sems["p"].inc(ins)
            rr_free = [b.mark("p")]
        resid_done = b.mark("p")

        lnout_free = [NONE, NONE]

        def pre_wait_fn(c, j, lnout_free=lnout_free):
            return lnout_free[c % 2]

        def post_fn(c, j, vmark, tt=tt, lnout_free=lnout_free):
            @b.on("gpsimd")
            def _(p, c=c, vmark=vmark, tt=tt):
                b.waits(p, vmark)
                n0 = tt * 512
                sst.inc(p.dma_start(
                    out=io["xfm3"][c * 128:(c + 1) * 128, n0:n0 + 512],
                    in_=lnout[c % 2].bitcast(F32)[:, 0:512]), 16)
            lnout_free[c % 2] = [("st", b.sems["st"].n)]

        out_aps = {c: lnout[c % 2][:, 0:512] for c in range(DC)}
        ln_done_prev = emit_ln(b, C, resid_h, ln_scr, [resid_done],
                               [0, 1, 2, 3, 4, 5], out_aps, pre_wait_fn,
                               post_fn, width=512)

    @b.on("gpsimd")
    def _(p):
        p.wait_ge(b.sems["st"].h, b.sems["st"].n)
    return io



def declare_l1ffn(nc, pfx="", xin=None, out_kind="ExternalOutput"):
    io = {}
    io["xin"] = xin if xin is not None else nc.dram_tensor(
        pfx + "xin", [D, T], F32R, kind="ExternalInput").ap()
    io["gq_t"] = nc.dram_tensor(pfx + "gq_t", [64, 128, 2048], I32, kind="ExternalInput").ap()
    io["uq_t"] = nc.dram_tensor(pfx + "uq_t", [64, 128, 2048], I32, kind="ExternalInput").ap()
    io["dq_t"] = nc.dram_tensor(pfx + "dq_t", [128, 128, 1024], I32, kind="ExternalInput").ap()
    io["gusc"] = nc.dram_tensor(pfx + "gusc", [8192, 2], F32, kind="ExternalInput").ap()
    io["dsc"] = nc.dram_tensor(pfx + "dsc", [2048, 1], F32, kind="ExternalInput").ap()
    io["xfm4"] = nc.dram_tensor(pfx + "xfm4", [D, T], F32, kind=out_kind).ap()
    return io


def host_l1ffn_weights(inp):
    gq = np.asarray(inp["L1_ffn_gate_w"], np.int32)
    uq = np.asarray(inp["L1_ffn_up_w"], np.int32)
    dq = np.asarray(inp["L1_ffn_down_w"], np.int32)

    def tile_i32(w):
        K, M = w.shape
        a = w.reshape(K // 128, 128, M // 128, 128).transpose(2, 1, 0, 3)
        return np.ascontiguousarray(a.reshape(M // 128, 128, K))

    gq_t = tile_i32(np.ascontiguousarray(gq.T))
    uq_t = tile_i32(np.ascontiguousarray(uq.T))
    dqT = np.ascontiguousarray(dq.T)
    a = dqT.reshape(8, 8, 128, 16, 128)
    a = a.transpose(0, 3, 2, 1, 4)
    dq_t = np.ascontiguousarray(a.reshape(128, 128, 1024))
    gusc = np.ascontiguousarray(np.stack(
        [np.asarray(inp["L1_ffn_gate_scale"], np.float32),
         np.asarray(inp["L1_ffn_up_scale"], np.float32)], axis=1))
    dsc = np.ascontiguousarray(
        np.asarray(inp["L1_ffn_down_scale"], np.float32).reshape(2048, 1))
    return dict(gq_t=gq_t, uq_t=uq_t, dq_t=dq_t, gusc=gusc, dsc=dsc)


def build_l1ffn(b, C, io, start_dep=()):
    ar = Arena(b, "ar4", 50400)
    xn = [ar.f32r(c * 1024, 1024) for c in range(DC)]
    dn_acc = [ar.f32(16384 + c * 1024, 1024) for c in range(DC)]
    H8 = 32768
    h8 = [ar.f32r(H8 + i * 1024, 1024) for i in range(8)]
    ln_scr = dict(sq=[ar.f32r(H8, 512), ar.f32r(H8 + 512, 512)],
                  mean=ar.f32r(H8 + 1024, 1024, p=1),
                  var=ar.f32r(H8 + 2048, 1024, p=1),
                  rstd=ar.f32r(H8 + 3072, 1024, p=1))
    lnout = [ar.f32r(H8 + 4096, 1024), ar.f32r(H8 + 5120, 1024)]
    WQ = 40960
    gqs = ar.t(WQ, [128, 2048], I32)
    uqs = ar.t(WQ + 2048, [128, 2048], I32)
    wg_fr = ar.f32r(WQ + 4096, 2048)
    wu_fr = ar.f32r(WQ + 6144, 2048)
    dqs = ar.t(WQ + 4096, [128, 1024], I32)
    wd_fr = ar.f32r(WQ + 5120, 1024)
    scl = ar.f32(49152, 144)
    gsb = ar.f32(49296, 512)
    usb = ar.f32(49808, 512)

    sld = b.sems["ld"]
    sld2 = b.sems["ld2"]
    sst = b.sems["st"]

    @b.on("sync")
    def _(q):
        b.waits(q, *C["ready"], *start_dep)
        for c in range(DC):
            sld.inc(q.dma_start(out=xn[c][:, :],
                                in_=io["xin"][c * 128:(c + 1) * 128, :]), 16)
        for ic in range(64):
            sld.inc(q.dma_start(out=scl[:, ic * 2:(ic + 1) * 2],
                                in_=io["gusc"][ic * 128:(ic + 1) * 128, :]), 16)
        for hc in range(DC):
            sld.inc(q.dma_start(out=scl[:, 128 + hc:129 + hc],
                                in_=io["dsc"][hc * 128:(hc + 1) * 128, :]), 16)
    xn_ld = b.mark("ld")

    gq_free = NONE
    wgfr_free = NONE
    wufr_free = NONE
    gsb_free = NONE
    usb_free = NONE
    h8_free = [NONE] * 8
    dq_free = NONE
    wdfr_free = NONE

    for icg in range(8):
        for i8 in range(8):
            ic = icg * 8 + i8

            @b.on("sync")
            def _(q, ic=ic):
                b.waits(q, gq_free)
                sld.inc(q.dma_start(out=gqs[:, :], in_=io["gq_t"][ic, :, :]), 16)
                sld.inc(q.dma_start(out=uqs[:, :], in_=io["uq_t"][ic, :, :]), 16)
            q_ld = b.mark("ld")

            @b.on("vector")
            def _(v, q_ld=q_ld):
                b.waits(v, q_ld, wgfr_free)
                ins = v.tensor_scalar_add(out=wg_fr, in0=gqs[:, :],
                                          scalar1=-128.0)
                b.sems["v"].inc(ins)
            wg_done = b.mark("v")

            @b.on("vector")
            def _(v, q_ld=q_ld):
                b.waits(v, q_ld, wufr_free)
                ins = v.tensor_scalar_add(out=wu_fr, in0=uqs[:, :],
                                          scalar1=-128.0)
                b.sems["v"].inc(ins)
            wu_done = b.mark("v")
            gq_free = [wg_done, wu_done]

            for tt in range(2):
                n0 = tt * 512
                gbk, ubk = tt, 2 + tt

                @b.on("tensor")
                def _(t, ic=ic, tt=tt, n0=n0, gbk=gbk, ubk=ubk,
                      wg_done=wg_done, wu_done=wu_done):
                    if tt == 0:
                        b.waits(t, wg_done, wu_done)
                        if ic == 0:
                            b.waits(t, xn_ld)
                    b.waits(t, b.bank_free[gbk], b.bank_free[ubk])
                    for k in range(DC):
                        t.matmul(out=b.pool[gbk][:, :],
                                 lhsT=wg_fr[:, k * 128:(k + 1) * 128],
                                 rhs=xn[k][:, n0:n0 + 512],
                                 start=(k == 0), stop=(k == DC - 1))
                    for k in range(DC):
                        ins = t.matmul(out=b.pool[ubk][:, :],
                                       lhsT=wu_fr[:, k * 128:(k + 1) * 128],
                                       rhs=xn[k][:, n0:n0 + 512],
                                       start=(k == 0), stop=(k == DC - 1))
                    b.sems["t"].inc(ins)
                mm = b.mark("t")
                wgfr_free = [mm]
                wufr_free = [mm]

                @b.on("scalar")
                def _(s, ic=ic, gbk=gbk, mm=mm):
                    b.waits(s, mm, gsb_free)
                    ins = s.activation(out=gsb, in_=b.pool[gbk][:, :],
                                       func=AF.Silu,
                                       scale=scl[:, ic * 2:ic * 2 + 1])
                    b.sems["s"].inc(ins)
                gs_done = b.mark("s")
                b.bank_free[gbk] = [gs_done]

                @b.on("vector")
                def _(v, ic=ic, ubk=ubk, mm=mm):
                    b.waits(v, mm, usb_free)
                    ins = v.tensor_scalar_mul(out=usb, in0=b.pool[ubk][:, :],
                                              scalar1=scl[:, ic * 2 + 1:ic * 2 + 2])
                    b.sems["v"].inc(ins)
                us_done = b.mark("v")
                b.bank_free[ubk] = [us_done]

                @b.on("gpsimd")
                def _(p, i8=i8, n0=n0, gs_done=gs_done, us_done=us_done):
                    b.waits(p, gs_done, us_done, h8_free[i8])
                    ins = p.tensor_tensor(out=h8[i8][:, n0:n0 + 512], in0=gsb,
                                          in1=usb, op=ALU.mult)
                    b.sems["p"].inc(ins)
                hm = b.mark("p")
                gsb_free = [hm]
                usb_free = [hm]
        h_ready = b.mark("p")

        for hc in range(DC):
            @b.on("sync")
            def _(q, icg=icg, hc=hc):
                b.waits(q, dq_free)
                if hc == 0:
                    b.waits(q, wgfr_free)
                sld2.inc(q.dma_start(out=dqs[:, :],
                                     in_=io["dq_t"][icg * 16 + hc, :, :]), 16)
            dq_ld = b.mark("ld2")

            @b.on("vector")
            def _(v, dq_ld=dq_ld):
                b.waits(v, dq_ld, wdfr_free)
                ins = v.tensor_scalar_add(out=wd_fr, in0=dqs[:, :],
                                          scalar1=-128.0)
                b.sems["v"].inc(ins)
            wd_done = b.mark("v")
            dq_free = [wd_done]

            for tt in range(2):
                n0 = tt * 512
                dbk = 4 + tt

                @b.on("tensor")
                def _(t, tt=tt, dbk=dbk, wd_done=wd_done, h_ready=h_ready):
                    if tt == 0:
                        b.waits(t, wd_done, h_ready)
                    b.waits(t, b.bank_free[dbk])
                    for k in range(8):
                        ins = t.matmul(out=b.pool[dbk][:, :],
                                       lhsT=wd_fr[:, k * 128:(k + 1) * 128],
                                       rhs=h8[k][:, n0:n0 + 512],
                                       start=(k == 0), stop=(k == 7))
                    b.sems["t"].inc(ins)
                dmm = b.mark("t")

                @b.on("vector")
                def _(v, icg=icg, hc=hc, tt=tt, n0=n0, dbk=dbk, dmm=dmm):
                    b.waits(v, dmm)
                    if icg == 0:
                        ins = v.tensor_scalar_mul(
                            out=dn_acc[hc][:, n0:n0 + 512], in0=b.pool[dbk][:, :],
                            scalar1=scl[:, 128 + hc:129 + hc])
                    else:
                        ins = v.scalar_tensor_tensor(
                            out=dn_acc[hc][:, n0:n0 + 512], in0=b.pool[dbk][:, :],
                            scalar=scl[:, 128 + hc:129 + hc],
                            in1=dn_acc[hc][:, n0:n0 + 512],
                            op0=ALU.mult, op1=ALU.add)
                    b.sems["v"].inc(ins)
                b.bank_free[dbk] = [b.mark("v")]
            wdfr_free = [b.mark("t")]
        down_done = b.mark("t")
        for i8 in range(8):
            h8_free[i8] = [down_done]
    dn_done = b.mark("v")

    @b.on("vector")
    def _(v):
        b.waits(v, dn_done)
        for c in range(DC):
            ins = v.tensor_tensor(out=xn[c][:, :], in0=xn[c][:, :],
                                  in1=dn_acc[c][:, :], op=ALU.add)
        b.sems["v"].inc(ins)
    resid_done = b.mark("v")

    lnout_free = [NONE, NONE]

    def pre_wait_fn(c, j):
        return lnout_free[c % 2]

    def post_fn(c, j, vmark):
        @b.on("gpsimd")
        def _(p, c=c, j=j, vmark=vmark):
            b.waits(p, vmark)
            n0, n1 = j * 512, (j + 1) * 512
            sst.inc(p.dma_start(out=io["xfm4"][c * 128:(c + 1) * 128, n0:n1],
                                in_=lnout[c % 2].bitcast(F32)[:, n0:n1]), 16)
        lnout_free[c % 2] = [("st", b.sems["st"].n)]

    out_aps = {c: lnout[c % 2] for c in range(DC)}
    emit_ln(b, C, xn, ln_scr, [resid_done], [0, 1, 2, 3, 4, 5],
            out_aps, pre_wait_fn, post_fn)

    @b.on("gpsimd")
    def _(p):
        p.wait_ge(b.sems["st"].h, b.sems["st"].n)
    return io



def build_fused():
    b = B("fused")
    nc = b.nc
    caps = declare_consts(nc)
    io1 = declare_l0attn(nc, pfx="a1_", out_kind="Internal")
    io2 = declare_l0ffn(nc, pfx="a2_", xin=fr(io1["xfm1"]), out_kind="Internal")
    io3 = declare_l1attn(nc, pfx="a3_", xin=fr(io2["xfm2"]), out_kind="Internal")
    io4 = declare_l1ffn(nc, pfx="a4_", xin=fr(io3["xfm3"]),
                        out_kind="ExternalOutput")
    C = load_consts(b, caps)

    def ends():
        return [b.mark(x) for x in ("t", "v", "s", "p", "st", "ld", "ld2", "cc")]

    build_l0attn(b, C, io1)
    build_l0ffn(b, C, io2, start_dep=ends())
    build_l1attn(b, C, io3, start_dep=ends())
    build_l1ffn(b, C, io4, start_dep=ends())
    b.close()
    return b


def host_fused_maps(inputs):
    w1 = host_l0attn_weights(inputs)
    w2 = host_l0ffn_weights(inputs)
    w3 = host_l1attn_weights(inputs)
    w4 = host_l1ffn_weights(inputs)
    maps = []
    for c in range(NCORES):
        m = {}
        for k, v in host_consts(c).items():
            m[k] = v
        for k, v in host_l0attn(inputs, c).items():
            m["a1_" + k] = v
        for k, v in w1.items():
            m["a1_" + k] = v
        for k, v in w2.items():
            m["a2_" + k] = v
        for k, v in w3.items():
            m["a3_" + k] = v
        for k, v in w4.items():
            m["a4_" + k] = v
        maps.append(m)
    return maps

_PROG = []


def kernel(**inputs):
    if not _PROG:
        _PROG.append(build_fused())
    b = _PROG[0]
    maps = host_fused_maps(inputs)
    res = run_bass_kernel_spmd(b.nc, maps, core_ids=list(range(NCORES)))
    out = np.concatenate([res.results[c]["a4_xfm4"].T for c in range(NCORES)],
                         axis=0)
    return np.ascontiguousarray(out.reshape(1, 8192, D).astype(np.float32))


# version 06 (final)
# speedup vs baseline: 1.7621x; 1.2558x over previous
import os
os.environ.setdefault("JAX_COMPILATION_CACHE_DIR", "/root/.jax_cache")
import contextlib
import numpy as np
import concourse.bass as bass
import concourse.mybir as mybir
from concourse.bass_utils import run_bass_kernel_spmd

F32 = mybir.dt.float32
F32R = mybir.dt.float32r
BF16 = mybir.dt.bfloat16
I32 = mybir.dt.int32
AF = mybir.ActivationFunctionType
ALU = mybir.AluOpType
AX = mybir.AxisListType

NCORES = 8
T = 1024
D = 2048
DC = D // 128
RG = [[0, 1, 2, 3, 4, 5, 6, 7]]
EPS = 1e-5
NONE = []


class Sem:
    def __init__(self, ctx, nc, name):
        self.h = ctx.enter_context(nc.semaphore(name))
        self.n = 0

    def inc(self, instr, k=1):
        instr.then_inc(self.h, k)
        self.n += k
        return self.n


class B:
    def __init__(self, name):
        self.nc = bass.Bass("TRN2", target_bir_lowering=False)
        self.ctx = contextlib.ExitStack()
        self.name = name
        self.block = self.ctx.enter_context(self.nc.Block())
        self.sems = {s: Sem(self.ctx, self.nc, "sem_" + s)
                     for s in ["ld", "ld2", "st", "t", "v", "s", "p", "cc"]}
        self.pool = [self.ctx.enter_context(
            self.nc.psum_tensor(f"bank{i}", [128, 512], F32)) for i in range(8)]
        self.bank_free = [NONE] * 8

    def sbuf(self, name, shape, dtype):
        return self.ctx.enter_context(self.nc.sbuf_tensor(name, shape, dtype))

    def on(self, eng):
        return getattr(self.block, eng)

    def waits(self, eng, *marks):
        flat = []
        for m in marks:
            if isinstance(m, list):
                flat.extend(m)
            elif m is not None:
                flat.append(m)
        best = {}
        for sem_name, cnt in flat:
            best[sem_name] = max(best.get(sem_name, 0), cnt)
        for sem_name, cnt in best.items():
            if cnt > 0:
                eng.wait_ge(self.sems[sem_name].h, cnt)

    def mark(self, sem_name):
        return (sem_name, self.sems[sem_name].n)

    def chain(self, eng, sem_name, ins):
        n = self.sems[sem_name].inc(ins)
        eng.wait_ge(self.sems[sem_name].h, n)
        return ins

    def close(self):
        self.ctx.close()


def fr(ap):
    return ap.bitcast(F32R)


class Arena:
    def __init__(self, b, name, cols):
        self.b = b
        self.name = name
        self.cols = cols
        self.n = 0
        used = b.nc.SBUF_PARTITION_SIZE_BYTES - b.nc.sbuf_bytes_remaining
        self.base = ((used + 31) // 32) * 32
        assert self.base + cols * 4 <= b.nc.SBUF_PARTITION_SIZE_BYTES, (
            self.base, cols)

    def t(self, col, shape, dtype):
        assert col % 8 == 0, col
        esz = 4 if dtype in (F32, F32R, I32) else 2
        ncols = (shape[-1] * esz + 3) // 4
        assert col + ncols <= self.cols, (col, shape, self.cols)
        self.n += 1
        h = self.b.nc.alloc_sbuf_tensor_at(
            f"{self.name}_{self.n}", list(shape), dtype,
            offset=self.base + col * 4)
        return h[tuple(slice(0, d) for d in shape)]

    def f32(self, col, n, p=128):
        return self.t(col, [p, n], F32)

    def f32r(self, col, n, p=128):
        return self.t(col, [p, n], F32R)



def host_tile_w_ocmajor(w, mt=128):
    K, M = w.shape
    assert K % 128 == 0 and M % mt == 0
    a = w.reshape(K // 128, 128, M // mt, mt)
    a = a.transpose(2, 1, 0, 3)
    return np.ascontiguousarray(a.reshape(M // mt, 128, (K // 128) * mt))


def host_consts(core):
    ident = np.eye(128, dtype=np.float32)
    ones_col = np.ones((128, 1), np.float32)
    ones_row = np.ones((1, 128), np.float32)
    lt = np.zeros((128, 8), np.float32)
    lt[:, :core] = 1.0
    prev = np.zeros((128, 8), np.float32)
    if core > 0:
        prev[:, core - 1] = 1.0
    return dict(c_ident=ident, c_ones_col=ones_col, c_ones_row=ones_row,
                c_lt=lt, c_ltinv=np.ascontiguousarray(1.0 - lt), c_prev=prev)


def declare_consts(nc):
    return dict(
        c_ident=nc.dram_tensor("c_ident", [128, 128], F32, kind="ExternalInput").ap(),
        c_ones_col=nc.dram_tensor("c_ones_col", [128, 1], F32R, kind="ExternalInput").ap(),
        c_ones_row=nc.dram_tensor("c_ones_row", [1, 128], F32R, kind="ExternalInput").ap(),
        c_lt=nc.dram_tensor("c_lt", [128, 8], F32, kind="ExternalInput").ap(),
        c_ltinv=nc.dram_tensor("c_ltinv", [128, 8], F32, kind="ExternalInput").ap(),
        c_prev=nc.dram_tensor("c_prev", [128, 8], F32, kind="ExternalInput").ap(),
    )


def load_consts(b, aps):
    C = {}
    C["ident"] = b.sbuf("cs_ident", [128, 128], F32)
    C["ones_col"] = b.sbuf("cs_ones_col", [128, 1], F32R)
    C["ones_row"] = b.sbuf("cs_ones_row", [1, 128], F32R)
    C["lt"] = b.sbuf("cs_lt", [128, 8], F32)
    C["ltinv"] = b.sbuf("cs_ltinv", [128, 8], F32)
    C["prev"] = b.sbuf("cs_prev", [128, 8], F32)
    ccols = b.sbuf("cs_cols", [128, 4], F32)

    @b.on("sync")
    def _(q):
        s = b.sems["ld"]
        for key, dst in [("c_ident", C["ident"]), ("c_ones_col", C["ones_col"]),
                         ("c_ones_row", C["ones_row"]), ("c_lt", C["lt"]),
                         ("c_ltinv", C["ltinv"]), ("c_prev", C["prev"])]:
            s.inc(q.dma_start(out=dst[:], in_=aps[key][:, :]), 16)

    @b.on("vector")
    def _(v):
        v.memset(ccols[:, 0:1], 1.0)
        v.memset(ccols[:, 1:2], -1.0)
        v.memset(ccols[:, 2:3], EPS)
        b.sems["v"].inc(v.memset(ccols[:, 3:4], 0.0))

    C["one"] = ccols[:, 0:1]
    C["negone"] = ccols[:, 1:2]
    C["eps"] = ccols[:, 2:3]
    C["zero"] = ccols[:, 3:4]
    C["ready"] = [b.mark("ld"), b.mark("v")]
    return C



def emit_ln(b, C, resid_tiles, scr, dep_marks, banks, out_aps,
            pre_wait_fn=None, post_fn=None, width=T):
    npc = (width + 511) // 512
    pieces = [(j * 512, min(width, (j + 1) * 512)) for j in range(npc)]
    sx = [b.pool[banks[0]][0:1, 0:512], b.pool[banks[1]][0:1, 0:512]]
    sqp = [b.pool[banks[2]][0:1, 0:512], b.pool[banks[3]][0:1, 0:512]]
    bcm, bcr = b.pool[banks[4]], b.pool[banks[5]]
    sq = scr["sq"]
    mean = scr["mean"][:, 0:width]
    var = scr["var"][:, 0:width]
    rstd = scr["rstd"][:, 0:width]

    sq_used = [NONE, NONE]
    for c in range(DC):
        for j, (n0, n1) in enumerate(pieces):
            @b.on("scalar")
            def _(s, c=c, j=j, n0=n0, n1=n1):
                if c == 0 and j == 0:
                    b.waits(s, *dep_marks)
                b.waits(s, sq_used[j])
                b.sems["s"].inc(s.activation(out=sq[j][:, 0:n1 - n0],
                                             in_=resid_tiles[c][:, n0:n1],
                                             func=AF.Square))
            sq_ready = b.mark("s")

            @b.on("tensor")
            def _(t, c=c, j=j, n0=n0, n1=n1, sq_ready=sq_ready):
                if c == 0 and j == 0:
                    b.waits(t, *dep_marks)
                    for bk in banks:
                        b.waits(t, b.bank_free[bk])
                b.waits(t, sq_ready)
                t.matmul(out=sx[j][:, 0:n1 - n0], lhsT=C["ones_col"][:, :],
                         rhs=resid_tiles[c][:, n0:n1], start=(c == 0),
                         stop=(c == DC - 1))
                ins = t.matmul(out=sqp[j][:, 0:n1 - n0], lhsT=C["ones_col"][:, :],
                               rhs=sq[j][:, 0:n1 - n0], start=(c == 0),
                               stop=(c == DC - 1))
                b.sems["t"].inc(ins)
            sq_used[j] = b.mark("t")

    mm_done = b.mark("t")

    @b.on("vector")
    def _(v):
        b.waits(v, mm_done)
        for j, (n0, n1) in enumerate(pieces):
            v.tensor_scalar_mul(out=mean[:, n0:n1], in0=sx[j][:, 0:n1 - n0],
                                scalar1=1.0 / D)
            v.tensor_scalar_mul(out=var[:, n0:n1], in0=sqp[j][:, 0:n1 - n0],
                                scalar1=1.0 / D)
        b.chain(v, "v", v.tensor_tensor(out=rstd, in0=mean, in1=mean,
                                        op=ALU.mult))
        b.sems["v"].inc(v.tensor_tensor(out=var, in0=var, in1=rstd,
                                        op=ALU.subtract))
    stats1 = b.mark("v")

    @b.on("scalar")
    def _(s):
        b.waits(s, stats1)
        b.sems["s"].inc(s.activation(out=rstd, in_=var, func=AF.Sqrt,
                                     bias=C["eps"][0:1, :]))
    stats2 = b.mark("s")

    @b.on("vector")
    def _(v):
        b.waits(v, stats2)
        with b.nc.allow_low_precision(reason="f32r LN stats"):
            b.sems["v"].inc(v.reciprocal(out=rstd, in_=rstd))
    stats3 = b.mark("v")

    for bk in banks[0:4]:
        b.bank_free[bk] = [stats1]

    prev_norm = None
    for j, (n0, n1) in enumerate(pieces):
        @b.on("tensor")
        def _(t, j=j, n0=n0, n1=n1, prev_norm=prev_norm):
            b.waits(t, stats3 if j == 0 else prev_norm)
            t.matmul(out=bcm[:, 0:n1 - n0], lhsT=C["ones_row"][:, :],
                     rhs=mean[:, n0:n1], start=True, stop=True)
            ins = t.matmul(out=bcr[:, 0:n1 - n0], lhsT=C["ones_row"][:, :],
                           rhs=rstd[:, n0:n1], start=True, stop=True)
            b.sems["t"].inc(ins)
        bc_done = b.mark("t")

        for c in range(DC):
            pw = pre_wait_fn(c, j) if pre_wait_fn else NONE

            @b.on("vector")
            def _(v, c=c, n0=n0, n1=n1, bc_done=bc_done, pw=pw):
                if c == 0:
                    b.waits(v, bc_done)
                b.waits(v, pw)
                v.tensor_tensor(out=out_aps[c][:, n0:n1],
                                in0=resid_tiles[c][:, n0:n1],
                                in1=bcm[:, 0:n1 - n0], op=ALU.subtract)
                ins = v.tensor_tensor(out=out_aps[c][:, n0:n1],
                                      in0=out_aps[c][:, n0:n1],
                                      in1=bcr[:, 0:n1 - n0], op=ALU.mult)
                b.sems["v"].inc(ins)
            if post_fn is not None:
                post_fn(c, j, b.mark("v"))
        prev_norm = b.mark("v")

    done = prev_norm
    for bk in banks[4:6]:
        b.bank_free[bk] = [done]
    return done



def declare_l0attn(nc, pfx="", out_kind="ExternalOutput"):
    io = {}
    io["x_loc"] = nc.dram_tensor(pfx + "x_loc", [T + 2, D], F32, kind="ExternalInput").ap()
    io["pcc"] = nc.dram_tensor(pfx + "pcc", [4096, 22], F32, kind="ExternalInput").ap()
    io["w_in_t"] = nc.dram_tensor(pfx + "w_in_t", [64, 128, DC * 128], F32R,
                                  kind="ExternalInput").ap()
    io["w_x_t"] = nc.dram_tensor(pfx + "w_x_t", [128, 32 * 160], F32R,
                                 kind="ExternalInput").ap()
    io["w_dt_t"] = nc.dram_tensor(pfx + "w_dt_t", [128, 4096], F32R,
                                  kind="ExternalInput").ap()
    io["w_out_t"] = nc.dram_tensor(pfx + "w_out_t", [64, 128, 8 * 128], F32R,
                                   kind="ExternalInput").ap()
    io["selE"] = nc.dram_tensor(pfx + "selE", [32, 4096], F32R,
                                kind="ExternalInput").ap()
    io["xfm1"] = nc.dram_tensor(pfx + "xfm1", [D, T], F32, kind=out_kind).ap()
    for nm, dt_ in [("xc_h", F32R), ("z_h", F32R), ("y_h", BF16), ("cds_h", F32),
                    ("yfin_h", F32R)]:
        io[nm] = nc.dram_tensor(pfx + nm, [4096, T], dt_, kind="Internal").ap()
    io["ccFD_in"] = nc.dram_tensor(pfx + "ccFD_in", [128, 1024], F32, kind="Internal").ap()
    io["ccFD_out"] = nc.dram_tensor(pfx + "ccFD_out", [1024, 1024], F32, kind="Internal").ap()
    return io


def host_l0attn(inp, core):
    x = np.asarray(inp["L0_attn_x"], np.float32).reshape(8192, D)
    t0 = core * T
    lo, hi = t0 - 1, t0 + T + 1
    xs = x[max(lo, 0):min(hi, 8192)]
    x_loc = np.pad(xs, ((max(0, -lo), max(0, hi - 8192)), (0, 0)))
    assert x_loc.shape == (T + 2, D)
    pcc = np.concatenate([
        np.asarray(inp["L0_attn_conv_w"], np.float32).reshape(4096, 3),
        np.asarray(inp["L0_attn_conv_b"], np.float32).reshape(4096, 1),
        np.asarray(inp["L0_attn_b_dt"], np.float32).reshape(4096, 1),
        np.asarray(inp["L0_attn_D"], np.float32).reshape(4096, 1),
        np.asarray(inp["L0_attn_A_log"], np.float32).reshape(4096, 16),
    ], axis=1)
    return dict(x_loc=np.ascontiguousarray(x_loc), pcc=np.ascontiguousarray(pcc))


def host_l0attn_weights(inp):
    w_in = np.asarray(inp["L0_attn_W_in"], np.float32)
    w_x = np.asarray(inp["L0_attn_W_x"], np.float32)
    w_dt = np.asarray(inp["L0_attn_W_dt"], np.float32)
    w_out = np.asarray(inp["L0_attn_W_out"], np.float32)
    w_in_t = host_tile_w_ocmajor(w_in)
    w_x_t = np.ascontiguousarray(
        w_x.reshape(32, 128, 160).transpose(1, 0, 2).reshape(128, 32 * 160))
    w_out_t = host_tile_w_ocmajor(w_out)
    w_out_t = np.ascontiguousarray(
        w_out_t.reshape(16, 128, 4, 1024).transpose(2, 0, 1, 3).reshape(64, 128, 1024))
    selE = np.zeros((32, 32 * 128), np.float32)
    for r in range(32):
        selE[r, r * 128:(r + 1) * 128] = 1.0
    return dict(w_in_t=w_in_t, w_x_t=w_x_t, w_dt_t=np.ascontiguousarray(w_dt),
                w_out_t=w_out_t, selE=selE)


def build_l0attn(b, C, io, start_dep=()):
    ar = Arena(b, "ar1", 47936)
    pcc_all = ar.f32(0, 704)
    A_all = ar.f32(704, 512)
    F_all = ar.f32(1216, 512)
    D_all = ar.f32(1728, 512)
    H_all = ar.f32(2240, 512)
    ones_t = ar.f32(2752, 1024)
    dtin_fr = ar.f32r(3776, 1024)
    bc_fr = ar.f32r(4800, 1024, p=32)
    LNS = 5824
    ln_scr = dict(sq=[ar.f32r(LNS, 512), ar.f32r(LNS + 512, 512)],
                  mean=ar.f32r(LNS + 1024, 1024, p=1),
                  var=ar.f32r(LNS + 2048, 1024, p=1),
                  rstd=ar.f32r(LNS + 3072, 1024, p=1))
    OA = LNS + 4096
    out_acc = [ar.f32r(OA + c * 1024, 1024) for c in range(DC)]
    RS = OA + 16384

    xfm = [ar.f32r(RS + c * 1032, 1026) for c in range(DC)]
    xrow = [ar.f32(OA + i * 2048, 2048) for i in range(2)]
    win_wb = [ar.f32r(OA + 4096 + i * 2048, 2048) for i in range(2)]
    xp_sb = ar.f32(OA + 8192, 1026)
    xc_st = ar.f32(OA + 9224, 1024)
    xc_frt = ar.f32r(OA + 10248, 1024)
    z_stg = ar.f32r(OA + 11272, 512)
    spB = ar.f32(OA + 9216, 1024)
    cds_j = ar.f32(OA + 10240, 1024)
    FD_r = ar.f32(RS + 0, 1024)
    Dm_s = ar.f32(RS + 1024, 512)
    selE_s = ar.f32r(42816, 4096, p=32)
    ybufs = [ar.f32r(RS + i * 1024, 1024) for i in range(8)]
    wout_wb = [ar.f32r(RS + 8192 + i * 4096, 4096) for i in range(2)]
    xrow2 = [ar.f32(RS + 10240 + i * 2048, 2048) for i in range(2)]
    lnout = [ar.f32r(RS + 14336, 1024), ar.f32r(RS + 15360, 1024)]

    sld = b.sems["ld"]
    sld2 = b.sems["ld2"]
    sst = b.sems["st"]

    wx_all = ar.f32r(42816, 5120)
    wdt_all = ar.f32r(OA + 11784, 4096)

    @b.on("sync")
    def _(q):
        b.waits(q, *C["ready"], *start_dep)
        sld.inc(q.dma_start(out=wx_all[:], in_=io["w_x_t"][:, :]), 16)
        sld.inc(q.dma_start(out=wdt_all[:], in_=io["w_dt_t"][:, :]), 16)
        for j in range(32):
            sld.inc(q.dma_start(out=pcc_all[:, j * 22:(j + 1) * 22],
                                in_=io["pcc"][j * 128:(j + 1) * 128, :]), 16)
    pcc_ld = b.mark("ld")

    @b.on("scalar")
    def _(s):
        b.waits(s, pcc_ld)
        for j in range(32):
            ins = s.activation(out=A_all[:, j * 16:(j + 1) * 16],
                               in_=pcc_all[:, j * 22 + 6:j * 22 + 22], func=AF.Exp)
        b.sems["s"].inc(ins)
    aexp = b.mark("s")

    @b.on("vector")
    def _(v):
        b.waits(v, aexp)
        v.tensor_scalar_mul(out=A_all, in0=A_all, scalar1=-1.0)
        b.sems["v"].inc(v.memset(ones_t, 1.0))
    a_ready = b.mark("v")

    row_sizes = [(i * 128, min(T + 2, (i + 1) * 128)) for i in range(9)]
    xrow_free = [NONE, NONE]
    for i, (r0, r1) in enumerate(row_sizes):
        nr = r1 - r0

        @b.on("sync")
        def _(q, i=i, r0=r0, r1=r1):
            b.waits(q, xrow_free[i % 2])
            if i == 0:
                b.waits(q, *C["ready"])
            sld.inc(q.dma_start(out=xrow[i % 2][0:r1 - r0, 0:D],
                                in_=io["x_loc"][r0:r1, :]), 16)
        xld = b.mark("ld")

        for c in range(DC):
            bk = 4 + (i * DC + c) % 4

            @b.on("tensor")
            def _(t, i=i, c=c, nr=nr, bk=bk, xld=xld):
                if c == 0:
                    b.waits(t, xld)
                b.waits(t, b.bank_free[bk])
                ins = t.transpose(out=b.pool[bk][0:128, 0:nr],
                                  in_=xrow[i % 2][0:nr, c * 128:(c + 1) * 128],
                                  identity=C["ident"][0:nr, 0:nr])
                b.sems["t"].inc(ins)
            tr_mark = b.mark("t")

            eng = "scalar" if c % 2 == 0 else "vector"

            @b.on(eng)
            def _(s, c=c, r0=r0, nr=nr, bk=bk, tr_mark=tr_mark, eng=eng):
                b.waits(s, tr_mark)
                if eng == "scalar":
                    ins = s.activation(out=xfm[c][:, r0:r0 + nr],
                                       in_=b.pool[bk][0:128, 0:nr], func=AF.Copy)
                else:
                    ins = s.tensor_copy(out=xfm[c][:, r0:r0 + nr],
                                        in_=b.pool[bk][0:128, 0:nr])
                b.sems["s" if eng == "scalar" else "v"].inc(ins)
            b.bank_free[bk] = [b.mark("s" if eng == "scalar" else "v")]
        xrow_free[i % 2] = [b.mark("t")]
    xfm_ready = [b.mark("s"), b.mark("v")]

    win_free = [NONE, NONE]
    xp_free = NONE
    xcst_free = NONE
    xcfr_free = NONE
    zstg_free = NONE
    xp_tts = [(0, 512), (512, 1024), (1024, 1026)]
    z_tts = [(1, 513), (513, 1025)]
    wx_last = None

    for oc in range(64):
        is_xp = oc < 32
        j = oc if is_xp else oc - 32
        tts = xp_tts if is_xp else z_tts

        @b.on("sync")
        def _(q, oc=oc):
            b.waits(q, win_free[oc % 2])
            if oc == 0:
                b.waits(q, xfm_ready)
            sld.inc(q.dma_start(out=win_wb[oc % 2][:], in_=io["w_in_t"][oc, :, :]), 16)
        wld = b.mark("ld")

        evac_last = None
        for ti, (n0, n1) in enumerate(tts):
            bk = 4 + (oc * 3 + ti) % 2

            @b.on("tensor")
            def _(t, oc=oc, ti=ti, n0=n0, n1=n1, bk=bk, wld=wld):
                if ti == 0:
                    b.waits(t, wld)
                    if oc == 0:
                        b.waits(t, xfm_ready)
                b.waits(t, b.bank_free[bk])
                for k in range(DC):
                    ins = t.matmul(out=b.pool[bk][:, 0:n1 - n0],
                                   lhsT=win_wb[oc % 2][:, k * 128:(k + 1) * 128],
                                   rhs=xfm[k][:, n0:n1],
                                   start=(k == 0), stop=(k == DC - 1))
                b.sems["t"].inc(ins)
            mm_mark = b.mark("t")

            if is_xp:
                @b.on("scalar")
                def _(s, n0=n0, n1=n1, bk=bk, mm_mark=mm_mark):
                    b.waits(s, mm_mark, xp_free)
                    ins = s.copy(out=xp_sb[:, n0:n1], in_=b.pool[bk][:, 0:n1 - n0])
                    b.sems["s"].inc(ins)
                b.bank_free[bk] = [b.mark("s")]
                evac_last = b.mark("s")
            else:
                @b.on("scalar")
                def _(s, n0=n0, n1=n1, bk=bk, mm_mark=mm_mark):
                    b.waits(s, mm_mark, zstg_free)
                    ins = s.activation(out=z_stg[:, 0:n1 - n0],
                                       in_=b.pool[bk][:, 0:n1 - n0], func=AF.Silu)
                    b.sems["s"].inc(ins)
                b.bank_free[bk] = [b.mark("s")]
                zmk = b.mark("s")

                @b.on("gpsimd")
                def _(p, j=j, n0=n0, n1=n1, zmk=zmk):
                    b.waits(p, zmk)
                    sst.inc(p.dma_start(
                        out=io["z_h"][j * 128:(j + 1) * 128, n0 - 1:n1 - 1],
                        in_=z_stg[:, 0:n1 - n0]), 16)
                zstg_free = [("st", b.sems["st"].n)]

        if is_xp:
            @b.on("vector")
            def _(v, j=j, evac_last=evac_last):
                b.waits(v, evac_last, xcst_free)
                if j == 0:
                    b.waits(v, a_ready)
                cw = pcc_all[:, j * 22:j * 22 + 3]
                v.tensor_scalar_mul(out=xc_st, in0=xp_sb[:, 0:1024],
                                    scalar1=cw[:, 0:1])
                v.scalar_tensor_tensor(out=xc_st, in0=xp_sb[:, 1:1025],
                                       scalar=cw[:, 1:2], in1=xc_st,
                                       op0=ALU.mult, op1=ALU.add)
                ins = v.scalar_tensor_tensor(out=xc_st, in0=xp_sb[:, 2:1026],
                                             scalar=cw[:, 2:3], in1=xc_st,
                                             op0=ALU.mult, op1=ALU.add)
                b.sems["v"].inc(ins)
            xp_free = [b.mark("v")]
            conv_done = b.mark("v")

            @b.on("scalar")
            def _(s, j=j, conv_done=conv_done):
                b.waits(s, conv_done, xcfr_free)
                ins = s.activation(out=xc_frt, in_=xc_st, func=AF.Silu,
                                   bias=pcc_all[:, j * 22 + 3:j * 22 + 4])
                b.sems["s"].inc(ins)
            xcst_free = [b.mark("s")]
            silu_done = b.mark("s")

            @b.on("tensor")
            def _(t, j=j, silu_done=silu_done):
                b.waits(t, silu_done)
                for tt in range(2):
                    m0 = tt * 512
                    t.matmul(out=b.pool[tt][:, :],
                             lhsT=wx_all[:, j * 160:j * 160 + 128],
                             rhs=xc_frt[:, m0:m0 + 512],
                             start=(j == 0), stop=(j == 31))
                    ins = t.matmul(out=b.pool[2 + tt][0:32, :],
                                   lhsT=wx_all[:, j * 160 + 128:j * 160 + 160],
                                   rhs=xc_frt[:, m0:m0 + 512],
                                   start=(j == 0), stop=(j == 31))
                b.sems["t"].inc(ins)
            wx_mark = b.mark("t")
            wx_last = wx_mark

            @b.on("gpsimd")
            def _(p, j=j, silu_done=silu_done):
                b.waits(p, silu_done)
                sst.inc(p.dma_start(out=io["xc_h"][j * 128:(j + 1) * 128, :],
                                    in_=xc_frt[:, :]), 16)
            xcfr_free = [wx_mark, ("st", b.sems["st"].n)]

        win_free[oc % 2] = [b.mark("t")]

    xc_all_stored = b.mark("st")
    wx_all_done = wx_last
    win_all_done = b.mark("t")

    @b.on("scalar")
    def _(s):
        b.waits(s, wx_all_done)
        for tt in range(2):
            s.activation(out=dtin_fr[:, tt * 512:(tt + 1) * 512],
                         in_=b.pool[tt][:, :], func=AF.Copy)
            ins = s.activation(out=bc_fr[:, tt * 512:(tt + 1) * 512],
                               in_=b.pool[2 + tt][0:32, :], func=AF.Copy)
        b.sems["s"].inc(ins)
    xdbl_done = b.mark("s")
    for bk in range(4):
        b.bank_free[bk] = [xdbl_done]

    @b.on("sync")
    def _(q):
        b.waits(q, win_all_done)
        sld.inc(q.dma_start(out=selE_s[:, :], in_=io["selE"][:, :]), 16)
    selE_ld = b.mark("ld")

    dt8 = [ar.f32(RS + jj * 1024, 1024) for jj in range(8)]
    v8b = [ar.t(RS + 8192 + jj * 512, [128, 1024], BF16) for jj in range(8)]
    y8b = [ar.t(RS + 12288 + jj * 512, [128, 1024], BF16) for jj in range(8)]
    dA2 = [ar.f32(OA + i * 1024, 1024) for i in range(2)]
    w2b = [ar.t(OA + 2048 + i * 512, [128, 1024], BF16) for i in range(2)]
    h2b = [ar.t(OA + 3072 + i * 512, [128, 1024], BF16) for i in range(2)]
    t2b = [ar.t(OA + 4096 + i * 512, [128, 1024], BF16) for i in range(2)]
    B2b = [ar.t(OA + 5120 + i * 512, [128, 1024], BF16) for i in range(2)]
    C2b = [ar.t(OA + 6144 + i * 512, [128, 1024], BF16) for i in range(2)]
    xcl2 = [ar.f32r(OA + 7168 + i * 1024, 1024) for i in range(2)]
    zl2 = [ar.f32r(OA + 2048, 1024), ar.f32r(OA + 5120, 1024)]
    yfin32 = ar.f32(OA + 3072, 1024)
    cds8 = dt8

    brow_free = [NONE, NONE]
    crow_free = [NONE, NONE]
    xcl2_free = [NONE, NONE]
    spB_free = NONE
    cdsj_free = NONE
    dA_free = [NONE, NONE]
    w_free = [NONE, NONE]
    h_free = [NONE, NONE]
    t_free = [NONE, NONE]
    Bb_free = [NONE, NONE]
    Cb_free = [NONE, NONE]
    dt8_free = [NONE] * 8
    v8_free = [NONE] * 8
    y8_free = [NONE] * 8

    ui = 0
    for jb in range(4):
        for jj in range(8):
            j = jb * 8 + jj

            @b.on("tensor")
            def _(t, j=j, jj=jj):
                if j == 0:
                    b.waits(t, xdbl_done)
                for tt in range(2):
                    b.waits(t, b.bank_free[tt])
                    ins = t.matmul(out=b.pool[tt][:, :],
                                   lhsT=wdt_all[:, j * 128:(j + 1) * 128],
                                   rhs=dtin_fr[:, tt * 512:(tt + 1) * 512],
                                   start=True, stop=True)
                b.sems["t"].inc(ins)
            dtp_mark = b.mark("t")

            bcol = pcc_all[:, j * 22 + 4:j * 22 + 5]

            @b.on("vector")
            def _(v, j=j, jj=jj, dtp_mark=dtp_mark, bcol=bcol):
                if j == 0:
                    b.waits(v, win_all_done, xc_all_stored)
                b.waits(v, dtp_mark, dt8_free[jj])
                for tt in range(2):
                    n0 = tt * 512
                    ins = v.tensor_scalar(out=dt8[jj][:, n0:n0 + 512],
                                          in0=b.pool[tt][:, :], scalar1=bcol,
                                          scalar2=0.0, op0=ALU.add, op1=ALU.max)
                b.sems["v"].inc(ins)
            m_done = b.mark("v")

            @b.on("scalar")
            def _(s, dtp_mark=dtp_mark, bcol=bcol):
                b.waits(s, dtp_mark, spB_free)
                for tt in range(2):
                    n0 = tt * 512
                    s.activation(out=spB[:, n0:n0 + 512], in_=b.pool[tt][:, :],
                                 func=AF.Abs, bias=bcol)
                s.activation(out=spB, in_=spB, func=AF.Exp, scale=C["negone"])
                ins = s.activation(out=spB, in_=spB, func=AF.Ln, bias=C["one"])
                b.sems["s"].inc(ins)
            sp_done = b.mark("s")
            b.bank_free[0] = [m_done, sp_done]
            b.bank_free[1] = [m_done, sp_done]

            @b.on("vector")
            def _(v, jj=jj, sp_done=sp_done):
                b.waits(v, sp_done)
                v.tensor_tensor(out=dt8[jj], in0=dt8[jj], in1=spB, op=ALU.add)
                b.waits(v, cdsj_free)
                ins = v.tensor_tensor_scan(out=cds_j, data0=ones_t, data1=dt8[jj],
                                           initial=0.0, op0=ALU.mult, op1=ALU.add)
                b.sems["v"].inc(ins)
            spB_free = [b.mark("v")]
            cds_done = b.mark("v")

            @b.on("sync")
            def _(q, j=j, jj=jj):
                if j == 0:
                    b.waits(q, xc_all_stored, win_all_done)
                b.waits(q, xcl2_free[jj % 2])
                sld2.inc(q.dma_start(out=xcl2[jj % 2][:, :],
                                     in_=io["xc_h"][j * 128:(j + 1) * 128, :]), 16)
            xc_ld = b.mark("ld2")

            @b.on("vector")
            def _(v, jj=jj, xc_ld=xc_ld):
                b.waits(v, xc_ld, v8_free[jj])
                ins = v.tensor_tensor(out=v8b[jj][:, :], in0=dt8[jj],
                                      in1=xcl2[jj % 2], op=ALU.mult)
                b.sems["v"].inc(ins)
            vj_done = b.mark("v")
            xcl2_free[jj % 2] = [vj_done]

            @b.on("gpsimd")
            def _(p, j=j, jj=jj, vj_done=vj_done):
                if j == 0:
                    b.waits(p, win_all_done)
                b.waits(p, y8_free[jj])
                ins = p.memset(y8b[jj][:, :], 0.0)
                b.sems["p"].inc(ins)
            y8_free[jj] = NONE

            @b.on("gpsimd")
            def _(p, j=j, cds_done=cds_done):
                b.waits(p, cds_done)
                sst.inc(p.dma_start(out=io["cds_h"][j * 128:(j + 1) * 128, :],
                                    in_=cds_j[:, :]), 16)
            cds_st = ("st", b.sems["st"].n)

            @b.on("scalar")
            def _(s, j=j, cds_done=cds_done):
                b.waits(s, cds_done)
                ins = s.activation(out=D_all[:, j * 16:(j + 1) * 16],
                                   in_=A_all[:, j * 16:(j + 1) * 16], func=AF.Exp,
                                   scale=cds_j[:, 1023:1024])
                b.sems["s"].inc(ins)
            cdsj_free = [cds_st, b.mark("s")]
        blk_pre = [b.mark("v"), b.mark("s"), b.mark("p")]

        for si in range(16):
            pr = si % 2

            @b.on("tensor")
            def _(t, si=si):
                if si == 0:
                    b.waits(t, selE_ld, xdbl_done)
                for tt in range(2):
                    n0 = tt * 512
                    b.waits(t, b.bank_free[2 + tt])
                    t.matmul(out=b.pool[2 + tt][:, :],
                             lhsT=selE_s[:, si * 128:(si + 1) * 128],
                             rhs=bc_fr[0:32, n0:n0 + 512], start=True, stop=True)
                    b.waits(t, b.bank_free[4 + tt])
                    ins = t.matmul(out=b.pool[4 + tt][:, :],
                                   lhsT=selE_s[:, (16 + si) * 128:(17 + si) * 128],
                                   rhs=bc_fr[0:32, n0:n0 + 512], start=True,
                                   stop=True)
                b.sems["t"].inc(ins)
            bc_mm = b.mark("t")

            @b.on("scalar")
            def _(s, pr=pr, bc_mm=bc_mm):
                b.waits(s, bc_mm, Bb_free[pr], Cb_free[pr])
                for tt in range(2):
                    n0 = tt * 512
                    s.activation(out=B2b[pr][:, n0:n0 + 512],
                                 in_=b.pool[2 + tt][:, :], func=AF.Copy)
                    ins = s.activation(out=C2b[pr][:, n0:n0 + 512],
                                       in_=b.pool[4 + tt][:, :], func=AF.Copy)
                b.sems["s"].inc(ins)
            bc_evac = b.mark("s")
            for bk in [2, 3, 4, 5]:
                b.bank_free[bk] = [bc_evac]

            for jj in range(8):
                j = jb * 8 + jj
                up = ui % 2
                ui += 1

                @b.on("scalar")
                def _(s, j=j, jj=jj, si=si, up=up, blk_pre=blk_pre):
                    if si == 0 and jj == 0:
                        b.waits(s, *blk_pre)
                    b.waits(s, dA_free[up])
                    ins = s.activation(out=dA2[up], in_=dt8[jj], func=AF.Exp,
                                       scale=A_all[:, j * 16 + si:j * 16 + si + 1])
                    b.sems["s"].inc(ins)
                dA_done = b.mark("s")

                @b.on("vector")
                def _(v, jj=jj, si=si, pr=pr, up=up, dA_done=dA_done,
                      bc_evac=bc_evac, j=j):
                    b.waits(v, dA_done, bc_evac, w_free[up])
                    v.tensor_tensor(out=w2b[up][:, :], in0=v8b[jj][:, :],
                                    in1=B2b[pr][:, :], op=ALU.mult)
                    b.waits(v, h_free[up])
                    b.chain(v, "v", v.tensor_tensor_scan(
                        out=h2b[up][:, :], data0=dA2[up], data1=w2b[up][:, :],
                        initial=0.0, op0=ALU.mult, op1=ALU.add))
                    b.waits(v, t_free[up])
                    b.chain(v, "v", v.tensor_copy(
                        out=F_all[:, j * 16 + si:j * 16 + si + 1],
                        in_=h2b[up][:, 1023:1024]))
                    ins = v.tensor_tensor(out=t2b[up][:, :], in0=h2b[up][:, :],
                                          in1=C2b[pr][:, :], op=ALU.mult)
                    b.sems["v"].inc(ins)
                unit_v = b.mark("v")
                dA_free[up] = [unit_v]
                w_free[up] = [unit_v]
                h_free[up] = [unit_v]
                if jj == 7:
                    Bb_free[pr] = [unit_v]
                    Cb_free[pr] = [unit_v]

                @b.on("gpsimd")
                def _(p, jj=jj, up=up, unit_v=unit_v):
                    b.waits(p, unit_v)
                    ins = p.tensor_tensor(out=y8b[jj][:, :], in0=y8b[jj][:, :],
                                          in1=t2b[up][:, :], op=ALU.add)
                    b.sems["p"].inc(ins)
                t_free[up] = [b.mark("p")]
        blk_scan = b.mark("p")

        @b.on("gpsimd")
        def _(p, jb=jb):
            for jj in range(8):
                j = jb * 8 + jj
                sst.inc(p.dma_start(out=io["y_h"][j * 128:(j + 1) * 128, :],
                                    in_=y8b[jj][:, :]), 16)
        ymk = ("st", b.sems["st"].n)
        for jj in range(8):
            y8_free[jj] = [ymk]
            dt8_free[jj] = [b.mark("s")]
            v8_free[jj] = [b.mark("v")]
    scan1_done = [b.mark("p"), b.mark("s"), b.mark("st"), b.mark("v"),
                  b.mark("t"), b.mark("ld2")]

    @b.on("gpsimd")
    def _(p):
        b.waits(p, *scan1_done)
        sst.inc(p.dma_start(out=io["ccFD_in"][:, 0:512], in_=F_all[:, :]), 16)
        sst.inc(p.dma_start(out=io["ccFD_in"][:, 512:1024], in_=D_all[:, :]), 16)
        p.wait_ge(b.sems["st"].h, b.sems["st"].n)
        b.sems["cc"].inc(p.collective_compute(
            "AllGather", ALU.bypass, replica_groups=RG,
            ins=[io["ccFD_in"].opt()], outs=[io["ccFD_out"].opt()]))
    cc_done = b.mark("cc")

    @b.on("vector")
    def _(v):
        b.waits(v, cc_done)
        b.sems["v"].inc(v.memset(H_all, 0.0))
    fd_prev = b.mark("v")
    for r in range(8):
        @b.on("sync")
        def _(q, r=r, fd_prev=fd_prev):
            b.waits(q, fd_prev)
            sld2.inc(q.dma_start(out=FD_r[:, :],
                                 in_=io["ccFD_out"][r * 128:(r + 1) * 128, :]), 16)
        fd_ld = b.mark("ld2")

        @b.on("vector")
        def _(v, r=r, fd_ld=fd_ld):
            b.waits(v, fd_ld)
            b.chain(v, "v", v.tensor_scalar(out=Dm_s, in0=FD_r[:, 512:1024],
                                            scalar1=C["lt"][:, r:r + 1],
                                            scalar2=C["ltinv"][:, r:r + 1],
                                            op0=ALU.mult, op1=ALU.add))
            b.chain(v, "v", v.tensor_tensor(out=H_all, in0=H_all, in1=Dm_s,
                                            op=ALU.mult))
            b.chain(v, "v", v.tensor_scalar_mul(out=Dm_s, in0=FD_r[:, 0:512],
                                                scalar1=C["lt"][:, r:r + 1]))
            b.chain(v, "v", v.tensor_tensor(out=H_all, in0=H_all, in1=Dm_s,
                                            op=ALU.add))
        fd_prev = b.mark("v")
    H_done = fd_prev

    e2 = dA2
    t92 = t2b
    e_free = dA_free
    t9_free = t_free
    C2_free = Cb_free
    cds8_free = [NONE] * 8
    y8p2_free = [NONE] * 8
    zl2_free = [NONE, NONE]
    yfin32_free = NONE

    for jb in range(4):
        for jj in range(8):
            j = jb * 8 + jj

            @b.on("sync")
            def _(q, j=j, jj=jj):
                if j == 0:
                    b.waits(q, H_done, *scan1_done)
                b.waits(q, cds8_free[jj])
                sld2.inc(q.dma_start(out=cds8[jj][:, :],
                                     in_=io["cds_h"][j * 128:(j + 1) * 128, :]), 16)

            @b.on("gpsimd")
            def _(p, j=j, jj=jj):
                if j == 0:
                    b.waits(p, H_done)
                b.waits(p, y8p2_free[jj])
                sst.inc(p.dma_start(out=y8b[jj][:, :],
                                    in_=io["y_h"][j * 128:(j + 1) * 128, :]), 16)
        ph2_ld = [b.mark("ld2"), b.mark("st")]

        for si in range(16):
            pr = si % 2

            @b.on("tensor")
            def _(t, si=si):
                for tt in range(2):
                    n0 = tt * 512
                    b.waits(t, b.bank_free[2 + tt])
                    ins = t.matmul(out=b.pool[2 + tt][:, :],
                                   lhsT=selE_s[:, (16 + si) * 128:(17 + si) * 128],
                                   rhs=bc_fr[0:32, n0:n0 + 512],
                                   start=True, stop=True)
                b.sems["t"].inc(ins)
            cmm = b.mark("t")

            @b.on("scalar")
            def _(s, pr=pr, cmm=cmm):
                b.waits(s, cmm, C2_free[pr])
                for tt in range(2):
                    n0 = tt * 512
                    ins = s.activation(out=C2b[pr][:, n0:n0 + 512],
                                       in_=b.pool[2 + tt][:, :], func=AF.Copy)
                b.sems["s"].inc(ins)
            cevac = b.mark("s")
            b.bank_free[2] = [cevac]
            b.bank_free[3] = [cevac]

            for jj in range(8):
                j = jb * 8 + jj
                up = ui % 2
                ui += 1

                @b.on("scalar")
                def _(s, j=j, jj=jj, si=si, up=up, ph2_ld=ph2_ld):
                    if si == 0 and jj == 0:
                        b.waits(s, *ph2_ld)
                    b.waits(s, e_free[up])
                    ins = s.activation(out=e2[up], in_=cds8[jj], func=AF.Exp,
                                       scale=A_all[:, j * 16 + si:j * 16 + si + 1])
                    b.sems["s"].inc(ins)
                e_done = b.mark("s")

                @b.on("vector")
                def _(v, j=j, si=si, pr=pr, up=up, e_done=e_done, cevac=cevac):
                    b.waits(v, e_done, cevac, t9_free[up])
                    ins = v.scalar_tensor_tensor(
                        out=t92[up][:, :], in0=e2[up],
                        scalar=H_all[:, j * 16 + si:j * 16 + si + 1],
                        in1=C2b[pr][:, :], op0=ALU.mult, op1=ALU.mult)
                    b.sems["v"].inc(ins)
                t9_done = b.mark("v")
                e_free[up] = [t9_done]
                if jj == 7:
                    C2_free[pr] = [t9_done]

                @b.on("gpsimd")
                def _(p, jj=jj, up=up, t9_done=t9_done):
                    b.waits(p, t9_done)
                    ins = p.tensor_tensor(out=y8b[jj][:, :], in0=y8b[jj][:, :],
                                          in1=t92[up][:, :], op=ALU.add)
                    b.sems["p"].inc(ins)
                t9_free[up] = [b.mark("p")]
        corr_blk = b.mark("p")
        sblk = b.mark("s")
        for jj in range(8):
            cds8_free[jj] = [sblk]

        for jj in range(8):
            j = jb * 8 + jj

            @b.on("sync")
            def _(q, j=j, jj=jj):
                b.waits(q, xcl2_free[jj % 2], zl2_free[jj % 2])
                sld2.inc(q.dma_start(out=xcl2[jj % 2][:, :],
                                     in_=io["xc_h"][j * 128:(j + 1) * 128, :]), 16)
                sld2.inc(q.dma_start(out=zl2[jj % 2][:, :],
                                     in_=io["z_h"][j * 128:(j + 1) * 128, :]), 16)
            fin_ld = b.mark("ld2")

            @b.on("vector")
            def _(v, j=j, jj=jj, fin_ld=fin_ld, corr_blk=corr_blk):
                b.waits(v, fin_ld, yfin32_free)
                if jj == 0:
                    b.waits(v, corr_blk)
                v.scalar_tensor_tensor(out=yfin32, in0=xcl2[jj % 2],
                                       scalar=pcc_all[:, j * 22 + 5:j * 22 + 6],
                                       in1=y8b[jj][:, :], op0=ALU.mult,
                                       op1=ALU.add)
                ins = v.tensor_tensor(out=fr(yfin32), in0=yfin32,
                                      in1=zl2[jj % 2], op=ALU.mult)
                b.sems["v"].inc(ins)
            yfin_done = b.mark("v")
            xcl2_free[jj % 2] = [yfin_done]
            zl2_free[jj % 2] = [yfin_done]
            y8p2_free[jj] = [yfin_done]

            @b.on("gpsimd")
            def _(p, j=j, yfin_done=yfin_done):
                b.waits(p, yfin_done)
                sst.inc(p.dma_start(out=io["yfin_h"][j * 128:(j + 1) * 128, :],
                                    in_=fr(yfin32)[:, :]), 16)
            yfin32_free = [("st", b.sems["st"].n)]
    yfin_all = b.mark("st")
    ph2_all = [b.mark("p"), b.mark("s"), b.mark("v"), b.mark("t"), yfin_all,
               b.mark("ld2")]


    ybuf_free = [ph2_all] * 8
    wout_free = [ph2_all, ph2_all]
    oc_evac = [NONE, NONE]
    for grp in range(4):
        @b.on("sync")
        def _(q, grp=grp):
            for k in range(8):
                b.waits(q, ybuf_free[k])
                kk = grp * 8 + k
                sld2.inc(q.dma_start(
                    out=ybufs[k][:, :],
                    in_=io["yfin_h"][kk * 128:(kk + 1) * 128, :]), 16)
        y_ld = b.mark("ld2")

        for oc in range(16):
            @b.on("sync")
            def _(q, grp=grp, oc=oc):
                b.waits(q, wout_free[oc % 2])
                sld.inc(q.dma_start(out=wout_wb[oc % 2][:, 0:1024],
                                    in_=io["w_out_t"][grp * 16 + oc, :, :]), 16)
            wo_ld = b.mark("ld")

            for tt in range(2):
                bk = 6 + tt
                n0 = tt * 512

                @b.on("tensor")
                def _(t, grp=grp, oc=oc, tt=tt, bk=bk, n0=n0, y_ld=y_ld,
                      wo_ld=wo_ld):
                    if tt == 0:
                        b.waits(t, wo_ld)
                        if oc == 0:
                            b.waits(t, y_ld)
                    b.waits(t, b.bank_free[bk])
                    for k in range(8):
                        ins = t.matmul(out=b.pool[bk][:, :],
                                       lhsT=wout_wb[oc % 2][:, k * 128:(k + 1) * 128],
                                       rhs=ybufs[k][:, n0:n0 + 512],
                                       start=(k == 0), stop=(k == 7))
                    b.sems["t"].inc(ins)
                mm = b.mark("t")

                @b.on("vector")
                def _(v, grp=grp, oc=oc, n0=n0, bk=bk, mm=mm):
                    b.waits(v, mm)
                    if grp == 0:
                        ins = v.tensor_copy(out=out_acc[oc][:, n0:n0 + 512],
                                            in_=b.pool[bk][:, :])
                    else:
                        ins = v.tensor_tensor(out=out_acc[oc][:, n0:n0 + 512],
                                              in0=out_acc[oc][:, n0:n0 + 512],
                                              in1=b.pool[bk][:, :], op=ALU.add)
                    b.sems["v"].inc(ins)
                b.bank_free[bk] = [b.mark("v")]
            wout_free[oc % 2] = [b.mark("t")]
        gm = b.mark("t")
        for k in range(8):
            ybuf_free[k] = [gm]
    wout_done = b.mark("v")

    xrow2_free = [NONE, NONE]
    for i in range(8):
        @b.on("sync")
        def _(q, i=i):
            if i < 2:
                b.waits(q, wout_done)
            b.waits(q, xrow2_free[i % 2])
            sld.inc(q.dma_start(out=xrow2[i % 2][0:128, 0:D],
                                in_=io["x_loc"][1 + i * 128:1 + (i + 1) * 128, :]),
                    16)
        x2_ld = b.mark("ld")

        for c in range(DC):
            bk = (i * DC + c) % 2

            @b.on("tensor")
            def _(t, i=i, c=c, bk=bk, x2_ld=x2_ld):
                if c == 0:
                    b.waits(t, x2_ld)
                b.waits(t, b.bank_free[bk])
                if i == 0 and c == 0:
                    b.waits(t, wout_done)
                ins = t.transpose(out=b.pool[bk][0:128, 0:128],
                                  in_=xrow2[i % 2][0:128, c * 128:(c + 1) * 128],
                                  identity=C["ident"][:, :])
                b.sems["t"].inc(ins)
            tr_mark = b.mark("t")

            @b.on("vector")
            def _(v, i=i, c=c, bk=bk, tr_mark=tr_mark):
                b.waits(v, tr_mark)
                ins = v.tensor_tensor(out=out_acc[c][:, i * 128:(i + 1) * 128],
                                      in0=out_acc[c][:, i * 128:(i + 1) * 128],
                                      in1=b.pool[bk][0:128, 0:128], op=ALU.add)
                b.sems["v"].inc(ins)
            b.bank_free[bk] = [b.mark("v")]
        xrow2_free[i % 2] = [b.mark("t")]
    resid_done = b.mark("v")

    resid_tiles = out_acc
    lnout_free = [NONE, NONE]

    def pre_wait_fn(c, j):
        return lnout_free[c % 2]

    def post_fn(c, j, vmark):
        @b.on("gpsimd")
        def _(p, c=c, j=j, vmark=vmark):
            b.waits(p, vmark)
            n0, n1 = j * 512, (j + 1) * 512
            sst.inc(p.dma_start(out=io["xfm1"][c * 128:(c + 1) * 128, n0:n1],
                                in_=lnout[c % 2].bitcast(F32)[:, n0:n1]), 16)
        lnout_free[c % 2] = [("st", b.sems["st"].n)]

    out_aps = {c: lnout[c % 2] for c in range(DC)}
    emit_ln(b, C, resid_tiles, ln_scr, [resid_done], [0, 1, 2, 3, 4, 5],
            out_aps, pre_wait_fn, post_fn)

    @b.on("gpsimd")
    def _(p):
        p.wait_ge(b.sems["st"].h, b.sems["st"].n)
    return io



def declare_l0ffn(nc, pfx="", xin=None, out_kind="ExternalOutput"):
    io = {}
    io["xin"] = xin if xin is not None else nc.dram_tensor(
        pfx + "xin", [D, T], F32R, kind="ExternalInput").ap()
    io["gate_t"] = nc.dram_tensor(pfx + "gate_t", [128, 256], F32R, kind="ExternalInput").ap()
    io["ebias"] = nc.dram_tensor(pfx + "ebias", [16, 1], F32, kind="ExternalInput").ap()
    io["wg_t"] = nc.dram_tensor(pfx + "wg_t", [136, 128, 2048], F32R, kind="ExternalInput").ap()
    io["wu_t"] = nc.dram_tensor(pfx + "wu_t", [136, 128, 2048], F32R, kind="ExternalInput").ap()
    io["wd_t"] = nc.dram_tensor(pfx + "wd_t", [272, 128, 1024], F32R, kind="ExternalInput").ap()
    io["xfm2"] = nc.dram_tensor(pfx + "xfm2", [D, T], F32, kind=out_kind).ap()
    return io


def host_l0ffn_weights(inp):
    wg = np.asarray(inp["L0_ffn_w_gate"], np.float32)
    wu = np.asarray(inp["L0_ffn_w_up"], np.float32)
    wd = np.asarray(inp["L0_ffn_w_down"], np.float32)
    swg = np.asarray(inp["L0_ffn_sw_gate"], np.float32)
    swu = np.asarray(inp["L0_ffn_sw_up"], np.float32)
    swd = np.asarray(inp["L0_ffn_sw_down"], np.float32)
    gate_w = np.asarray(inp["L0_ffn_gate_w"], np.float32)

    def tile_gu(w):
        return host_tile_w_ocmajor(np.ascontiguousarray(w.T))

    wg_t = np.concatenate([tile_gu(wg[e]) for e in range(16)] + [tile_gu(swg)])
    wu_t = np.concatenate([tile_gu(wu[e]) for e in range(16)] + [tile_gu(swu)])

    def tile_d(w):
        return host_tile_w_ocmajor(np.ascontiguousarray(w.T))

    wd_t = np.concatenate([tile_d(wd[e]) for e in range(16)] + [tile_d(swd)])
    gt = np.ascontiguousarray(
        gate_w.T.reshape(16, 128, 16).transpose(1, 0, 2).reshape(128, 256))
    eb = np.ascontiguousarray(np.asarray(inp["L0_ffn_expert_bias"],
                                         np.float32).reshape(16, 1))
    return dict(gate_t=gt, ebias=eb,
                wg_t=np.ascontiguousarray(wg_t),
                wu_t=np.ascontiguousarray(wu_t),
                wd_t=np.ascontiguousarray(wd_t))


def build_l0ffn(b, C, io, start_dep=()):
    ar = Arena(b, "ar2", 50520)
    xn = [ar.f32r(c * 1024, 1024) for c in range(DC)]
    dn_acc = [ar.t(16384 + c * 512, [128, 1024], BF16) for c in range(DC)]
    HG = 24576
    h8 = [ar.f32r(HG + i * 1024, 1024) for i in range(8)]
    lnout = [ar.f32r(HG, 1024), ar.f32r(HG + 1024, 1024)]
    WB = 32768
    wgb = [ar.f32r(WB + i * 2048, 2048) for i in range(2)]
    wub = [ar.f32r(WB + 4096 + i * 2048, 2048) for i in range(2)]
    wdb = [ar.f32r(WB + 8192 + i * 1024, 1024) for i in range(2)]
    ln_scr = dict(sq=[ar.f32r(WB, 512), ar.f32r(WB + 512, 512)],
                  mean=ar.f32r(WB + 1024, 1024, p=1),
                  var=ar.f32r(WB + 2048, 1024, p=1),
                  rstd=ar.f32r(WB + 3072, 1024, p=1))
    MS = 43008
    gate_all = ar.f32r(MS, 256)
    sc_E = ar.f32(MS + 256, 1024, p=16)
    bb_E = ar.f32(MS + 1280, 1024, p=16)
    cwE = ar.f32r(MS + 2304, 1024, p=16)
    ebias_s = ar.f32(MS + 3328, 8, p=16)
    smallT = ar.f32(MS + 3336, 80)
    bb_T = smallT[:, 0:16]
    scT = smallT[:, 16:32]
    indacc = smallT[:, 32:48]
    tmp16 = smallT[:, 48:64]
    mxcol = smallT[:, 64:65]
    rscol = smallT[:, 65:66]
    cwrow = ar.f32r(MS + 3416, 1024, p=1)
    cwb = ar.f32(MS + 4440, 1024)
    gsb = [ar.f32(MS + 5464, 512), ar.f32(MS + 5976, 512)]
    hh = [ar.f32(MS + 6488, 512), ar.f32(MS + 7000, 512)]

    sld = b.sems["ld"]
    sld2 = b.sems["ld2"]
    sst = b.sems["st"]

    @b.on("sync")
    def _(q):
        b.waits(q, *C["ready"], *start_dep)
        for c in range(DC):
            sld.inc(q.dma_start(out=xn[c][:, :],
                                in_=io["xin"][c * 128:(c + 1) * 128, :]), 16)
        sld.inc(q.dma_start(out=gate_all[:, :], in_=io["gate_t"][:, :]), 16)
        sld.inc(q.dma_start(out=ebias_s[:, 0:1], in_=io["ebias"][:, :]), 16)
    xn_ld = b.mark("ld")

    @b.on("tensor")
    def _(t):
        b.waits(t, xn_ld)
        for tt in range(2):
            n0 = tt * 512
            for k in range(DC):
                ins = t.matmul(out=b.pool[tt][0:16, :],
                               lhsT=gate_all[:, k * 16:(k + 1) * 16],
                               rhs=xn[k][:, n0:n0 + 512],
                               start=(k == 0), stop=(k == DC - 1))
        b.sems["t"].inc(ins)
    lg_mm = b.mark("t")

    @b.on("scalar")
    def _(s):
        b.waits(s, lg_mm)
        for tt in range(2):
            n0 = tt * 512
            ins = s.activation(out=sc_E[:, n0:n0 + 512], in_=b.pool[tt][0:16, :],
                               func=AF.Sigmoid)
        b.sems["s"].inc(ins)
    sc_done = b.mark("s")
    b.bank_free[0] = [sc_done]
    b.bank_free[1] = [sc_done]

    @b.on("vector")
    def _(v):
        b.waits(v, sc_done)
        ins = v.tensor_scalar_add(out=bb_E, in0=sc_E, scalar1=ebias_s[:, 0:1])
        b.sems["v"].inc(ins)
    bb_done = b.mark("v")

    for tg in range(8):
        m0 = tg * 128

        @b.on("tensor")
        def _(t, tg=tg, m0=m0):
            if tg == 0:
                b.waits(t, bb_done)
            b.waits(t, b.bank_free[2], b.bank_free[3])
            t.transpose(out=b.pool[2][0:128, 0:16], in_=bb_E[0:16, m0:m0 + 128],
                        identity=C["ident"][0:16, 0:16])
            ins = t.transpose(out=b.pool[3][0:128, 0:16],
                              in_=sc_E[0:16, m0:m0 + 128],
                              identity=C["ident"][0:16, 0:16])
            b.sems["t"].inc(ins)
        tr_done = b.mark("t")

        @b.on("vector")
        def _(v, tg=tg, tr_done=tr_done):
            b.waits(v, tr_done)
            b.chain(v, "v", v.tensor_copy(out=bb_T, in_=b.pool[2][0:128, 0:16]))
            b.chain(v, "v", v.tensor_copy(out=scT, in_=b.pool[3][0:128, 0:16]))
            b.chain(v, "v", v.memset(indacc, 0.0))
            for it in range(4):
                b.chain(v, "v", v.reduce_max(out=mxcol, in_=bb_T, axis=AX.X))
                b.chain(v, "v", v.tensor_single_scalar(out=tmp16, in_=bb_T,
                                                       scalar=mxcol,
                                                       op=ALU.is_ge))
                b.chain(v, "v", v.tensor_tensor(out=indacc, in0=indacc,
                                                in1=tmp16, op=ALU.max))
                b.chain(v, "v", v.scalar_tensor_tensor(out=bb_T, in0=tmp16,
                                                       scalar=-1e30, in1=bb_T,
                                                       op0=ALU.mult, op1=ALU.add))
            b.chain(v, "v", v.tensor_tensor(out=tmp16, in0=indacc, in1=scT,
                                            op=ALU.mult))
            b.chain(v, "v", v.reduce_sum(out=rscol, in_=tmp16, axis=AX.X))
            b.chain(v, "v", v.reciprocal(out=rscol, in_=rscol))
            b.chain(v, "v", v.tensor_scalar_mul(out=tmp16, in0=tmp16,
                                                scalar1=rscol))
        cw_done = b.mark("v")
        b.bank_free[2] = [cw_done]
        b.bank_free[3] = [cw_done]

        @b.on("tensor")
        def _(t, tg=tg, cw_done=cw_done):
            b.waits(t, cw_done, b.bank_free[4])
            ins = t.transpose(out=b.pool[4][0:16, 0:128], in_=tmp16[:, :],
                              identity=C["ident"][:, :])
            b.sems["t"].inc(ins)
        cwT_done = b.mark("t")

        @b.on("scalar")
        def _(s, tg=tg, m0=m0, cwT_done=cwT_done):
            b.waits(s, cwT_done)
            ins = s.activation(out=cwE[:, m0:m0 + 128], in_=b.pool[4][0:16, 0:128],
                               func=AF.Copy)
            b.sems["s"].inc(ins)
        b.bank_free[4] = [b.mark("s")]
    cwE_done = b.mark("s")

    wg_free = [NONE, NONE]
    wu_free = [NONE, NONE]
    wd_free = [NONE, NONE]
    cwrow_free = NONE
    cwb_free = NONE
    gsb_free = [NONE, NONE]
    hh_free = [NONE, NONE]
    h8_free = [NONE] * 8
    ei = 0

    for e in range(17):
        routed = e < 16
        if routed:
            @b.on("gpsimd")
            def _(p, e=e):
                if e == 0:
                    b.waits(p, cwE_done)
                b.waits(p, cwrow_free)
                sst.inc(p.dma_start(out=cwrow[:, :], in_=cwE[e:e + 1, :]), 16)
            cwrow_ld = b.mark("st")

            @b.on("tensor")
            def _(t, cwrow_ld=cwrow_ld):
                b.waits(t, cwrow_ld, b.bank_free[6], b.bank_free[7])
                for tt in range(2):
                    n0 = tt * 512
                    ins = t.matmul(out=b.pool[6 + tt][:, :],
                                   lhsT=C["ones_row"][:, :],
                                   rhs=cwrow[:, n0:n0 + 512], start=True,
                                   stop=True)
                b.sems["t"].inc(ins)
            cwmm = b.mark("t")
            cwrow_free = [cwmm]

            @b.on("vector")
            def _(v, cwmm=cwmm):
                b.waits(v, cwmm, cwb_free)
                for tt in range(2):
                    n0 = tt * 512
                    ins = v.tensor_copy(out=cwb[:, n0:n0 + 512],
                                        in_=b.pool[6 + tt][:, :])
                b.sems["v"].inc(ins)
            cwb_done = b.mark("v")
            b.bank_free[6] = [cwb_done]
            b.bank_free[7] = [cwb_done]
        else:
            cwb_done = NONE

        for ic in range(8):
            wslot = e * 8 + ic

            @b.on("sync")
            def _(q, e=e, ic=ic, wslot=wslot):
                b.waits(q, wg_free[ic % 2], wu_free[ic % 2])
                sld.inc(q.dma_start(out=wgb[ic % 2][:, :],
                                    in_=io["wg_t"][wslot, :, :]), 16)
                sld.inc(q.dma_start(out=wub[ic % 2][:, :],
                                    in_=io["wu_t"][wslot, :, :]), 16)
            w_ld = b.mark("ld")

            for tt in range(2):
                n0 = tt * 512
                gbk, ubk = tt, 2 + tt
                up = ei % 2
                ei += 1

                @b.on("tensor")
                def _(t, e=e, ic=ic, tt=tt, n0=n0, gbk=gbk, ubk=ubk, w_ld=w_ld):
                    if tt == 0:
                        b.waits(t, w_ld)
                        if e == 0 and ic == 0:
                            b.waits(t, xn_ld)
                    b.waits(t, b.bank_free[gbk], b.bank_free[ubk])
                    for k in range(DC):
                        t.matmul(out=b.pool[gbk][:, :],
                                 lhsT=wgb[ic % 2][:, k * 128:(k + 1) * 128],
                                 rhs=xn[k][:, n0:n0 + 512],
                                 start=(k == 0), stop=(k == DC - 1))
                    for k in range(DC):
                        ins = t.matmul(out=b.pool[ubk][:, :],
                                       lhsT=wub[ic % 2][:, k * 128:(k + 1) * 128],
                                       rhs=xn[k][:, n0:n0 + 512],
                                       start=(k == 0), stop=(k == DC - 1))
                    b.sems["t"].inc(ins)
                gu_mm = b.mark("t")
                if tt == 1:
                    wg_free[ic % 2] = [gu_mm]
                    wu_free[ic % 2] = [gu_mm]

                @b.on("scalar")
                def _(s, gbk=gbk, up=up, gu_mm=gu_mm):
                    b.waits(s, gu_mm, gsb_free[up])
                    ins = s.activation(out=gsb[up], in_=b.pool[gbk][:, :],
                                       func=AF.Silu)
                    b.sems["s"].inc(ins)
                gs_done = b.mark("s")
                b.bank_free[gbk] = [gs_done]

                @b.on("vector")
                def _(v, ic=ic, tt=tt, n0=n0, ubk=ubk, up=up, gs_done=gs_done,
                      routed=routed, cwb_done=cwb_done):
                    b.waits(v, gs_done, hh_free[up])
                    if routed:
                        v.tensor_tensor(out=hh[up], in0=gsb[up],
                                        in1=b.pool[ubk][:, :], op=ALU.mult)
                        b.waits(v, h8_free[ic])
                        ins = v.tensor_tensor(out=h8[ic][:, n0:n0 + 512],
                                              in0=hh[up], in1=cwb[:, n0:n0 + 512],
                                              op=ALU.mult)
                    else:
                        b.waits(v, h8_free[ic])
                        ins = v.tensor_tensor(out=h8[ic][:, n0:n0 + 512],
                                              in0=gsb[up], in1=b.pool[ubk][:, :],
                                              op=ALU.mult)
                    b.sems["v"].inc(ins)
                hdone = b.mark("v")
                gsb_free[up] = [hdone]
                hh_free[up] = [hdone]
                b.bank_free[ubk] = [hdone]
        h_ready = b.mark("v")

        for hc in range(DC):
            wslot = e * 16 + hc

            @b.on("sync")
            def _(q, wslot=wslot, hc=hc):
                b.waits(q, wd_free[hc % 2])
                sld2.inc(q.dma_start(out=wdb[hc % 2][:, :],
                                     in_=io["wd_t"][wslot, :, :]), 16)
            wd_ld = b.mark("ld2")

            for tt in range(2):
                n0 = tt * 512
                dbk = 4 + tt

                @b.on("tensor")
                def _(t, hc=hc, tt=tt, dbk=dbk, wd_ld=wd_ld, h_ready=h_ready):
                    if tt == 0:
                        b.waits(t, wd_ld)
                        if hc == 0:
                            b.waits(t, h_ready)
                    b.waits(t, b.bank_free[dbk])
                    for k in range(8):
                        ins = t.matmul(out=b.pool[dbk][:, :],
                                       lhsT=wdb[hc % 2][:, k * 128:(k + 1) * 128],
                                       rhs=h8[k][:, n0:n0 + 512],
                                       start=(k == 0), stop=(k == 7))
                    b.sems["t"].inc(ins)
                dn_mm = b.mark("t")
                if tt == 1:
                    wd_free[hc % 2] = [dn_mm]

                @b.on("vector")
                def _(v, e=e, hc=hc, n0=n0, dbk=dbk, dn_mm=dn_mm):
                    b.waits(v, dn_mm)
                    if e == 0:
                        ins = v.tensor_copy(out=dn_acc[hc][:, n0:n0 + 512],
                                            in_=b.pool[dbk][:, :])
                    else:
                        ins = v.tensor_tensor(out=dn_acc[hc][:, n0:n0 + 512],
                                              in0=dn_acc[hc][:, n0:n0 + 512],
                                              in1=b.pool[dbk][:, :], op=ALU.add)
                    b.sems["v"].inc(ins)
                b.bank_free[dbk] = [b.mark("v")]
        down_done = b.mark("t")
        for ic in range(8):
            h8_free[ic] = [down_done]
    dn_done = b.mark("v")

    @b.on("vector")
    def _(v):
        b.waits(v, dn_done)
        for c in range(DC):
            ins = v.tensor_tensor(out=xn[c][:, :], in0=xn[c][:, :],
                                  in1=dn_acc[c][:, :], op=ALU.add)
        b.sems["v"].inc(ins)
    resid_done = b.mark("v")

    lnout_free = [NONE, NONE]

    def pre_wait_fn(c, j):
        return lnout_free[c % 2]

    def post_fn(c, j, vmark):
        @b.on("gpsimd")
        def _(p, c=c, j=j, vmark=vmark):
            b.waits(p, vmark)
            n0, n1 = j * 512, (j + 1) * 512
            sst.inc(p.dma_start(out=io["xfm2"][c * 128:(c + 1) * 128, n0:n1],
                                in_=lnout[c % 2].bitcast(F32)[:, n0:n1]), 16)
        lnout_free[c % 2] = [("st", b.sems["st"].n)]

    out_aps = {c: lnout[c % 2] for c in range(DC)}
    emit_ln(b, C, xn, ln_scr, [resid_done], [0, 1, 2, 3, 4, 5],
            out_aps, pre_wait_fn, post_fn)

    @b.on("gpsimd")
    def _(p):
        p.wait_ge(b.sems["st"].h, b.sems["st"].n)
    return io



def declare_l1attn(nc, pfx="", xin=None, out_kind="ExternalOutput"):
    io = {}
    io["xin"] = xin if xin is not None else nc.dram_tensor(
        pfx + "xin", [D, T], F32R, kind="ExternalInput").ap()
    io["wact_t"] = nc.dram_tensor(pfx + "wact_t", [16, 128, 4096], F32R,
                                  kind="ExternalInput").ap()
    io["wrec_t"] = nc.dram_tensor(pfx + "wrec_t", [16, 128, 4096], F32R,
                                  kind="ExternalInput").ap()
    io["wval_t"] = nc.dram_tensor(pfx + "wval_t", [16, 128, 2048], F32R,
                                  kind="ExternalInput").ap()
    io["xfm3"] = nc.dram_tensor(pfx + "xfm3", [D, T], F32, kind=out_kind).ap()
    io["cc_sh_in"] = nc.dram_tensor(pfx + "cc_sh_in", [2048, 1], F32, kind="Internal").ap()
    io["cc_sh_out"] = nc.dram_tensor(pfx + "cc_sh_out", [16384, 1], F32, kind="Internal").ap()
    io["cc_mx_in"] = nc.dram_tensor(pfx + "cc_mx_in", [1024, 1], F32, kind="Internal").ap()
    io["cc_mx_out"] = nc.dram_tensor(pfx + "cc_mx_out", [8192, 1], F32, kind="Internal").ap()
    io["cc_q_in"] = nc.dram_tensor(pfx + "cc_q_in", [2048, 1], F32, kind="Internal").ap()
    io["cc_q_out"] = nc.dram_tensor(pfx + "cc_q_out", [16384, 1], F32, kind="Internal").ap()
    return io


def host_l1attn_weights(inp):
    W_act = np.asarray(inp["L1_attn_W_act"], np.float32)
    W_rec = np.asarray(inp["L1_attn_W_rec"], np.float32)
    W_val = np.asarray(inp["L1_attn_W_val"], np.float32)
    perm = np.concatenate([np.arange(0, 2048, 2), np.arange(1, 2048, 2)])
    wact = np.ascontiguousarray(W_act.T[:, perm])
    wrec = np.ascontiguousarray(W_rec.T)
    wval = np.ascontiguousarray(W_val.T[perm, :])
    return dict(wact_t=host_tile_w_ocmajor(wact),
                wrec_t=host_tile_w_ocmajor(wrec),
                wval_t=host_tile_w_ocmajor(wval))


def build_l1attn(b, C, io, start_dep=()):
    ar = Arena(b, "ar3", 51392)
    x1e = [ar.f32r(c * 1032, 1032) for c in range(DC)]
    XV = 16512
    xv = [ar.f32(XV + c * 1024, 1024) for c in range(DC)]
    kv = [ar.f32r(XV + c * 1024, 1024) for c in range(DC)]
    resid = kv
    M = 32896
    m_t = [ar.f32(M + i * 1024, 1024) for i in range(8)]
    W = 41088
    wactb = ar.f32r(W, 4096)
    wactb2 = ar.f32r(M, 4096)
    wvalb2 = ar.f32r(49344, 2048)
    xvtmp = ar.f32(W + 4096, 512)
    pong = [ar.f32(W + i * 1024, 1024) for i in range(2)]
    tsc = [ar.f32(W + 2048 + i * 1024, 1024) for i in range(4)]
    mscr = ar.f32(W + 6144, 1024)
    wvalb = ar.f32r(W, 2048)
    wrecb = ar.f32r(W + 2048, 4096)
    rr = ar.f32(W + 6144, 512)
    r2 = ar.f32(W + 6656, 512)
    r4 = ar.f32(W + 7168, 512)
    ln_scr = dict(sq=[ar.f32r(W, 512), ar.f32r(W + 512, 512)],
                  mean=ar.f32r(W + 1024, 1024, p=1),
                  var=ar.f32r(W + 2048, 1024, p=1),
                  rstd=ar.f32r(W + 3072, 1024, p=1))
    lnout = [ar.f32r(W + 4096, 1024), ar.f32r(W + 5120, 1024)]
    SM = 49280
    shtab = ar.f32(SM, 8)
    mtab = ar.f32(SM + 8, 8)
    rtab = ar.f32(SM + 16, 8)
    itab = ar.f32(SM + 24, 8)
    maxoff = ar.f32(SM + 32, 8)
    pcol = ar.f32(SM + 40, 8)
    shcol = ar.f32(SM + 48, 8)
    assert SM + 56 <= 50400

    sld = b.sems["ld"]
    sld2 = b.sems["ld2"]
    sst = b.sems["st"]

    @b.on("sync")
    def _(q):
        b.waits(q, *C["ready"], *start_dep)
        for c in range(DC):
            sld.inc(q.dma_start(out=x1e[c][:, 1:1025],
                                in_=io["xin"][c * 128:(c + 1) * 128, :]), 16)
    x_ld = b.mark("ld")

    @b.on("gpsimd")
    def _(p):
        b.waits(p, x_ld)
        for c in range(DC):
            sst.inc(p.dma_start(out=io["cc_sh_in"][c * 128:(c + 1) * 128, :],
                                in_=x1e[c][:, 1024:1025].bitcast(F32)), 16)
        p.wait_ge(b.sems["st"].h, b.sems["st"].n)
        b.sems["cc"].inc(p.collective_compute(
            "AllGather", ALU.bypass, replica_groups=RG,
            ins=[io["cc_sh_in"].opt()], outs=[io["cc_sh_out"].opt()]))
    sh_cc = b.mark("cc")

    prev_comb = NONE
    for c in range(DC):
        @b.on("sync")
        def _(q, c=c, prev_comb=prev_comb):
            if c == 0:
                b.waits(q, sh_cc)
            b.waits(q, prev_comb)
            for r in range(8):
                sld2.inc(q.dma_start(
                    out=shtab[:, r:r + 1],
                    in_=io["cc_sh_out"][r * 2048 + c * 128:r * 2048 + (c + 1) * 128, :]),
                    16)
        sh_ld = b.mark("ld2")

        @b.on("vector")
        def _(v, c=c, sh_ld=sh_ld):
            b.waits(v, sh_ld)
            b.chain(v, "v", v.tensor_tensor(out=shcol, in0=shtab,
                                            in1=C["prev"][:, :], op=ALU.mult))
            with b.nc.allow_low_precision(reason="f32r shift column"):
                b.chain(v, "v", v.reduce_sum(out=x1e[c][:, 0:1], in_=shcol,
                                             axis=AX.X))
        prev_comb = b.mark("v")
    shift_done = b.mark("v")

    wact_free = [NONE, NONE]
    wactbs = [wactb, wactb2]
    xvtmp_free = NONE

    def ct_rhs(k, n0, n1):
        if k < DC:
            return x1e[k][:, n0:n1]
        return x1e[k - DC][:, n0 + 1:n1 + 1]

    for oc in range(DC):
        @b.on("sync")
        def _(q, oc=oc):
            if oc == 0:
                b.waits(q, shift_done)
            b.waits(q, wact_free[oc % 2])
            sld.inc(q.dma_start(out=wactbs[oc % 2][:, :],
                                in_=io["wact_t"][oc, :, :]), 16)
        wld = b.mark("ld")

        for tt in range(2):
            n0 = tt * 512
            bk = tt

            @b.on("tensor")
            def _(t, oc=oc, tt=tt, n0=n0, bk=bk, wld=wld):
                if tt == 0:
                    b.waits(t, wld)
                    if oc == 0:
                        b.waits(t, shift_done)
                b.waits(t, b.bank_free[bk])
                for k in range(32):
                    ins = t.matmul(out=b.pool[bk][:, :],
                                   lhsT=wactbs[oc % 2][:, k * 128:(k + 1) * 128],
                                   rhs=ct_rhs(k, n0, n0 + 512),
                                   start=(k == 0), stop=(k == 31))
                b.sems["t"].inc(ins)
            mm = b.mark("t")

            @b.on("vector")
            def _(v, bk=bk, mm=mm):
                b.waits(v, mm, xvtmp_free)
                ins = v.tensor_scalar_max(out=xvtmp, in0=b.pool[bk][:, :],
                                          scalar1=0.0)
                b.sems["v"].inc(ins)
            relu_done = b.mark("v")
            b.bank_free[bk] = [relu_done]

            @b.on("scalar")
            def _(s, oc=oc, n0=n0, relu_done=relu_done):
                b.waits(s, relu_done)
                ins = s.activation(out=xv[oc][:, n0:n0 + 512], in_=xvtmp,
                                   func=AF.Sigmoid)
                b.sems["s"].inc(ins)
            xvtmp_free = [b.mark("s")]
        wact_free[oc % 2] = [b.mark("t")]
    xv_done = b.mark("s")

    mag_marks = []
    for pc in range(8):
        @b.on("scalar")
        def _(s, pc=pc):
            if pc == 0:
                b.waits(s, xv_done, b.mark("t"))
            b.waits(s, b.mark("v"))
            s.activation(out=mscr, in_=xv[pc][:, :], func=AF.Square)
            ins = s.activation(out=m_t[pc][:, :], in_=xv[8 + pc][:, :],
                               func=AF.Square)
            b.sems["s"].inc(ins)
        sq_done = b.mark("s")

        @b.on("vector")
        def _(v, pc=pc, sq_done=sq_done):
            b.waits(v, sq_done)
            ins = v.tensor_tensor(out=m_t[pc][:, :], in0=m_t[pc][:, :],
                                  in1=mscr, op=ALU.add)
            b.sems["v"].inc(ins)
        add_done = b.mark("v")

        @b.on("scalar")
        def _(s, pc=pc, add_done=add_done):
            b.waits(s, add_done)
            ins = s.activation(out=m_t[pc][:, :], in_=m_t[pc][:, :], func=AF.Sqrt)
            b.sems["s"].inc(ins)
        mag_done = b.mark("s")

        @b.on("vector")
        def _(v, pc=pc, mag_done=mag_done):
            b.waits(v, mag_done)
            cur, oth = m_t[pc], mscr
            for k in [1, 2, 4, 8, 16, 32, 64, 128, 256, 512]:
                b.chain(v, "v", v.tensor_copy(out=oth[:, 0:k], in_=cur[:, 0:k]))
                b.chain(v, "v", v.tensor_max(out=oth[:, k:1024], in0=cur[:, k:1024],
                                             in1=cur[:, 0:1024 - k]))
                cur, oth = oth, cur
            assert cur is m_t[pc]
        cm_done = b.mark("v")

        @b.on("gpsimd")
        def _(p, pc=pc, cm_done=cm_done):
            b.waits(p, cm_done)
            sst.inc(p.dma_start(out=io["cc_mx_in"][pc * 128:(pc + 1) * 128, :],
                                in_=m_t[pc][:, 1023:1024]), 16)
    mx_st = b.mark("st")

    @b.on("gpsimd")
    def _(p):
        p.wait_ge(b.sems["st"].h, b.sems["st"].n)
        b.sems["cc"].inc(p.collective_compute(
            "AllGather", ALU.bypass, replica_groups=RG,
            ins=[io["cc_mx_in"].opt()], outs=[io["cc_mx_out"].opt()]))
    mx_cc = b.mark("cc")

    @b.on("vector")
    def _(v):
        b.waits(v, mx_cc)
        ins = v.tensor_scalar_mul(out=maxoff, in0=C["ltinv"][:, :], scalar1=-1e30)
        b.sems["v"].inc(ins)
    moff_done = b.mark("v")

    q_marks = []
    for pc in range(8):
        @b.on("sync")
        def _(q, pc=pc):
            b.waits(q, mx_cc)
            if pc > 0:
                b.waits(q, q_marks[-1])
            for r in range(8):
                sld2.inc(q.dma_start(
                    out=mtab[:, r:r + 1],
                    in_=io["cc_mx_out"][r * 1024 + pc * 128:r * 1024 + (pc + 1) * 128, :]),
                    16)
        mt_ld = b.mark("ld2")

        @b.on("vector")
        def _(v, pc=pc, mt_ld=mt_ld):
            b.waits(v, mt_ld, moff_done)
            b.chain(v, "v", v.tensor_tensor(out=shcol, in0=mtab, in1=maxoff,
                                            op=ALU.add))
            b.chain(v, "v", v.reduce_max(out=pcol[:, 0:1], in_=shcol, axis=AX.X))
            b.chain(v, "v", v.tensor_scalar_max(out=m_t[pc][:, :],
                                                in0=m_t[pc][:, :],
                                                scalar1=pcol[:, 0:1]))
            v.reciprocal(out=m_t[pc][:, :], in_=m_t[pc][:, :])
            v.tensor_tensor(out=xv[pc][:, :], in0=xv[pc][:, :], in1=m_t[pc][:, :],
                            op=ALU.mult)
            ins = v.tensor_tensor(out=xv[8 + pc][:, :], in0=xv[8 + pc][:, :],
                                  in1=m_t[pc][:, :], op=ALU.mult)
            b.sems["v"].inc(ins)
        norm_done = b.mark("v")

        @b.on("vector")
        def _(v, pc=pc):
            ra, ia = xv[pc], xv[8 + pc]
            rb, ib = pong[0], pong[1]
            for k in [1, 2, 4, 8, 16, 32, 64, 128, 256, 512]:
                b.chain(v, "v", v.tensor_copy(out=rb[:, 0:k], in_=ra[:, 0:k]))
                b.chain(v, "v", v.tensor_copy(out=ib[:, 0:k], in_=ia[:, 0:k]))
                v.tensor_tensor(out=tsc[0][:, 0:1024 - k], in0=ra[:, k:1024],
                                in1=ra[:, 0:1024 - k], op=ALU.mult)
                v.tensor_tensor(out=tsc[1][:, 0:1024 - k], in0=ia[:, k:1024],
                                in1=ia[:, 0:1024 - k], op=ALU.mult)
                v.tensor_tensor(out=tsc[2][:, 0:1024 - k], in0=ra[:, k:1024],
                                in1=ia[:, 0:1024 - k], op=ALU.mult)
                v.tensor_tensor(out=tsc[3][:, 0:1024 - k], in0=ia[:, k:1024],
                                in1=ra[:, 0:1024 - k], op=ALU.mult)
                v.tensor_tensor(out=rb[:, k:1024], in0=tsc[0][:, 0:1024 - k],
                                in1=tsc[1][:, 0:1024 - k], op=ALU.subtract)
                b.chain(v, "v", v.tensor_tensor(out=ib[:, k:1024],
                                                in0=tsc[2][:, 0:1024 - k],
                                                in1=tsc[3][:, 0:1024 - k],
                                                op=ALU.add))
                ra, ia = rb, ib
                rb, ib = (xv[pc], xv[8 + pc]) if ra is pong[0] else (pong[0], pong[1])
            assert ra is xv[pc]
        cp_done = b.mark("v")

        @b.on("gpsimd")
        def _(p, pc=pc, cp_done=cp_done):
            b.waits(p, cp_done)
            sst.inc(p.dma_start(out=io["cc_q_in"][pc * 128:(pc + 1) * 128, :],
                                in_=xv[pc][:, 1023:1024]), 16)
            sst.inc(p.dma_start(out=io["cc_q_in"][1024 + pc * 128:1024 + (pc + 1) * 128, :],
                                in_=xv[8 + pc][:, 1023:1024]), 16)
        q_marks.append(("st", b.sems["st"].n))
    q_st = b.mark("st")

    @b.on("gpsimd")
    def _(p):
        p.wait_ge(b.sems["st"].h, b.sems["st"].n)
        b.sems["cc"].inc(p.collective_compute(
            "AllGather", ALU.bypass, replica_groups=RG,
            ins=[io["cc_q_in"].opt()], outs=[io["cc_q_out"].opt()]))
    q_cc = b.mark("cc")

    prev_q = NONE
    kvdone = []
    for pc in range(8):
        @b.on("sync")
        def _(q, pc=pc, prev_q=prev_q):
            if pc == 0:
                b.waits(q, q_cc)
            b.waits(q, prev_q)
            for r in range(8):
                sld2.inc(q.dma_start(
                    out=rtab[:, r:r + 1],
                    in_=io["cc_q_out"][r * 2048 + pc * 128:r * 2048 + (pc + 1) * 128, :]),
                    16)
                sld2.inc(q.dma_start(
                    out=itab[:, r:r + 1],
                    in_=io["cc_q_out"][r * 2048 + 1024 + pc * 128:
                                       r * 2048 + 1024 + (pc + 1) * 128, :]), 16)
        qt_ld = b.mark("ld2")

        @b.on("vector")
        def _(v, pc=pc, qt_ld=qt_ld):
            b.waits(v, qt_ld)
            Pr, Pi = pcol[:, 1:2], pcol[:, 2:3]
            qr, qi = pcol[:, 3:4], pcol[:, 4:5]
            t0c, t1c = pcol[:, 5:6], pcol[:, 6:7]
            b.chain(v, "v", v.memset(Pr, 1.0))
            b.chain(v, "v", v.memset(Pi, 0.0))
            for r in range(8):
                b.chain(v, "v", v.tensor_scalar(out=qr, in0=rtab[:, r:r + 1],
                                                scalar1=C["lt"][:, r:r + 1],
                                                scalar2=C["ltinv"][:, r:r + 1],
                                                op0=ALU.mult, op1=ALU.add))
                b.chain(v, "v", v.tensor_tensor(out=qi, in0=itab[:, r:r + 1],
                                                in1=C["lt"][:, r:r + 1],
                                                op=ALU.mult))
                b.chain(v, "v", v.tensor_tensor(out=t0c, in0=Pr, in1=qr,
                                                op=ALU.mult))
                b.chain(v, "v", v.tensor_tensor(out=t1c, in0=Pi, in1=qi,
                                                op=ALU.mult))
                b.chain(v, "v", v.tensor_tensor(out=t1c, in0=t0c, in1=t1c,
                                                op=ALU.subtract))
                b.chain(v, "v", v.tensor_tensor(out=t0c, in0=Pr, in1=qi,
                                                op=ALU.mult))
                b.chain(v, "v", v.tensor_tensor(out=Pi, in0=Pi, in1=qr,
                                                op=ALU.mult))
                b.chain(v, "v", v.tensor_tensor(out=Pi, in0=Pi, in1=t0c,
                                                op=ALU.add))
                b.chain(v, "v", v.tensor_copy(out=Pr, in_=t1c))
            v.tensor_scalar_mul(out=pong[0], in0=xv[pc][:, :], scalar1=Pr)
            v.tensor_scalar_mul(out=pong[1], in0=xv[8 + pc][:, :], scalar1=Pi)
            v.tensor_scalar_mul(out=mscr, in0=xv[8 + pc][:, :], scalar1=Pr)
            b.chain(v, "v", v.tensor_scalar_mul(out=tsc[0], in0=xv[pc][:, :],
                                                scalar1=Pi))
            v.tensor_tensor(out=kv[pc][:, :], in0=pong[0], in1=pong[1],
                            op=ALU.subtract)
            ins = v.tensor_tensor(out=kv[8 + pc][:, :], in0=mscr, in1=tsc[0],
                                  op=ALU.add)
            b.sems["v"].inc(ins)
        prev_q = b.mark("v")
        kvdone.append(prev_q)
    kv_done = b.mark("v")

    resid_h = [ar.f32r(M + i * 512, 512) for i in range(DC)]
    wvalbs = [wvalb, wvalb2]
    wval_free = [NONE, NONE]
    wv_free = NONE
    rr_free = NONE
    ln_done_prev = NONE
    for tt in range(2):
        n0 = tt * 512
        oc_marks = []
        for oc in range(DC):
            @b.on("sync")
            def _(q, tt=tt, oc=oc, ln_done_prev=ln_done_prev):
                if oc == 0:
                    if tt == 0:
                        b.waits(q, kv_done)
                    else:
                        b.waits(q, ln_done_prev)
                b.waits(q, wval_free[oc % 2])
                sld.inc(q.dma_start(out=wvalbs[oc % 2][:, :],
                                    in_=io["wval_t"][oc, :, :]), 16)
                b.waits(q, wv_free)
                sld.inc(q.dma_start(out=wrecb[:, :], in_=io["wrec_t"][oc, :, :]), 16)
            wld = b.mark("ld")

            vbk = 2 + oc % 2
            rbk = 4 + oc % 2

            @b.on("tensor")
            def _(t, tt=tt, oc=oc, n0=n0, vbk=vbk, rbk=rbk, wld=wld,
                  ln_done_prev=ln_done_prev):
                b.waits(t, wld)
                if oc == 0 and tt == 0:
                    b.waits(t, kv_done)
                b.waits(t, b.bank_free[vbk], b.bank_free[rbk])
                for k in range(DC):
                    t.matmul(out=b.pool[vbk][:, :],
                             lhsT=wvalbs[oc % 2][:, k * 128:(k + 1) * 128],
                             rhs=kv[k][:, n0:n0 + 512],
                             start=(k == 0), stop=(k == DC - 1))
                for k in range(32):
                    ins = t.matmul(out=b.pool[rbk][:, :],
                                   lhsT=wrecb[:, k * 128:(k + 1) * 128],
                                   rhs=ct_rhs(k, n0, n0 + 512),
                                   start=(k == 0), stop=(k == 31))
                b.sems["t"].inc(ins)
            mm = b.mark("t")
            wv_free = [mm]
            wval_free[oc % 2] = [mm]

            @b.on("vector")
            def _(v, rbk=rbk, mm=mm):
                b.waits(v, mm, rr_free)
                ins = v.tensor_scalar_max(out=rr, in0=b.pool[rbk][:, :],
                                          scalar1=0.0)
                b.sems["v"].inc(ins)
            relu_done = b.mark("v")
            b.bank_free[rbk] = [relu_done]

            @b.on("scalar")
            def _(s, relu_done=relu_done):
                b.waits(s, relu_done)
                ins = s.activation(out=rr, in_=rr, func=AF.Sigmoid)
                b.sems["s"].inc(ins)
            sig_done = b.mark("s")

            @b.on("vector")
            def _(v, vbk=vbk, sig_done=sig_done):
                b.waits(v, sig_done)
                v.tensor_tensor(out=r2, in0=rr, in1=rr, op=ALU.mult)
                b.chain(v, "v", v.tensor_tensor(out=r4, in0=r2, in1=r2,
                                                op=ALU.mult))
                ins = v.tensor_tensor(out=r2, in0=b.pool[vbk][:, :], in1=r4,
                                      op=ALU.mult)
                b.sems["v"].inc(ins)
            o_done = b.mark("v")
            rr_free = [o_done]
            b.bank_free[vbk] = [o_done]

            @b.on("gpsimd")
            def _(p, oc=oc, n0=n0, o_done=o_done):
                b.waits(p, o_done)
                ins = p.tensor_tensor(out=resid_h[oc][:, :], in0=r2,
                                      in1=x1e[oc][:, 1 + n0:513 + n0],
                                      op=ALU.add)
                b.sems["p"].inc(ins)
            rr_free = [b.mark("p")]
        resid_done = b.mark("p")

        lnout_free = [NONE, NONE]

        def pre_wait_fn(c, j, lnout_free=lnout_free):
            return lnout_free[c % 2]

        def post_fn(c, j, vmark, tt=tt, lnout_free=lnout_free):
            @b.on("gpsimd")
            def _(p, c=c, vmark=vmark, tt=tt):
                b.waits(p, vmark)
                n0 = tt * 512
                sst.inc(p.dma_start(
                    out=io["xfm3"][c * 128:(c + 1) * 128, n0:n0 + 512],
                    in_=lnout[c % 2].bitcast(F32)[:, 0:512]), 16)
            lnout_free[c % 2] = [("st", b.sems["st"].n)]

        out_aps = {c: lnout[c % 2][:, 0:512] for c in range(DC)}
        ln_done_prev = emit_ln(b, C, resid_h, ln_scr, [resid_done],
                               [0, 1, 2, 3, 4, 5], out_aps, pre_wait_fn,
                               post_fn, width=512)

    @b.on("gpsimd")
    def _(p):
        p.wait_ge(b.sems["st"].h, b.sems["st"].n)
    return io



def declare_l1ffn(nc, pfx="", xin=None, out_kind="ExternalOutput"):
    io = {}
    io["xin"] = xin if xin is not None else nc.dram_tensor(
        pfx + "xin", [D, T], F32R, kind="ExternalInput").ap()
    io["gq_t"] = nc.dram_tensor(pfx + "gq_t", [64, 128, 2048], I32, kind="ExternalInput").ap()
    io["uq_t"] = nc.dram_tensor(pfx + "uq_t", [64, 128, 2048], I32, kind="ExternalInput").ap()
    io["dq_t"] = nc.dram_tensor(pfx + "dq_t", [128, 128, 1024], I32, kind="ExternalInput").ap()
    io["gusc"] = nc.dram_tensor(pfx + "gusc", [8192, 2], F32, kind="ExternalInput").ap()
    io["dsc"] = nc.dram_tensor(pfx + "dsc", [2048, 1], F32, kind="ExternalInput").ap()
    io["xfm4"] = nc.dram_tensor(pfx + "xfm4", [D, T], F32, kind=out_kind).ap()
    return io


def host_l1ffn_weights(inp):
    gq = np.asarray(inp["L1_ffn_gate_w"], np.int32)
    uq = np.asarray(inp["L1_ffn_up_w"], np.int32)
    dq = np.asarray(inp["L1_ffn_down_w"], np.int32)

    def tile_i32(w):
        K, M = w.shape
        a = w.reshape(K // 128, 128, M // 128, 128).transpose(2, 1, 0, 3)
        return np.ascontiguousarray(a.reshape(M // 128, 128, K))

    gq_t = tile_i32(np.ascontiguousarray(gq.T))
    uq_t = tile_i32(np.ascontiguousarray(uq.T))
    dqT = np.ascontiguousarray(dq.T)
    a = dqT.reshape(8, 8, 128, 16, 128)
    a = a.transpose(0, 3, 2, 1, 4)
    dq_t = np.ascontiguousarray(a.reshape(128, 128, 1024))
    gusc = np.ascontiguousarray(np.stack(
        [np.asarray(inp["L1_ffn_gate_scale"], np.float32),
         np.asarray(inp["L1_ffn_up_scale"], np.float32)], axis=1))
    dsc = np.ascontiguousarray(
        np.asarray(inp["L1_ffn_down_scale"], np.float32).reshape(2048, 1))
    return dict(gq_t=gq_t, uq_t=uq_t, dq_t=dq_t, gusc=gusc, dsc=dsc)


def build_l1ffn(b, C, io, start_dep=()):
    ar = Arena(b, "ar4", 50400)
    xn = [ar.f32r(c * 1024, 1024) for c in range(DC)]
    dn_acc = [ar.f32(16384 + c * 1024, 1024) for c in range(DC)]
    H8 = 32768
    h8 = [ar.f32r(H8 + i * 1024, 1024) for i in range(8)]
    ln_scr = dict(sq=[ar.f32r(H8, 512), ar.f32r(H8 + 512, 512)],
                  mean=ar.f32r(H8 + 1024, 1024, p=1),
                  var=ar.f32r(H8 + 2048, 1024, p=1),
                  rstd=ar.f32r(H8 + 3072, 1024, p=1))
    lnout = [ar.f32r(H8 + 4096, 1024), ar.f32r(H8 + 5120, 1024)]
    WQ = 40960
    gqs = ar.t(WQ, [128, 2048], I32)
    uqs = ar.t(WQ + 2048, [128, 2048], I32)
    wg_fr = ar.f32r(WQ + 4096, 2048)
    wu_fr = ar.f32r(WQ + 6144, 2048)
    dqs = ar.t(WQ + 4096, [128, 1024], I32)
    wd_fr = ar.f32r(WQ + 5120, 1024)
    scl = ar.f32(49152, 144)
    gsb = ar.f32(49296, 512)
    usb = ar.f32(49808, 512)

    sld = b.sems["ld"]
    sld2 = b.sems["ld2"]
    sst = b.sems["st"]

    @b.on("sync")
    def _(q):
        b.waits(q, *C["ready"], *start_dep)
        for c in range(DC):
            sld.inc(q.dma_start(out=xn[c][:, :],
                                in_=io["xin"][c * 128:(c + 1) * 128, :]), 16)
        for ic in range(64):
            sld.inc(q.dma_start(out=scl[:, ic * 2:(ic + 1) * 2],
                                in_=io["gusc"][ic * 128:(ic + 1) * 128, :]), 16)
        for hc in range(DC):
            sld.inc(q.dma_start(out=scl[:, 128 + hc:129 + hc],
                                in_=io["dsc"][hc * 128:(hc + 1) * 128, :]), 16)
    xn_ld = b.mark("ld")

    gq_free = NONE
    wgfr_free = NONE
    wufr_free = NONE
    gsb_free = NONE
    usb_free = NONE
    h8_free = [NONE] * 8
    dq_free = NONE
    wdfr_free = NONE

    for icg in range(8):
        for i8 in range(8):
            ic = icg * 8 + i8

            @b.on("sync")
            def _(q, ic=ic):
                b.waits(q, gq_free)
                sld.inc(q.dma_start(out=gqs[:, :], in_=io["gq_t"][ic, :, :]), 16)
                sld.inc(q.dma_start(out=uqs[:, :], in_=io["uq_t"][ic, :, :]), 16)
            q_ld = b.mark("ld")

            @b.on("vector")
            def _(v, q_ld=q_ld):
                b.waits(v, q_ld, wgfr_free)
                ins = v.tensor_scalar_add(out=wg_fr, in0=gqs[:, :],
                                          scalar1=-128.0)
                b.sems["v"].inc(ins)
            wg_done = b.mark("v")

            @b.on("vector")
            def _(v, q_ld=q_ld):
                b.waits(v, q_ld, wufr_free)
                ins = v.tensor_scalar_add(out=wu_fr, in0=uqs[:, :],
                                          scalar1=-128.0)
                b.sems["v"].inc(ins)
            wu_done = b.mark("v")
            gq_free = [wg_done, wu_done]

            for tt in range(2):
                n0 = tt * 512
                gbk, ubk = tt, 2 + tt

                @b.on("tensor")
                def _(t, ic=ic, tt=tt, n0=n0, gbk=gbk, ubk=ubk,
                      wg_done=wg_done, wu_done=wu_done):
                    if tt == 0:
                        b.waits(t, wg_done, wu_done)
                        if ic == 0:
                            b.waits(t, xn_ld)
                    b.waits(t, b.bank_free[gbk], b.bank_free[ubk])
                    for k in range(DC):
                        t.matmul(out=b.pool[gbk][:, :],
                                 lhsT=wg_fr[:, k * 128:(k + 1) * 128],
                                 rhs=xn[k][:, n0:n0 + 512],
                                 start=(k == 0), stop=(k == DC - 1))
                    for k in range(DC):
                        ins = t.matmul(out=b.pool[ubk][:, :],
                                       lhsT=wu_fr[:, k * 128:(k + 1) * 128],
                                       rhs=xn[k][:, n0:n0 + 512],
                                       start=(k == 0), stop=(k == DC - 1))
                    b.sems["t"].inc(ins)
                mm = b.mark("t")
                wgfr_free = [mm]
                wufr_free = [mm]

                @b.on("scalar")
                def _(s, ic=ic, gbk=gbk, mm=mm):
                    b.waits(s, mm, gsb_free)
                    ins = s.activation(out=gsb, in_=b.pool[gbk][:, :],
                                       func=AF.Silu,
                                       scale=scl[:, ic * 2:ic * 2 + 1])
                    b.sems["s"].inc(ins)
                gs_done = b.mark("s")
                b.bank_free[gbk] = [gs_done]

                @b.on("vector")
                def _(v, ic=ic, ubk=ubk, mm=mm):
                    b.waits(v, mm, usb_free)
                    ins = v.tensor_scalar_mul(out=usb, in0=b.pool[ubk][:, :],
                                              scalar1=scl[:, ic * 2 + 1:ic * 2 + 2])
                    b.sems["v"].inc(ins)
                us_done = b.mark("v")
                b.bank_free[ubk] = [us_done]

                @b.on("gpsimd")
                def _(p, i8=i8, n0=n0, gs_done=gs_done, us_done=us_done):
                    b.waits(p, gs_done, us_done, h8_free[i8])
                    ins = p.tensor_tensor(out=h8[i8][:, n0:n0 + 512], in0=gsb,
                                          in1=usb, op=ALU.mult)
                    b.sems["p"].inc(ins)
                hm = b.mark("p")
                gsb_free = [hm]
                usb_free = [hm]
        h_ready = b.mark("p")

        for hc in range(DC):
            @b.on("sync")
            def _(q, icg=icg, hc=hc):
                b.waits(q, dq_free)
                if hc == 0:
                    b.waits(q, wgfr_free)
                sld2.inc(q.dma_start(out=dqs[:, :],
                                     in_=io["dq_t"][icg * 16 + hc, :, :]), 16)
            dq_ld = b.mark("ld2")

            @b.on("vector")
            def _(v, dq_ld=dq_ld):
                b.waits(v, dq_ld, wdfr_free)
                ins = v.tensor_scalar_add(out=wd_fr, in0=dqs[:, :],
                                          scalar1=-128.0)
                b.sems["v"].inc(ins)
            wd_done = b.mark("v")
            dq_free = [wd_done]

            for tt in range(2):
                n0 = tt * 512
                dbk = 4 + tt

                @b.on("tensor")
                def _(t, tt=tt, dbk=dbk, wd_done=wd_done, h_ready=h_ready):
                    if tt == 0:
                        b.waits(t, wd_done, h_ready)
                    b.waits(t, b.bank_free[dbk])
                    for k in range(8):
                        ins = t.matmul(out=b.pool[dbk][:, :],
                                       lhsT=wd_fr[:, k * 128:(k + 1) * 128],
                                       rhs=h8[k][:, n0:n0 + 512],
                                       start=(k == 0), stop=(k == 7))
                    b.sems["t"].inc(ins)
                dmm = b.mark("t")

                @b.on("vector")
                def _(v, icg=icg, hc=hc, tt=tt, n0=n0, dbk=dbk, dmm=dmm):
                    b.waits(v, dmm)
                    if icg == 0:
                        ins = v.tensor_scalar_mul(
                            out=dn_acc[hc][:, n0:n0 + 512], in0=b.pool[dbk][:, :],
                            scalar1=scl[:, 128 + hc:129 + hc])
                    else:
                        ins = v.scalar_tensor_tensor(
                            out=dn_acc[hc][:, n0:n0 + 512], in0=b.pool[dbk][:, :],
                            scalar=scl[:, 128 + hc:129 + hc],
                            in1=dn_acc[hc][:, n0:n0 + 512],
                            op0=ALU.mult, op1=ALU.add)
                    b.sems["v"].inc(ins)
                b.bank_free[dbk] = [b.mark("v")]
            wdfr_free = [b.mark("t")]
        down_done = b.mark("t")
        for i8 in range(8):
            h8_free[i8] = [down_done]
    dn_done = b.mark("v")

    @b.on("vector")
    def _(v):
        b.waits(v, dn_done)
        for c in range(DC):
            ins = v.tensor_tensor(out=xn[c][:, :], in0=xn[c][:, :],
                                  in1=dn_acc[c][:, :], op=ALU.add)
        b.sems["v"].inc(ins)
    resid_done = b.mark("v")

    lnout_free = [NONE, NONE]

    def pre_wait_fn(c, j):
        return lnout_free[c % 2]

    def post_fn(c, j, vmark):
        @b.on("gpsimd")
        def _(p, c=c, j=j, vmark=vmark):
            b.waits(p, vmark)
            n0, n1 = j * 512, (j + 1) * 512
            sst.inc(p.dma_start(out=io["xfm4"][c * 128:(c + 1) * 128, n0:n1],
                                in_=lnout[c % 2].bitcast(F32)[:, n0:n1]), 16)
        lnout_free[c % 2] = [("st", b.sems["st"].n)]

    out_aps = {c: lnout[c % 2] for c in range(DC)}
    emit_ln(b, C, xn, ln_scr, [resid_done], [0, 1, 2, 3, 4, 5],
            out_aps, pre_wait_fn, post_fn)

    @b.on("gpsimd")
    def _(p):
        p.wait_ge(b.sems["st"].h, b.sems["st"].n)
    return io



def build_fused():
    b = B("fused")
    nc = b.nc
    caps = declare_consts(nc)
    io1 = declare_l0attn(nc, pfx="a1_", out_kind="Internal")
    io2 = declare_l0ffn(nc, pfx="a2_", xin=fr(io1["xfm1"]), out_kind="Internal")
    io3 = declare_l1attn(nc, pfx="a3_", xin=fr(io2["xfm2"]), out_kind="Internal")
    io4 = declare_l1ffn(nc, pfx="a4_", xin=fr(io3["xfm3"]),
                        out_kind="ExternalOutput")
    C = load_consts(b, caps)

    def ends():
        return [b.mark(x) for x in ("t", "v", "s", "p", "st", "ld", "ld2", "cc")]

    build_l0attn(b, C, io1)
    build_l0ffn(b, C, io2, start_dep=ends())
    build_l1attn(b, C, io3, start_dep=ends())
    build_l1ffn(b, C, io4, start_dep=ends())
    b.close()
    return b


def host_fused_maps(inputs):
    w1 = host_l0attn_weights(inputs)
    w2 = host_l0ffn_weights(inputs)
    w3 = host_l1attn_weights(inputs)
    w4 = host_l1ffn_weights(inputs)
    maps = []
    for c in range(NCORES):
        m = {}
        for k, v in host_consts(c).items():
            m[k] = v
        for k, v in host_l0attn(inputs, c).items():
            m["a1_" + k] = v
        for k, v in w1.items():
            m["a1_" + k] = v
        for k, v in w2.items():
            m["a2_" + k] = v
        for k, v in w3.items():
            m["a3_" + k] = v
        for k, v in w4.items():
            m["a4_" + k] = v
        maps.append(m)
    return maps

_PROG = []


def kernel(**inputs):
    if not _PROG:
        _PROG.append(build_fused())
    b = _PROG[0]
    maps = host_fused_maps(inputs)
    res = run_bass_kernel_spmd(b.nc, maps, core_ids=list(range(NCORES)))
    out = np.concatenate([res.results[c]["a4_xfm4"].T for c in range(NCORES)],
                         axis=0)
    return np.ascontiguousarray(out.reshape(1, 8192, D).astype(np.float32))
